# version 04
# baseline (speedup 1.0000x reference)
import numpy as np
from contextlib import ExitStack
from collections import defaultdict
import concourse.bass as bass
import concourse.mybir as mybir
from concourse.bass_utils import run_bass_kernel_spmd

AF = mybir.ActivationFunctionType
ALU = mybir.AluOpType
AX = mybir.AxisListType
F32, BF16 = mybir.dt.float32, mybir.dt.bfloat16

NCORE = 8
TOK = 2048
NT = TOK // 128
D = 1024
DCH = D // 128

ENGINES = ("tensor", "vector", "scalar", "gpsimd", "sync")


class Em:

    def __init__(self, nc):
        self.nc = nc
        self.q = {e: [] for e in ENGINES}
        self.cnt = defaultdict(int)
        self.lastw = {e: defaultdict(int) for e in ENGINES}
        self.own = {"T": "tensor", "V": "vector", "S": "scalar", "P": "gpsimd", "Y": "sync"}
        self.semnames = set(self.own)
        self.handles = {}

    def emit(self, eng, build, waits=(), inc=None, inc_by=1, strict=False):
        real = []
        for w in waits:
            if not w:
                continue
            s, v = w
            if self.own.get(s) == eng and not strict:
                continue
            if self.lastw[eng][s] < v:
                self.lastw[eng][s] = v
                real.append((s, v))
        tok = None
        if inc:
            self.semnames.add(inc)
            self.cnt[inc] += inc_by
            tok = (inc, self.cnt[inc])

        def fn(e, _real=tuple(real), _b=build, _inc=inc, _by=inc_by, _em=self):
            for (s, v) in _real:
                e.wait_ge(_em.handles[s], v)
            try:
                ins = _b(e)
            except Exception:
                import sys
                c = getattr(_b, "__code__", None)
                cl = getattr(_b, "__closure__", None)
                print("EM-OP FAILED:", getattr(_b, "_site", None) or
                      (f"{c.co_filename}:{c.co_firstlineno}" if c else "?"),
                      file=sys.stderr)
                if cl:
                    for cell in cl:
                        try:
                            print("  cell:", repr(cell.cell_contents)[:300], file=sys.stderr)
                        except Exception:
                            pass
                raise
            if _inc is not None:
                assert ins is not None, "inc on op that built nothing"
                ins.then_inc(_em.handles[_inc], _by)

        self.q[eng].append(fn)
        return tok

    def tensor_tok(self):
        return ("T", self.cnt["T"])

    DMA_POOL = 20

    def dma(self, eng, out, in_, waits=(), sem=None, **kw):
        if sem is None:
            tag = {"gpsimd": "dg", "sync": "dy", "scalar": "ds"}[eng]
            if not hasattr(self, "_dmarr"):
                self._dmarr = defaultdict(int)
                self._dmaprev = {}
            idx = self._dmarr[tag] % self.DMA_POOL
            self._dmarr[tag] += 1
            sem = f"{tag}{idx}"
            prev = self._dmaprev.get(sem)
            waits = list(waits) + ([prev] if prev else [])
        self.semnames.add(sem)
        import traceback as _tb
        _site = _tb.extract_stack()[-2]
        lam = lambda e: e.dma_start(out=out, in_=in_, **kw)
        lam._site = f"{_site.filename}:{_site.lineno}"
        tok = self.emit(eng, lam, waits=waits, inc=sem, inc_by=16)
        self._dmaprev[sem] = tok
        return tok

    def all_toks(self):
        toks = [(s, self.cnt[s]) for s in ("T", "V", "S", "P", "Y") if self.cnt[s]]
        if hasattr(self, "_dmaprev"):
            toks += list(self._dmaprev.values())
        return toks

    def phase_barrier(self):
        toks = self.all_toks()
        for eng in ENGINES:
            self.emit(eng, lambda e: None, waits=toks, strict=True)

    def finalize(self, ctx):
        for s in sorted(self.semnames):
            self.handles[s] = ctx.enter_context(self.nc.semaphore(s))
        block = ctx.enter_context(self.nc.Block())
        for engname in ENGINES:
            ops = self.q[engname]
            dec = getattr(block, engname)
            dec((lambda _ops: lambda e: [fn(e) for fn in _ops])(ops))


class Arena:
    _uid = [0]

    def __init__(self, nc, base=17408, top=229300):
        self.nc, self.base, self.top = nc, base, top
        self.off = base
        self.peak = base

    def reset(self):
        self.off = self.base

    def alloc(self, name, shape, dt):
        esz = {F32: 4, BF16: 2}[dt]
        nbytes = int(np.prod(shape[1:])) * esz
        nbytes = (nbytes + 31) // 32 * 32
        Arena._uid[0] += 1
        h = self.nc.alloc_sbuf_tensor_at(f"{name}_{Arena._uid[0]}", list(shape), dt,
                                         offset=self.off)
        self.off += nbytes
        self.peak = max(self.peak, self.off)
        assert self.off <= self.top, f"arena overflow at {name}: {self.off}"
        return h


def _dyn(ap_obj, extra):
    return bass.AP(ap_obj.tensor, ap_obj.offset + extra, ap_obj.ap)


def _phase_io(nc, fused, decls):
    if fused is None:
        return {n: nc.dram_tensor(n, s, d, kind=k) for (n, s, d, k) in decls}
    io = fused["io"]
    return {n: io[n] for (n, s, d, k) in decls}


def host_prep_l1ffn(inp):
    H = {}
    C = lambda a: np.ascontiguousarray(a, dtype=np.float32)
    H["ident"] = np.eye(128, dtype=np.float32)
    H["rW1"] = C(inp["L1_ffn_rW1"].reshape(8, 128, 16, 128).transpose(1, 0, 2, 3))
    H["rb1"] = C(np.asarray(inp["L1_ffn_rb1"]).reshape(16, 128).T)
    H["rW2"] = C(inp["L1_ffn_rW2"].reshape(16, 128, 4).transpose(1, 0, 2))
    H["rb2"] = C(np.asarray(inp["L1_ffn_rb2"]).reshape(1, 4))
    H["We1"] = C(inp["L1_ffn_We1"].reshape(4, 8, 128, 32, 128).transpose(2, 0, 1, 3, 4))
    H["be1"] = C(np.asarray(inp["L1_ffn_be1"]).reshape(4, 32, 128).transpose(2, 0, 1))
    H["We2"] = C(inp["L1_ffn_We2"].reshape(4, 32, 128, 1024).transpose(2, 0, 1, 3))
    H["be2"] = C(np.asarray(inp["L1_ffn_be2"]).reshape(1, 4, 1024))
    return H


def host_prep_l0ffn(inp):
    H = {}
    C = lambda a: np.ascontiguousarray(a, dtype=np.float32)
    Wc = np.asarray(inp["L0_ffn_Wc"]).reshape(1024, 128)
    H["Wc"] = C(Wc.reshape(8, 128, 128).transpose(1, 0, 2))
    H["bc"] = C(np.asarray(inp["L0_ffn_bc"]).reshape(1, 128))
    H["tie"] = C(np.tile(np.linspace(0.0, 1e-06, 32, dtype=np.float32), 16).reshape(1, 512))
    W1 = np.asarray(inp["L0_ffn_W1"]).reshape(128, 1024, 32)
    H["W1"] = C(W1.transpose(1, 0, 2).reshape(8, 128, 128, 32).transpose(1, 0, 2, 3))
    H["b1"] = C(np.asarray(inp["L0_ffn_b1"]).reshape(128, 32).T)
    W2 = np.asarray(inp["L0_ffn_W2"]).reshape(128, 1024, 32)
    H["W2"] = C(W2.transpose(2, 0, 1))
    H["b2"] = C(np.asarray(inp["L0_ffn_b2"]).reshape(128, 1024))
    return H


def emit_ln(E, xtile, xhat_bf, st, ready=(), war=()):
    s1, s2, mean, var, rstd, t0, ex2, tln = (st[:, 0:1], st[:, 1:2], st[:, 2:3],
                                             st[:, 3:4], st[:, 4:5], st[:, 5:6],
                                             st[:, 6:7], st[:, 7:8])
    sq = E.ln_sq
    ta = E.emit("vector", lambda v: v.reduce_sum(s1, xtile, AX.X), waits=list(ready), inc="V")
    E.emit("vector", lambda v: v.tensor_tensor(sq, xtile, xtile, ALU.mult), inc="V")
    tb = E.emit("vector", lambda v: v.reduce_sum(s2, sq, AX.X), inc="V")
    tc = E.emit("vector", lambda v: v.tensor_scalar_mul(mean, s1, 1.0 / 1024.0),
                waits=[ta], inc="V", strict=True)
    td = E.emit("vector", lambda v: v.tensor_scalar_mul(ex2, s2, 1.0 / 1024.0),
                waits=[tb], inc="V", strict=True)
    te = E.emit("vector", lambda v: v.tensor_tensor(t0, mean, mean, ALU.mult),
                waits=[tc], inc="V", strict=True)
    t6 = E.emit("vector", lambda v: v.tensor_tensor(var, ex2, t0, ALU.subtract),
                waits=[td, te], inc="V", strict=True)
    cst = E.lnconsts
    E.emit("scalar", lambda s: s.activation(out=tln, in_=var, func=AF.Ln, bias=cst[:, 0:1]),
           waits=[t6], inc="S")
    t8 = E.emit("scalar", lambda s: s.activation(out=rstd, in_=tln, func=AF.Exp,
                                                 scale=cst[:, 1:2]), inc="S")
    t9 = E.emit("vector", lambda v: v.tensor_scalar(xhat_bf, xtile, mean, rstd,
                                                    ALU.subtract, ALU.mult),
                waits=[t8, tc] + list(war), inc="V", strict=True)
    return t9


def build_l1ffn(nc, dbg=False, with_y=False, fused=None, xin_name="x_in", xout_name="x_out"):
    dio = _phase_io(nc, fused, [
        (xin_name, [TOK, D], F32, "ExternalInput"),
        (xout_name, [TOK, D], F32, "ExternalOutput"),
        ("rW1", [128, 8, 16, 128], F32, "ExternalInput"),
        ("rb1", [128, 16], F32, "ExternalInput"),
        ("rW2", [128, 16, 4], F32, "ExternalInput"),
        ("rb2", [1, 4], F32, "ExternalInput"),
        ("We1", [128, 4, 8, 32, 128], F32, "ExternalInput"),
        ("be1", [128, 4, 32], F32, "ExternalInput"),
        ("We2", [128, 4, 32, 1024], F32, "ExternalInput"),
        ("be2", [1, 4, 1024], F32, "ExternalInput"),
        ("ident", [128, 128], F32, "ExternalInput"),
    ] + ([("y_in", [TOK, D], BF16, "ExternalInput")] if (with_y and fused is None) else [])
      + ([("ya2a_out", [8, 128, TOK], BF16, "Internal")] if (with_y and fused is not None) else []))
    x_in, x_out = dio[xin_name], dio[xout_name]
    w_rW1, w_rb1, w_rW2, w_rb2 = dio["rW1"], dio["rb1"], dio["rW2"], dio["rb2"]
    w_We1, w_be1, w_We2, w_be2 = dio["We1"], dio["be1"], dio["We2"], dio["be2"]
    w_ident = dio["ident"]
    if with_y:
        if fused is None:
            y_in = dio["y_in"]
        else:
            ya2a_out = dio["ya2a_out"]
    if dbg:
        d_xhatT = nc.dram_tensor("d_xhatT", [128, DCH, TOK], F32, kind="ExternalOutput")
        d_cw = nc.dram_tensor("d_cw", [128, NT, 4], F32, kind="ExternalOutput")
        d_hT = nc.dram_tensor("d_hT", [128, 32, 512], F32, kind="ExternalOutput")
        d_rhT = nc.dram_tensor("d_rhT", [128, 32, 512], F32, kind="ExternalOutput")
        d_ln = nc.dram_tensor("d_ln", [128, NT, 8], F32, kind="ExternalOutput")

    if fused is None:
        ctx = ExitStack()
        sb = lambda name, shape, dt: ctx.enter_context(nc.sbuf_tensor(name, shape, dt))
    else:
        fused["ar"].reset()
        sb = fused["ar"].alloc

    xt = sb("xt", [128, D], F32)
    xhat = sb("xhat", [128, D], BF16)
    xhatT = sb("xhatT", [128, DCH, TOK], BF16)
    hT = sb("hT", [128, 32, 512], BF16)
    We1_sb = sb("We1_sb", [128, 8, 32, 128], BF16)
    We2_sb = sb("We2_sb", [128, 32, 1024], BF16)
    rW2_sb = sb("rW2_sb", [128, 16, 4], BF16)
    rb1_sb = sb("rb1_sb", [128, 16], F32)
    rb2_sb = sb("rb2_sb", [1, 4], BF16)
    be1_sb = sb("be1_sb", [128, 4, 32], F32)
    be2_sb = sb("be2_sb", [1, 1024], BF16)
    ident_sb = sb("ident_sb", [128, 128], BF16)
    ones_sb = sb("ones_sb", [1, 128], BF16)
    cw = sb("cw", [128, NT, 4], F32)
    lnst = sb("lnst", [128, NT, 8], F32)
    r4 = sb("r4", [128, 12], F32)
    r44 = sb("r44", [128, 16], F32)
    stage = sb("stage", [128, 2, 512], F32)

    if fused is None:
        ps_tr = [ctx.enter_context(nc.psum_tensor(f"ps_tr{i}", [128, 128], BF16))[:]
                 for i in range(2)]
        ps_h = [ctx.enter_context(nc.psum_tensor(f"ps_h{i}", [128, 512], F32))[:]
                for i in range(2)]
        ps_y = [ctx.enter_context(nc.psum_tensor(f"ps_y{i}", [128, 512], F32))[:]
                for i in range(2)]
        ps_l = ctx.enter_context(nc.psum_tensor("ps_l", [128, 512], F32))[:]
        ps_l4 = ctx.enter_context(nc.psum_tensor("ps_l4", [128, 4], F32))[:]
        E = Em(nc)
    else:
        P8 = fused["P8"]
        ps_tr = [P8[6][:].bitcast(BF16)[:, 0:128], P8[7][:].bitcast(BF16)[:, 0:128]]
        ps_h = [P8[0][:], P8[1][:]]
        ps_y = [P8[2][:], P8[3][:]]
        ps_l = P8[4][:]
        ps_l4 = P8[5][:, 0:4]
        E = fused["E"]

    t_cp = None if with_y else E.dma("scalar", x_out[:, :], x_in[:, :])
    t_id = E.dma("gpsimd", ident_sb[:], w_ident[:, :])
    t_rb1 = E.dma("gpsimd", rb1_sb[:], w_rb1[:, :])
    t_rW2 = E.dma("gpsimd", rW2_sb[:], w_rW2[:, :, :])
    t_rb2 = E.dma("gpsimd", rb2_sb[:], w_rb2[:, :])
    t_be1 = E.dma("gpsimd", be1_sb[:], w_be1[:, :, :])
    t_ones = E.emit("vector", lambda v: v.memset(ones_sb[:], 1.0), inc="V")
    lncst = sb("lncst", [128, 2], F32)
    E.emit("vector", lambda v: v.memset(lncst[:, 0:1], 1e-5), inc="V")
    E.emit("vector", lambda v: v.memset(lncst[:, 1:2], -0.5), inc="V")
    E.lnconsts = lncst
    E.ln_sq = stage[:].rearrange("p a b -> p (a b)")

    ln_done = {}
    tr_free = {}
    xload = {}
    basew = {}
    for ti in range(NT):
        xload[ti] = E.dma("sync", xt[:], x_in[ti * 128:(ti + 1) * 128, :],
                          waits=[ln_done.get(ti - 1), basew.get(ti - 1)])
        rdy = [xload[ti]]
        if with_y:
            yt = hT[:].rearrange("p a b -> p (a b)")[:, 0:D]
            if fused is None:
                yl = E.dma("scalar", yt[:], y_in[ti * 128:(ti + 1) * 128, :],
                           waits=[ln_done.get(ti - 1)])
            else:
                ycol = hT[:].rearrange("p a b -> p (a b)")[:, D:D + 256]
                for ch in range(DCH):
                    sl = ch % 2
                    lc = E.dma("scalar", ycol[:, sl * 128:(sl + 1) * 128],
                               ya2a_out[ch, :, ti * 128:(ti + 1) * 128],
                               waits=[tr_free.get(("y", ti, ch - 2)),
                                      ln_done.get(ti - 1)])
                    tt = E.emit("tensor",
                                (lambda t, _s=ycol[:, sl * 128:(sl + 1) * 128], _p=ps_tr[sl]:
                                 t.transpose(_p, _s, ident_sb[:])),
                                waits=[lc, t_id, tr_free.get((ti - 1, (sl) + 6)),
                                       tr_free.get(("y", ti, ch - 2))], inc="T")
                    yev = E.emit("vector",
                                 (lambda v, _p=ps_tr[sl], _d=yt[:, ch * 128:(ch + 1) * 128]:
                                  v.tensor_copy(_d, _p)),
                                 waits=[tt], inc="V")
                    tr_free[("y", ti, ch)] = yev
                tr_free[(ti - 1, 6)] = tr_free[("y", ti, DCH - 2)]
                tr_free[(ti - 1, 7)] = tr_free[("y", ti, DCH - 1)]
                yl = tr_free[("y", ti, DCH - 1)]
            ad = E.emit("vector", lambda v: v.tensor_tensor(xt[:], xt[:], yt[:], ALU.add),
                        waits=[xload[ti], yl], inc="V")
            basew[ti] = E.dma("sync", x_out[ti * 128:(ti + 1) * 128, :], xt[:], waits=[ad])
            rdy = [ad, basew[ti]]
        t9 = emit_ln(E, xt[:], xhat[:], lnst[:, ti, :], ready=rdy,
                     war=[tr_free.get((ti - 1, DCH - 1))])
        ln_done[ti] = t9
        for ch in range(DCH):
            pst = ps_tr[ch % 2]
            tt = E.emit("tensor",
                        (lambda t, _s=xhat[:, ch * 128:(ch + 1) * 128], _p=pst:
                         t.transpose(_p, _s, ident_sb[:])),
                        waits=[t9, t_id, tr_free.get((ti - 1, (ch % 2) + 6)),
                               tr_free.get((ti, ch - 2))], inc="T")
            ev = E.emit("vector",
                        (lambda v, _p=pst, _d=xhatT[:, ch, ti * 128:(ti + 1) * 128]:
                         v.tensor_copy(_d, _p)),
                        waits=[tt], inc="V")
            tr_free[(ti, ch)] = ev
    xT_all = ev
    if dbg:
        E.dma("gpsimd", d_xhatT[:, :, :], xhatT[:, :, :], waits=[xT_all])
        E.dma("gpsimd", d_ln[:, :, :], lnst[:, :, :], waits=[xT_all])

    NBLK = TOK // 512
    cw_tok = [None] * NT
    rhT = hT[:].rearrange("p a b -> p (a b)")
    ps_l_free = None
    ps_l4_free = None
    rw1_free = {}
    rhT_war = ("V", E.cnt["V"]) if with_y else None
    for blk in range(NBLK):
        t0c = blk * 512
        rh_last = None
        for hch in range(16):
            slot = hch % 2
            ld = E.dma("gpsimd", We1_sb[:, :, slot, :], w_rW1[:, :, hch, :],
                       waits=[rw1_free.get(slot)])
            tt = None
            for dch in range(DCH):
                tt = E.emit("tensor",
                            (lambda t, _l=We1_sb[:, dch, slot, :], _r=xhatT[:, dch, t0c:t0c + 512],
                             _st=(dch == 0), _sp=(dch == DCH - 1):
                             t.matmul(ps_l, _l, _r, start=_st, stop=_sp)),
                            waits=[ld, xT_all, ps_l_free], inc="T")
            rw1_free[slot] = tt
            ev = E.emit("scalar",
                        (lambda s, _o=rhT[:, hch * 512:(hch + 1) * 512], _b=rb1_sb[:, hch:hch + 1]:
                         s.activation(out=_o, in_=ps_l, func=AF.Gelu, bias=_b)),
                        waits=[tt, t_rb1, rhT_war, getattr(E, "_dbg_rhT_dump", None)], inc="S")
            ps_l_free = ev
            rh_last = ev
        for sub in range(4):
            ti = blk * 4 + sub
            for hch in range(16):
                E.emit("tensor",
                       (lambda t, _l=rhT[:, hch * 512 + sub * 128: hch * 512 + sub * 128 + 128],
                        _r=rW2_sb[:, hch, :], _st=(hch == 0):
                        t.matmul(ps_l4, _l, _r, start=_st, stop=False)),
                       waits=[rh_last, t_rW2, ps_l4_free], inc="T")
            tb = E.emit("tensor",
                        lambda t: t.matmul(ps_l4, ones_sb[0:1, :], rb2_sb[0:1, :],
                                           start=False, stop=True),
                        waits=[t_rb2, t_ones], inc="T")
            p4, m1, s4, rcp = r4[:, 0:4], r4[:, 4:5], r4[:, 5:6], r4[:, 6:7]
            cnt = r4[:, 8:12]
            cwt = cw[:, ti, :]
            c1 = E.emit("vector", lambda v: v.reduce_max(m1, ps_l4, AX.X), waits=[tb], inc="V")
            c2 = E.emit("vector", lambda v: v.tensor_scalar_sub(p4, ps_l4, m1),
                        waits=[c1], inc="V", strict=True)
            ps_l4_free = c2
            e3 = E.emit("vector", lambda v: v.tensor_copy(r44[:, 0:4], p4),
                        waits=[c2], inc="V", strict=True)
            e3b = E.emit("scalar", lambda s: s.activation(out=p4, in_=r44[:, 0:4], func=AF.Exp),
                         waits=[e3], inc="S")
            c4 = E.emit("vector", lambda v: v.reduce_sum(s4, p4, AX.X), waits=[e3b], inc="V")
            c5 = E.emit("vector", lambda v: v.reciprocal(rcp, s4),
                        waits=[c4], inc="V", strict=True)
            c6 = E.emit("vector", lambda v: v.tensor_scalar_mul(p4, p4, rcp),
                        waits=[c5], inc="V", strict=True)
            pa = p4.unsqueeze(1).broadcast_to([128, 4, 4])
            pb = p4.unsqueeze(2).broadcast_to([128, 4, 4])
            c7 = E.emit("vector", lambda v: v.tensor_tensor(
                r44[:].rearrange("p (a b) -> p a b", b=4), pa, pb, ALU.is_gt),
                waits=[c6], inc="V", strict=True)
            c8 = E.emit("vector", lambda v: v.reduce_sum(
                cnt, r44[:].rearrange("p (a b) -> p a b", b=4), AX.X),
                waits=[c7], inc="V", strict=True)
            c9 = E.emit("vector", lambda v: v.tensor_single_scalar(cnt, cnt, 1.5, ALU.is_lt),
                        waits=[c8], inc="V", strict=True)
            c10 = E.emit("vector", (lambda v, _c=cwt: v.tensor_tensor(_c, p4, cnt, ALU.mult)),
                         waits=[c9], inc="V", strict=True)
            cw_tok[ti] = E.emit("vector", (lambda v, _c=cwt: v.tensor_scalar_mul(_c, _c, 0.5)),
                                waits=[c10], inc="V", strict=True)
        rhT_war = E.tensor_tok()
        if dbg and blk == 0:
            dmp = E.dma("gpsimd", d_rhT[:, :, :], hT[:, :, :], waits=[rh_last])
            E._dbg_rhT_dump = dmp

    acc_dma = [None] * NT
    stage_free = {}
    ps_h_free = {}
    ps_y_free = {}
    nstage = 0
    router_T = E.tensor_tok()
    if dbg:
        E.dma("gpsimd", d_cw[:, :, :], cw[:, :, :], waits=[cw_tok[NT - 1]])
    we1_free = router_T
    we2_free = None
    be2_free = None
    hT_war = rhT_war
    for e in range(4):
        l1 = E.dma("gpsimd", We1_sb[:], w_We1[:, e, :, :, :], waits=[we1_free])
        l2 = E.dma("gpsimd", We2_sb[:], w_We2[:, e, :, :], waits=[we2_free])
        t_be2 = E.dma("gpsimd", be2_sb[:], w_be2[0:1, e, :], waits=[be2_free])
        for blk in range(NBLK):
            t0c = blk * 512
            h_last = None
            for hch in range(32):
                bank = ps_h[hch % 2]
                tt = None
                for dch in range(DCH):
                    tt = E.emit("tensor",
                                (lambda t, _l=We1_sb[:, dch, hch, :], _r=xhatT[:, dch, t0c:t0c + 512],
                                 _o=bank, _st=(dch == 0), _sp=(dch == DCH - 1):
                                 t.matmul(_o, _l, _r, start=_st, stop=_sp)),
                                waits=[l1, xT_all, ps_h_free.get(hch % 2)], inc="T")
                ev = E.emit("scalar",
                            (lambda s, _o=hT[:, hch, :], _b=be1_sb[:, e, hch:hch + 1], _p=bank:
                             s.activation(out=_o, in_=_p, func=AF.Gelu, bias=_b)),
                            waits=[tt, t_be1, hT_war, getattr(E, "_dbg_hT_dump", None)], inc="S")
                ps_h_free[hch % 2] = ev
                h_last = ev
            if dbg and e == 0 and blk == 0:
                dmp = E.dma("gpsimd", d_hT[:, :, :], hT[:, :, :], waits=[h_last])
                E._dbg_hT_dump = dmp
            if blk == NBLK - 1:
                we1_free = E.tensor_tok()
            for sub in range(4):
                ti = blk * 4 + sub
                for half in range(2):
                    bank = ps_y[half]
                    for hch in range(32):
                        E.emit("tensor",
                               (lambda t, _l=hT[:, hch, sub * 128:(sub + 1) * 128],
                                _r=We2_sb[:, hch, half * 512:(half + 1) * 512],
                                _o=bank, _st=(hch == 0):
                                t.matmul(_o, _l, _r, start=_st, stop=False)),
                               waits=[l2, h_last, ps_y_free.get(half)], inc="T")
                    tb = E.emit("tensor",
                                (lambda t, _o=bank, _r=be2_sb[0:1, half * 512:(half + 1) * 512]:
                                 t.matmul(_o, ones_sb[0:1, :], _r, start=False, stop=True)),
                                waits=[t_be2], inc="T")
                    sslot = nstage % 2
                    nstage += 1
                    evt = E.emit("vector",
                                 (lambda v, _p=bank, _s=stage[:, sslot, :], _c=cw[:, ti, e:e + 1]:
                                  v.tensor_scalar_mul(_s, _p, _c)),
                                 waits=[tb, cw_tok[ti], stage_free.get(sslot)], inc="V")
                    ps_y_free[half] = evt
                    dmat = E.dma("gpsimd",
                                 x_out[ti * 128:(ti + 1) * 128, half * 512:(half + 1) * 512],
                                 stage[:, sslot, :],
                                 waits=[evt, t_cp if not with_y else basew[ti],
                                        acc_dma[ti]], accum_op=ALU.add)
                    acc_dma[ti] = dmat
                    stage_free[sslot] = dmat
            hT_war = E.tensor_tok()
        we2_free = E.tensor_tok()
        be2_free = we2_free

    E.emit("sync", lambda e: None, waits=[t for t in acc_dma] + ([t_cp] if t_cp else []))
    if fused is None:
        E.finalize(ctx)
        ctx.close()
    return nc


def build_l0ffn(nc, dbg=False, fused=None, xin_name="x_in", xout_name="x_out"):
    dio = _phase_io(nc, fused, [
        (xin_name, [TOK, D], F32, "ExternalInput"),
        (xout_name, [TOK, D], F32, "ExternalOutput"),
        ("Wc", [128, 8, 128], F32, "ExternalInput"),
        ("bc", [1, 128], F32, "ExternalInput"),
        ("tie", [1, 512], F32, "ExternalInput"),
        ("W1", [128, 8, 128, 32], F32, "ExternalInput"),
        ("b1", [32, 128], F32, "ExternalInput"),
        ("W2", [32, 128, 1024], F32, "ExternalInput"),
        ("b2", [128, 1024], F32, "ExternalInput"),
        ("ident", [128, 128], F32, "ExternalInput"),
        ("Rd", [128, 64, 1024], BF16, "Internal"),
    ])
    x_in, x_out = dio[xin_name], dio[xout_name]
    w_Wc, w_bc, w_tie, w_ident = dio["Wc"], dio["bc"], dio["tie"], dio["ident"]
    w_W1, w_b1, w_W2, w_b2, Rd = dio["W1"], dio["b1"], dio["W2"], dio["b2"], dio["Rd"]
    if dbg:
        d_lg = nc.dram_tensor("d_lg", [128, TOK], F32, kind="ExternalOutput")
        d_pt = nc.dram_tensor("d_pt", [128, TOK], F32, kind="ExternalOutput")
        d_xt = nc.dram_tensor("d_xt", [128, 8, 64, 64], F32, kind="ExternalOutput")

    if fused is None:
        ctx = ExitStack()
        sb = lambda name, shape, dt: ctx.enter_context(nc.sbuf_tensor(name, shape, dt))
    else:
        fused["ar"].reset()
        sb = fused["ar"].alloc

    xt = sb("xt", [128, D], F32)
    xhat = sb("xhat", [128, NT, D], BF16)
    xhatT = sb("xhatT", [128, DCH, TOK], BF16)
    XT = sb("XT", [128, 8, 64, 64], BF16)
    W1_sb = sb("W1_sb", [128, 8, 64, 32], BF16)
    W2_sb = sb("W2_sb", [32, 2, 1024], BF16)
    lg_sb = sb("lg_sb", [128, 512], F32)
    pt_sb = sb("pt_sb", [128, TOK], BF16)
    pbig = sb("pbig", [128, NT, 512], BF16)
    gmax = sb("gmax", [128, 64], F32)
    inner = sb("inner", [32, 2, 64], BF16)
    Wc_sb = sb("Wc_sb", [128, 8, 128], BF16)
    bc_sb = sb("bc_sb", [1, 128], BF16)
    tie_sb = sb("tie_sb", [1, 512], BF16)
    b1_sb = sb("b1_sb", [32, 128], F32)
    b2_sb = sb("b2_sb", [128, 1024], BF16)
    ident_sb = sb("ident_sb", [128, 128], BF16)
    ones_sb = sb("ones_sb", [1, 512], BF16)
    lnst = sb("lnst", [128, NT, 8], F32)
    lncst = sb("lncst", [128, 2], F32)
    rstage = sb("rstage", [64, 2, 512], BF16)
    rsb = sb("rsb", [128, 1024], BF16)

    if fused is None:
        psf = [ctx.enter_context(nc.psum_tensor(f"psf{i}", [128, 512], F32))[:]
               for i in range(6)]
        ps_bf = ctx.enter_context(nc.psum_tensor("ps_bf", [128, 512], BF16))[:]
        ps_tr = ctx.enter_context(nc.psum_tensor("ps_tr", [128, 128], BF16))[:]
        E = Em(nc)
    else:
        P8 = fused["P8"]
        psf = [P8[i][:] for i in range(6)]
        ps_bf = P8[6][:].bitcast(BF16)[:, 0:512]
        ps_tr = P8[7][:].bitcast(BF16)[:, 0:128]
        E = fused["E"]
    E.lnconsts = lncst
    E.ln_sq = XT[:].rearrange("p a b c -> p (a b c)")[:, 0:2048].bitcast(F32)

    t_id = E.dma("gpsimd", ident_sb[:], w_ident[:, :])
    t_wc = E.dma("gpsimd", Wc_sb[:], w_Wc[:, :, :])
    t_bc = E.dma("gpsimd", bc_sb[:], w_bc[:, :])
    t_tie = E.dma("gpsimd", tie_sb[:], w_tie[:, :])
    t_b1 = E.dma("gpsimd", b1_sb[:], w_b1[:, :])
    t_b2 = E.dma("gpsimd", b2_sb[:], w_b2[:, :])
    E.emit("vector", lambda v: v.memset(ones_sb[:], 1.0), inc="V")
    E.emit("vector", lambda v: v.memset(lncst[:, 0:1], 1e-5), inc="V")
    E.emit("vector", lambda v: v.memset(lncst[:, 1:2], -0.5), inc="V")

    ln_done = {}
    tr_free = {}
    for ti in range(NT):
        xl = E.dma("sync", xt[:], x_in[ti * 128:(ti + 1) * 128, :], waits=[ln_done.get(ti - 1)])
        t9 = emit_ln(E, xt[:], xhat[:, ti, :], lnst[:, ti, :], ready=[xl], war=[])
        ln_done[ti] = t9
        for ch in range(DCH):
            tt = E.emit("tensor",
                        (lambda t, _s=xhat[:, ti, ch * 128:(ch + 1) * 128]:
                         t.transpose(ps_tr, _s, ident_sb[:])),
                        waits=[t9, t_id, tr_free.get((ti, ch - 1)), tr_free.get((ti - 1, DCH - 1))],
                        inc="T")
            ev = E.emit("vector",
                        (lambda v, _d=xhatT[:, ch, ti * 128:(ti + 1) * 128]:
                         v.tensor_copy(_d, ps_tr)),
                        waits=[tt], inc="V")
            tr_free[(ti, ch)] = ev
    xT_all = ev

    NBLK = TOK // 512
    lg_evac = {}
    route_done = []
    for blk in range(NBLK):
        t0c = blk * 512
        bank = psf[blk % 2]
        for dch in range(DCH):
            E.emit("tensor",
                   (lambda t, _l=Wc_sb[:, dch, :], _r=xhatT[:, dch, t0c:t0c + 512],
                    _o=bank, _st=(dch == 0):
                    t.matmul(_o, _l, _r, start=_st, stop=False)),
                   waits=[t_wc, xT_all, lg_evac.get(blk % 2)], inc="T")
        E.emit("tensor", (lambda t, _o=bank: t.matmul(_o, bc_sb[0:1, :], ones_sb[0:1, 0:512],
                                                      start=False, stop=False)),
               waits=[t_bc], inc="T")
        tt = E.emit("tensor", (lambda t, _o=bank: t.matmul(_o, ones_sb[0:1, 0:128], tie_sb[0:1, :],
                                                           start=False, stop=True)),
                    waits=[t_tie], inc="T")
        ev = E.emit("vector", (lambda v, _o=bank, _d=lg_sb[:, :]:
                               v.tensor_copy(_d, _o)), waits=[tt, lg_evac.get("route")],
                    inc="V")
        lg_evac[blk % 2] = ev
        lgv = lg_sb[:, :].rearrange("p (g w) -> p g w", w=32)
        g16 = gmax[:, blk * 16:(blk + 1) * 16]
        r1 = E.emit("vector", (lambda v, _l=lgv, _g=g16: v.tensor_reduce(_g, _l, AX.X, ALU.max)),
                    waits=[ev], inc="V", strict=True)
        gb = g16.unsqueeze(2).broadcast_to([128, 16, 32])
        r2 = E.emit("vector", (lambda v, _l=lgv, _g=gb,
                               _d=pt_sb[:, t0c:t0c + 512].rearrange("p (g w) -> p g w", w=32):
                               v.tensor_tensor(_d, _l, _g, ALU.is_equal)),
                    waits=[r1], inc="V", strict=True)
        lg_evac["route"] = r2
        if dbg:
            lg_evac["route"] = E.dma("gpsimd", d_lg[:, t0c:t0c + 512], lg_sb[:, :], waits=[r2])
        route_done.append(r2)
    logits_V = ("V", E.cnt["V"])

    pb_toks = []
    for ti in range(NT):
        mz = E.emit("vector", lambda v: v.memset(ps_bf.bitcast(F32), 0.0),
                    waits=[route_done[ti // 4], pb_toks[-1] if pb_toks else None],
                    inc="V", strict=True)
        trs = []
        for j in range(4):
            tt = E.emit("tensor",
                        (lambda t, _i=pt_sb[:, ti * 128 + 32 * j: ti * 128 + 32 * j + 32],
                         _o=ps_bf[32 * j:32 * j + 32, j * 128:(j + 1) * 128], _j=j:
                         t.transpose(_o, _i, ident_sb[:], tile_position=(0, 32 * _j))),
                        waits=[mz, route_done[-1] if ti == 0 else None], inc="T")
            trs.append(tt)
        ev = E.emit("vector", (lambda v, _d=pbig[:, ti, :]: v.tensor_copy(_d, ps_bf)),
                    waits=trs, inc="V")
        pb_toks.append(ev)
    pbig_all = pb_toks[-1]

    w1_free = None
    gather_ps_free = {}
    prev_phase_acts = None
    w1_bank_free = {}
    w2_bank_free = {}
    w2_slot_free = {}
    inner_slot_free = {}
    rstage_free = {}
    rd_done = []
    for eb in range(2):
        l_w1 = E.dma("gpsimd", W1_sb[:], w_W1[:, :, eb * 64:(eb + 1) * 64, :], waits=[w1_free])
        gather_evacs = []
        for ti in range(NT):
            for ch in range(DCH):
                bi = 2 + (ch % 2)
                bank = psf[bi]
                rhs = pbig[:, ti, :].rearrange("p (g e) -> p g e", e=128)[:, :, eb * 64:(eb + 1) * 64]
                tt = E.emit("tensor",
                            (lambda t, _l=xhat[:, ti, ch * 128:(ch + 1) * 128], _r=rhs,
                             _o=bank[:, 0:256]:
                             t.matmul(_o, _l, _r, start=True, stop=True)),
                            waits=[pbig_all, gather_ps_free.get(bi), prev_phase_acts], inc="T")
                dst = XT[:, ch, :, :].rearrange("p e g -> p g e")[:, ti * 4:ti * 4 + 4, :]
                eng = "vector" if (ti + ch) % 2 == 0 else "scalar"
                xw = w1_free if eb == 1 else None
                if eng == "vector":
                    ev = E.emit("vector",
                                (lambda v, _s=bank[:, 0:256].rearrange("p (g e) -> p g e", e=64),
                                 _d=dst: v.tensor_copy(_d, _s)), waits=[tt, xw], inc="V")
                else:
                    ev = E.emit("scalar",
                                (lambda s, _s=bank[:, 0:256].rearrange("p (g e) -> p g e", e=64),
                                 _d=dst: s.copy(_d, _s)), waits=[tt, xw], inc="S")
                gather_ps_free[bi] = ev
                gather_evacs.append(ev)
        gV, gS = ("V", E.cnt["V"]), ("S", E.cnt["S"])
        rd_last = None
        for ei in range(64):
            es = eb * 64 + ei
            slot = ei % 2
            lw2 = E.dma("gpsimd", W2_sb[:, slot, :], w_W2[:, es, :], waits=[w2_slot_free.get(slot)])
            w1b = psf[slot][0:32, 0:64]
            tt = None
            for ch in range(DCH):
                tt = E.emit("tensor",
                            (lambda t, _l=W1_sb[:, ch, ei, :], _r=XT[:, ch, ei, :],
                             _o=w1b, _st=(ch == 0), _sp=(ch == DCH - 1):
                             t.matmul(_o, _l, _r, start=_st, stop=_sp)),
                            waits=[l_w1, gV, gS, logits_V, w1_bank_free.get(slot)], inc="T")
            iv = E.emit("scalar",
                        (lambda s, _o=inner[:, slot, :], _b=b1_sb[:, es:es + 1], _p=w1b:
                         s.activation(out=_o, in_=_p, func=AF.Relu, bias=_b)),
                        waits=[tt, t_b1, inner_slot_free.get(slot)], inc="S")
            w1_bank_free[slot] = iv
            for half in range(2):
                yb = psf[4 + half][0:64, :]
                tt2 = E.emit("tensor",
                             (lambda t, _l=inner[:, slot, :],
                              _r=W2_sb[:, slot, half * 512:(half + 1) * 512], _o=yb:
                              t.matmul(_o, _l, _r, start=True, stop=True)),
                             waits=[iv, lw2, w2_bank_free.get(half)], inc="T")
                sev = E.emit("vector",
                             (lambda v, _s=rstage[:, half, :], _p=yb: v.tensor_copy(_s, _p)),
                             waits=[tt2, rstage_free.get(half)], inc="V")
                w2_bank_free[half] = sev
                dm = E.dma("sync", Rd[es, :, half * 512:(half + 1) * 512], rstage[:, half, :],
                           waits=[sev])
                rstage_free[half] = dm
                rd_last = dm
            inner_slot_free[slot] = E.tensor_tok()
            w2_slot_free[slot] = E.tensor_tok()
        w1_free = E.tensor_tok()
        prev_phase_acts = ("S", E.cnt["S"])
        rd_done.append(rd_last)

    sc_ps_free = {}
    rsb_free = None
    xt_free2 = ln_done[NT - 1]
    w1_acts = ("S", E.cnt["S"])
    out_done = []
    ost = lg_sb[:, :]
    for ti in range(NT):
        xl = E.dma("scalar", xt[:], x_in[ti * 128:(ti + 1) * 128, :], waits=[xt_free2])
        mms = []
        for j in range(4):
            lr = E.dma("sync", rsb[:, :], Rd[:, ti * 4 + j, :],
                       waits=rd_done + ([rsb_free] if rsb_free else []))
            for half in range(2):
                E.emit("tensor",
                       (lambda t, _l=pt_sb[:, (ti * 4 + j) * 32:(ti * 4 + j) * 32 + 32],
                        _r=rsb[:, half * 512:(half + 1) * 512],
                        _o=psf[2 + half][32 * j:32 * j + 32, :], _j=j:
                        t.matmul(_o, _l, _r, start=True, stop=False,
                                 tile_position=(0, 32 * _j))),
                       waits=[lr, route_done[-1], sc_ps_free.get(half), w1_acts], inc="T")
                tt = E.emit("tensor",
                            (lambda t, _l=pt_sb[:, (ti * 4 + j) * 32:(ti * 4 + j) * 32 + 32],
                             _r=b2_sb[:, half * 512:(half + 1) * 512],
                             _o=psf[2 + half][32 * j:32 * j + 32, :], _j=j:
                             t.matmul(_o, _l, _r, start=False, stop=True,
                                      tile_position=(0, 32 * _j))),
                            waits=[t_b2], inc="T")
                mms.append(tt)
            rsb_free = E.tensor_tok()
        prevh = sc_ps_free.get("ost")
        for half in range(2):
            e1 = E.emit("vector",
                        (lambda v, _p=psf[2 + half]: v.tensor_scalar_mul(ost, _p[:], 0.5)),
                        waits=mms + ([prevh] if prevh else []), inc="V", strict=True)
            e3 = E.emit("vector",
                        (lambda v, _x=xt[:, half * 512:(half + 1) * 512]:
                         v.tensor_tensor(ost, ost, _x, ALU.add)),
                        waits=[e1, xl], inc="V", strict=True)
            sc_ps_free[half] = e3
            dm = E.dma("sync",
                       x_out[ti * 128:(ti + 1) * 128, half * 512:(half + 1) * 512], ost,
                       waits=[e3])
            prevh = dm
            out_done.append(dm)
        sc_ps_free["ost"] = prevh
        xt_free2 = sc_ps_free[1]

    if dbg:
        E.dma("gpsimd", d_pt[:, :], pt_sb[:, :], waits=[route_done[-1]])
        E.dma("gpsimd", d_xt[:, :, :, :], XT[:, :, :, :], waits=[gV, gS])
    E.emit("sync", lambda e: None, waits=out_done)
    if fused is None:
        E.finalize(ctx)
        ctx.close()
    return nc


def host_prep_l0attn(inp):
    H = {}
    C = lambda a: np.ascontiguousarray(a, dtype=np.float32)
    Wq = np.asarray(inp["L0_attn_Wqkv"])
    H["Wqkv"] = C(np.concatenate([Wq, Wq], axis=0))
    H["bqkv"] = C(np.asarray(inp["L0_attn_bqkv"]).reshape(1, 192))
    H["Wr"] = C(np.asarray(inp["L0_attn_Wr"]).reshape(8, 128, 1024).transpose(1, 0, 2))
    H["br"] = C(np.asarray(inp["L0_attn_br"]).reshape(1, 1024))
    mj = np.zeros((128, 4, 512), np.float32)
    for j in range(4):
        kk = np.arange(128)[:, None] + 128 * j
        qq = np.arange(512)[None, :]
        mj[:, j, :] = (kk < qq).astype(np.float32)
    H["masks"] = mj
    return H


def build_attn_pre(nc, fused=None, xin_name="x_in"):
    dio = _phase_io(nc, fused, [
        (xin_name, [TOK, D], F32, "ExternalInput"),
        ("ident", [128, 128], F32, "ExternalInput"),
    ] + ([("xhT_out", [128, DCH, TOK], BF16, "ExternalOutput")] if fused is None
         else [("xag_in", [128, DCH, TOK], BF16, "Internal")]))
    x_in, w_ident = dio[xin_name], dio["ident"]
    if fused is None:
        xhT_out = dio["xhT_out"]
        ctx = ExitStack()
        sb = lambda name, shape, dt: ctx.enter_context(nc.sbuf_tensor(name, shape, dt))
    else:
        xag_in = dio["xag_in"]
        fused["ar"].reset()
        sb = fused["ar"].alloc
    xt = sb("xt", [128, D], F32)
    xhat = sb("xhat", [128, D], BF16)
    xhatT = sb("xhatT", [128, DCH, TOK], BF16)
    ident_sb = sb("ident_sb", [128, 128], BF16)
    lnst = sb("lnst", [128, NT, 8], F32)
    lncst = sb("lncst", [128, 2], F32)
    sqb = sb("sqb", [128, D], F32)
    if fused is None:
        ps_tr = ctx.enter_context(nc.psum_tensor("ps_tr", [128, 128], BF16))[:]
        E = Em(nc)
    else:
        ps_tr = fused["P8"][7][:].bitcast(BF16)[:, 0:128]
        E = fused["E"]
    E.lnconsts = lncst
    E.ln_sq = sqb[:]
    t_id = E.dma("gpsimd", ident_sb[:], w_ident[:, :])
    E.emit("vector", lambda v: v.memset(lncst[:, 0:1], 1e-5), inc="V")
    E.emit("vector", lambda v: v.memset(lncst[:, 1:2], -0.5), inc="V")
    ln_done = {}
    tr_free = {}
    for ti in range(NT):
        xl = E.dma("sync", xt[:], x_in[ti * 128:(ti + 1) * 128, :], waits=[ln_done.get(ti - 1)])
        t9 = emit_ln(E, xt[:], xhat[:], lnst[:, ti, :], ready=[xl],
                     war=[tr_free.get((ti - 1, DCH - 1))])
        ln_done[ti] = t9
        for ch in range(DCH):
            tt = E.emit("tensor",
                        (lambda t, _s=xhat[:, ch * 128:(ch + 1) * 128]:
                         t.transpose(ps_tr, _s, ident_sb[:])),
                        waits=[t9, t_id, tr_free.get((ti, ch - 1)), tr_free.get((ti - 1, DCH - 1))],
                        inc="T")
            ev = E.emit("vector",
                        (lambda v, _d=xhatT[:, ch, ti * 128:(ti + 1) * 128]:
                         v.tensor_copy(_d, ps_tr)),
                        waits=[tt], inc="V")
            tr_free[(ti, ch)] = ev
    fin = tr_free[(NT - 1, DCH - 1)]
    dms = []
    if fused is None:
        dms.append(E.dma("sync", xhT_out[:, :, :], xhatT[:, :, :], waits=[fin]))
    else:
        dms.append(E.dma("sync", xag_in[:, :, :], xhatT[:, :, :], waits=[fin]))
    E.emit("sync", lambda e: None, waits=dms)
    if fused is None:
        E.finalize(ctx)
        ctx.close()
    return nc


def build_attn_core(nc, dbg=False, fused=None):
    S = 8192
    NKT = S // 128
    NQB = S // 512
    dio = _phase_io(nc, fused, [
        ("Wqkv", [128, 192], F32, "ExternalInput"),
        ("bqkv", [1, 192], F32, "ExternalInput"),
        ("masks", [128, 4, 512], F32, "ExternalInput"),
        ("ident", [128, 128], F32, "ExternalInput"),
    ] + ([("xhT", [128, 2, S], BF16, "ExternalInput"),
          ("oT_out", [128, 2, S], BF16, "ExternalOutput")] if fused is None
         else [("xag_out", [8 * 128, DCH, TOK], BF16, "Internal"),
               ("oa2a_in", [8, 128, 2, TOK], BF16, "Internal")]))
    w_Wqkv, w_bqkv, w_masks, w_ident = dio["Wqkv"], dio["bqkv"], dio["masks"], dio["ident"]
    if fused is None:
        xhT, oT_out = dio["xhT"], dio["oT_out"]
    else:
        xag_out, oa2a_in = dio["xag_out"], dio["oa2a_in"]
        pb_, ph_ = fused["pb"], fused["ph"]
    if dbg:
        d_qT = nc.dram_tensor("d_qT", [64, S], F32, kind="ExternalOutput")
        d_kT = nc.dram_tensor("d_kT", [64, S], F32, kind="ExternalOutput")
        d_vp = nc.dram_tensor("d_vp", [128, 32, 130], F32, kind="ExternalOutput")
        d_ts = nc.dram_tensor("d_ts", [64, 65], F32, kind="ExternalOutput")
        d_bo = nc.dram_tensor("d_bo", [128, 512], F32, kind="ExternalOutput")
        d_pb = nc.dram_tensor("d_pb", [128, 512], F32, kind="ExternalOutput")
    if fused is None:
        ctx = ExitStack()
        sb = lambda name, shape, dt: ctx.enter_context(nc.sbuf_tensor(name, shape, dt))
    else:
        fused["ar"].reset()
        sb = fused["ar"].alloc
    ident_sb = sb("ident_sb", [128, 128], BF16)
    xh = sb("xh", [128, 2, S], BF16)
    qT = sb("qT", [64, S], BF16)
    kT = sb("kT", [64, S], BF16)
    vp = sb("vp", [128, NKT // 2, 130], BF16)
    pb = sb("pb", [128, 4, 512], BF16)
    oh = sb("oh", [64, 4, S], BF16)
    masks = sb("masks_sb", [128, 4, 512], BF16)
    ones_sb = sb("ones_sb", [128, 512], BF16)
    tsT = sb("tsT", [64, 65], BF16)
    ts_col = sb("ts_col", [65, 64], BF16)
    bq_sb = sb("bq_sb", [1, 192], BF16)
    sums = sb("sums", [128, S], F32)
    onesf = sb("onesf", [128, 64], F32)
    Wq_sb = sb("Wq_sb", [128, 192], BF16)

    if fused is None:
        S4 = [ctx.enter_context(nc.psum_tensor(f"ps_s{i}", [128, 512], F32))[:]
              for i in range(4)]
        ps_o = [ctx.enter_context(nc.psum_tensor(f"ps_o{i}", [128, 512], F32))[:]
                for i in range(2)]
        ps_q = ctx.enter_context(nc.psum_tensor("ps_q", [128, 512], F32))[:]
        ps_tr = ctx.enter_context(nc.psum_tensor("ps_tr", [128, 128], BF16))[:]
        E = Em(nc)
    else:
        P8 = fused["P8"]
        S4 = [P8[0][:], P8[1][:], P8[5][:], P8[6][:]]
        ps_o = [P8[2][:], P8[3][:]]
        ps_q = P8[4][:]
        ps_tr = P8[7][:].bitcast(BF16)[:, 0:128]
        E = fused["E"]
    if fused is None:
        t_x = E.dma("sync", xh[:, :, :], xhT[:, :, :])
    else:
        for ch in range(2):
            v = (xag_out.ap().rearrange("(s p) c t -> p c s t", p=128)
                 [:, ch:ch + 1, 0:4, :].squeeze(1))
            vin = _dyn(v, pb_ * (4 * 128 * DCH * TOK) + ph_ * (2 * TOK))
            t_x = E.dma("gpsimd", xh[:, ch, :].rearrange("p (r t) -> p r t", r=4), vin)
    t_w = E.dma("gpsimd", Wq_sb[:], w_Wqkv[:, :])
    t_b = E.dma("gpsimd", bq_sb[:], w_bqkv[:, :])
    t_m = E.dma("gpsimd", masks[:], w_masks[:, :, :])
    t_id = E.dma("gpsimd", ident_sb[:], w_ident[:, :])
    E.emit("vector", lambda v: v.memset(ones_sb[:], 1.0), inc="V")
    E.emit("vector", lambda v: v.memset(onesf[:], 1.0), inc="V")
    ones_tok = ("V", E.cnt["V"])

    oh_dmas = []
    bg = {}
    pb_guard = {}
    og = {}
    for hh in range(4):
        chunk = hh // 2
        base = 64 * (hh % 2)
        tp = (base, 0)
        nb = 0
        qk_evac = None
        for qc in range(2):
            dstT = qT if qc == 0 else kT
            for tt_i in range(NQB):
                c0 = tt_i * 512
                idx = nb % 4
                bank = S4[idx]
                nb += 1
                E.emit("tensor",
                       (lambda t, _l=Wq_sb[base:base + 64, qc * 64:(qc + 1) * 64],
                        _r=xh[base:base + 64, chunk, c0:c0 + 512], _tp=tp, _o=bank:
                        t.matmul(_o[0:64, :], _l, _r, start=True, stop=False,
                                 tile_position=_tp)),
                       waits=[t_x, t_w, bg.get(idx)], inc="T")
                m2 = E.emit("tensor",
                            (lambda t, _l=bq_sb[0:1, qc * 64:(qc + 1) * 64], _o=bank:
                             t.matmul(_o[0:64, :], _l, ones_sb[0:1, :],
                                      start=False, stop=True)),
                            waits=[t_b, ones_tok], inc="T")
                ev = E.emit("vector",
                            (lambda v, _d=dstT[0:64, c0:c0 + 512], _o=bank:
                             v.tensor_copy(_d, _o[0:64, :])),
                            waits=[m2], inc="V")
                bg[idx] = ev
                qk_evac = ev
        v_evac = None
        for kt in range(NKT):
            c0 = kt * 128
            idx = nb % 4
            bank = S4[idx]
            nb += 1
            E.emit("tensor",
                   (lambda t, _l=xh[base:base + 64, chunk, c0:c0 + 128],
                    _r=Wq_sb[base:base + 64, 128:192], _tp=tp, _o=bank:
                    t.matmul(_o[:, 0:64], _l, _r, start=True, stop=False, tile_position=_tp)),
                   waits=[t_x, t_w, bg.get(idx)], inc="T")
            E.emit("tensor",
                   (lambda t, _o=bank:
                    t.matmul(_o[:, 0:64], ones_sb[0:1, 0:128], bq_sb[0:1, 128:192],
                             start=False, stop=False)),
                   waits=[t_b], inc="T")
            m3 = E.emit("tensor",
                        (lambda t, _o=bank:
                         t.matmul(_o[:, 64:65], ones_sb[0:1, 0:128],
                                  ones_sb[0:1, 0:1], start=False, stop=True)),
                        inc="T")
            ev = E.emit("vector",
                        (lambda v, _d=vp[:, kt // 2, (kt % 2) * 65:(kt % 2) * 65 + 65], _o=bank:
                         v.tensor_copy(_d, _o[:, 0:65])),
                        waits=[m3], inc="V")
            bg[idx] = ev
            v_evac = ev
        ts_mm = None
        for kt in range(NKT):
            ts_mm = E.emit("tensor",
                           (lambda t, _l=vp[:, kt // 2, (kt % 2) * 65:(kt % 2) * 65 + 65],
                            _c=63 - kt:
                            t.matmul(ps_q[0:65, _c:_c + 1], _l, ones_sb[:, 0:1],
                                     start=True, stop=True)),
                           waits=[v_evac, bg.get("q")], inc="T")
        e1 = E.emit("vector", lambda v: v.tensor_copy(ts_col[:, :], ps_q[0:65, 0:64]),
                    waits=[ts_mm], inc="V")
        bg["q"] = e1
        tr1 = E.emit("tensor", lambda t: t.transpose(ps_tr[0:64, 0:65], ts_col[:, :],
                                                     ident_sb[0:65, 0:65]),
                     waits=[e1, t_id, bg.get("tr")], inc="T")
        e2 = E.emit("vector", lambda v: v.tensor_copy(tsT[:, :], ps_tr[0:64, 0:65]),
                    waits=[tr1], inc="V")
        bg["tr"] = e2

        seq = [(B, kt) for B in range(NQB) for kt in range(4 * B + 4)]
        exp_tok = {}

        def emit_qk(i, _seq=seq, _hh=hh, _qk=qk_evac):
            B, kt = _seq[i]
            idx = i % 4
            bank = S4[idx]
            q0 = B * 512
            sm = E.emit("tensor",
                        (lambda t, _l=kT[0:64, kt * 128:(kt + 1) * 128],
                         _r=qT[0:64, q0:q0 + 512], _o=bank:
                         t.matmul(_o, _l, _r, start=True, stop=True)),
                        waits=[_qk, bg.get(idx)], inc="T")
            if kt >= 4 * B:
                j = kt - 4 * B
                sm = E.emit("vector",
                            (lambda v, _o=bank, _m=masks[:, j, :]:
                             v.tensor_tensor(_o, _o, _m, ALU.mult)),
                            waits=[sm, t_m], inc="V")
            ex = E.emit("scalar",
                        (lambda s, _o=bank, _d=pb[:, idx, :]:
                         s.activation(out=_d, in_=_o, func=AF.Exp, scale=0.125)),
                        waits=[sm, pb_guard.get(idx)], inc="S")
            exp_tok[i] = ex
            bg[idx] = ex

        LOOK = 4
        for i in range(min(LOOK, len(seq))):
            emit_qk(i)
        qnext = LOOK
        for i, (B, kt) in enumerate(seq):
            bo = ps_o[B % 2]
            if kt == 0 and B < NQB - 1:
                ksuf = 4 * (NQB - 1 - B)
                E.emit("tensor",
                       (lambda t, _k=ksuf, _o=bo: t.matmul(_o[0:65, :], tsT[0:_k, :],
                                                           ones_sb[0:_k, :],
                                                           start=True, stop=False)),
                       waits=[e2, og.get(B % 2)], inc="T")
            av = E.emit("tensor",
                        (lambda t, _l=vp[:, kt // 2, (kt % 2) * 65:(kt % 2) * 65 + 65],
                         _r=pb[:, i % 4, :], _o=bo,
                         _st=(B == NQB - 1 and kt == 0), _sp=(kt == 4 * B + 3):
                         t.matmul(_o[0:65, :], _l, _r, start=_st, stop=_sp)),
                        waits=[exp_tok.pop(i), v_evac, og.get(B % 2)], inc="T")
            pb_guard[i % 4] = av
            if qnext < len(seq):
                emit_qk(qnext)
                qnext += 1
            if kt == 4 * B + 3:
                q0 = B * 512
                E.emit("vector",
                       (lambda v, _d=oh[0:64, hh, q0:q0 + 512], _o=bo:
                        v.tensor_copy(_d, _o[0:64, :])),
                       waits=[av], inc="V")
                evs = E.emit("vector",
                             (lambda v, _d=sums[64:65, q0:q0 + 512], _o=bo:
                              v.tensor_copy(_d, _o[64:65, :])),
                             waits=[av], inc="V")
                og[B % 2] = evs
        rcp = E.emit("vector", lambda v: v.reciprocal(sums[64:65, :], sums[64:65, :]),
                     waits=[og[0], og[1]], inc="V", strict=True)
        nm = None
        for j in range(NQB):
            idx = j % 4
            bank = S4[idx]
            bc = E.emit("tensor",
                        (lambda t, _o=bank, _r=sums[64:65, j * 512:(j + 1) * 512]:
                         t.matmul(_o[0:64, :], onesf[64:65, 0:64], _r, start=True, stop=True,
                                  tile_position=(64, 0))),
                        waits=[rcp, bg.get(idx)], inc="T")
            nm = E.emit("vector",
                        (lambda v, _d=oh[0:64, hh, j * 512:(j + 1) * 512], _o=bank:
                         v.tensor_tensor(_d, _d, _o[0:64, :], ALU.mult)),
                        waits=[bc], inc="V", strict=True)
            bg[idx] = nm
        if dbg and hh == 0:
            E.dma("gpsimd", d_qT[:, :], qT[:, :], waits=[qk_evac])
            E.dma("gpsimd", d_kT[:, :], kT[:, :], waits=[qk_evac])
            E.dma("gpsimd", d_vp[:, :, :], vp[:, :, :], waits=[v_evac])
            E.dma("gpsimd", d_ts[:, :], tsT[:, :], waits=[e2])
        odone = [nm]
        if fused is None:
            oh_dmas.append(E.dma("sync", oT_out[base:base + 64, chunk, :],
                                 oh[0:64, hh, :], waits=odone))
        else:
            for m in range(4):
                ov = oa2a_in[m, base:base + 64, chunk, :]
                ov = _dyn(ov, pb_ * (4 * 128 * 2 * TOK))
                oh_dmas.append(E.dma("gpsimd", ov,
                                     oh[0:64, hh, m * TOK:(m + 1) * TOK], waits=odone))
    E.emit("sync", lambda e: None, waits=oh_dmas)
    if fused is None:
        E.finalize(ctx)
        ctx.close()
    return nc


def build_attn_proj(nc, fused=None, xin_name="x_in", xout_name="x_out"):
    dio = _phase_io(nc, fused, [
        (xin_name, [TOK, D], F32, "ExternalInput"),
        (xout_name, [TOK, D], F32, "ExternalOutput"),
        ("Wr", [128, 8, 1024], F32, "ExternalInput"),
        ("br", [1, 1024], F32, "ExternalInput"),
    ] + ([("oT", [128, DCH, TOK], BF16, "ExternalInput")] if fused is None
         else [("oa2a_out", [8, 128, 2, TOK], BF16, "Internal")]))
    x_in, x_out, w_Wr, w_br = dio[xin_name], dio[xout_name], dio["Wr"], dio["br"]
    if fused is None:
        oT = dio["oT"]
        ctx = ExitStack()
        sb = lambda name, shape, dt: ctx.enter_context(nc.sbuf_tensor(name, shape, dt))
    else:
        oa2a_out = dio["oa2a_out"]
        fused["ar"].reset()
        sb = fused["ar"].alloc
    o_sb = sb("o_sb", [128, DCH, TOK], BF16)
    Wr_sb = sb("Wr_sb", [128, 8, 1024], BF16)
    br_sb = sb("br_sb", [1, 1024], BF16)
    xt = sb("xt", [128, D], F32)
    ost = sb("ost", [128, D], F32)
    ones_sb = sb("ones_sb", [1, 128], BF16)
    if fused is None:
        ps = [ctx.enter_context(nc.psum_tensor(f"ps{i}", [128, 512], F32))[:]
              for i in range(2)]
        E = Em(nc)
        t_o = E.dma("sync", o_sb[:, :, :], oT[:, :, :])
    else:
        P8 = fused["P8"]
        ps = [P8[0][:], P8[1][:]]
        E = fused["E"]
        pb_ = fused["pb"]
        for r_ in range(4):
            v = oa2a_out[r_, :, :, :]
            vin = _dyn(v, pb_ * (4 * 128 * 2 * TOK))
            t_o = E.dma("gpsimd", o_sb[:, 2 * r_:2 * r_ + 2, :], vin)
    t_w = E.dma("gpsimd", Wr_sb[:], w_Wr[:, :, :])
    t_b = E.dma("gpsimd", br_sb[:], w_br[:, :])
    ot1 = E.emit("vector", lambda v: v.memset(ones_sb[:], 1.0), inc="V")
    guard = {}
    outs = []
    for ti in range(NT):
        xl = E.dma("scalar", xt[:], x_in[ti * 128:(ti + 1) * 128, :], waits=[guard.get("xt")])
        for half in range(2):
            bank = ps[half]
            for och in range(DCH):
                E.emit("tensor",
                       (lambda t, _l=o_sb[:, och, ti * 128:(ti + 1) * 128],
                        _r=Wr_sb[:, och, half * 512:(half + 1) * 512], _o=bank,
                        _st=(och == 0):
                        t.matmul(_o, _l, _r, start=_st, stop=False)),
                       waits=[t_o, t_w, guard.get(("ps", half))], inc="T")
            tb = E.emit("tensor",
                        (lambda t, _o=bank, _r=br_sb[0:1, half * 512:(half + 1) * 512]:
                         t.matmul(_o, ones_sb[0:1, :], _r, start=False, stop=True)),
                        waits=[t_b, ot1], inc="T")
            e1 = E.emit("vector",
                        (lambda v, _o=bank, _d=ost[:, half * 512:(half + 1) * 512],
                         _x=xt[:, half * 512:(half + 1) * 512]:
                         v.tensor_tensor(_d, _o, _x, ALU.add)),
                        waits=[tb, xl, guard.get("ost")], inc="V")
            guard[("ps", half)] = e1
        guard["xt"] = e1
        dm = E.dma("sync", x_out[ti * 128:(ti + 1) * 128, :], ost[:], waits=[e1])
        guard["ost"] = dm
        outs.append(dm)
    E.emit("sync", lambda e: None, waits=outs)
    if fused is None:
        E.finalize(ctx)
        ctx.close()
    return nc


def host_prep_conv(inp):
    H = {}
    C = lambda a: np.ascontiguousarray(a, dtype=np.float32)
    N1 = N2 = 128
    NFULL = N1 * N2
    K2 = 65
    t2 = np.arange(N2)[:, None]
    k2 = np.arange(K2)[None, :]
    H["L1c"] = C(np.concatenate([np.cos(2 * np.pi * t2 * k2 / N2),
                                 -np.sin(2 * np.pi * t2 * k2 / N2)], axis=1))
    t1 = np.arange(N1)[:, None]
    k1 = np.arange(N1)[None, :]
    M2 = np.zeros((128, K2, 3, 128), np.float32)
    for q in range(K2):
        ang = 2 * np.pi * t1 * (q + 128 * k1) / NFULL
        M2[:, q, 0, :] = np.cos(ang)
        M2[:, q, 1, :] = np.sin(ang)
        M2[:, q, 2, :] = -np.sin(ang)
    H["M2"] = C(M2)
    k1c = np.arange(N1)[:, None]
    jb = np.arange(N1)[None, :]
    H["I1"] = C(np.stack([
        np.concatenate([np.cos(2 * np.pi * k1c * jb / N1),
                        np.sin(2 * np.pi * k1c * jb / N1)], axis=1),
        np.concatenate([-np.sin(2 * np.pi * k1c * jb / N1),
                        np.cos(2 * np.pi * k1c * jb / N1)], axis=1)], axis=1))
    alpha = np.full(K2, 2.0)
    alpha[0] = 1.0
    alpha[64] = 1.0
    ja = np.arange(64)[None, :]
    k2c = np.arange(K2)[:, None]
    I2 = np.zeros((65, 128, 2, 64), np.float32)
    for b in range(N1):
        ang = 2 * np.pi * (b + 128 * ja) * k2c / NFULL
        I2[:, b, 0, :] = alpha[:, None] * np.cos(ang) / NFULL
        I2[:, b, 1, :] = -alpha[:, None] * np.sin(ang) / NFULL
    H["I2"] = C(I2)
    H["Wp"] = C(np.asarray(inp["L1_attn_Wp"]).reshape(8, 128, 4096).transpose(1, 0, 2))
    H["bp"] = C(np.asarray(inp["L1_attn_bp"]).reshape(1, 4096))
    H["Wf"] = C(np.asarray(inp["L1_attn_Wf"]).reshape(8, 128, 3072).transpose(1, 0, 2))
    H["bf"] = C(np.asarray(inp["L1_attn_bf"]).reshape(1, 3072))
    T, Dm = 8192, 1024
    pos = np.arange(T, dtype=np.float64)[:, None]
    i = np.arange(Dm)[None, :]
    angle = pos / np.power(10000.0, (2 * (i // 2)).astype(np.float64) / Dm)
    pe = np.where(i % 2 == 0, np.sin(angle), np.cos(angle)).astype(np.float32)
    H["posT"] = C(pe.reshape(8192, 8, 128).transpose(2, 1, 0))
    return H


def build_xp(nc, fused=None, xin_name="x_in"):
    dio = _phase_io(nc, fused, [
        (xin_name, [TOK, D], F32, "ExternalInput"),
        ("Wp", [128, 8, 4096], F32, "ExternalInput"),
        ("bp", [1, 4096], F32, "ExternalInput"),
        ("Wf", [128, 8, 3072], F32, "ExternalInput"),
        ("bf", [1, 3072], F32, "ExternalInput"),
        ("posT", [128, 8, 1024], F32, "ExternalInput"),
        ("ident", [128, 128], F32, "ExternalInput"),
    ] + ([("xp_out", [TOK, 4096], BF16, "ExternalOutput"),
          ("f_out", [1024, 3072], BF16, "ExternalOutput")] if fused is None
         else [("xpa2a_in", [8, 4, TOK, 128], BF16, "Internal"),
               ("fa2a_in", [8, 3, 1024, 128], BF16, "Internal")]))
    x_in, w_ident, w_pos = dio[xin_name], dio["ident"], dio["posT"]
    w_Wp, w_bp, w_Wf, w_bf = dio["Wp"], dio["bp"], dio["Wf"], dio["bf"]
    if fused is None:
        xp_out, f_out = dio["xp_out"], dio["f_out"]
        ctx = ExitStack()
        sb = lambda name, shape, dt: ctx.enter_context(nc.sbuf_tensor(name, shape, dt))
    else:
        xpa2a_in, fa2a_in = dio["xpa2a_in"], dio["fa2a_in"]
        fused["ar"].reset()
        sb = fused["ar"].alloc
    xt = sb("xt", [128, D], F32)
    xhat = sb("xhat", [128, D], BF16)
    xhatT = sb("xhatT", [128, DCH, TOK], BF16)
    posT_sb = sb("posT_sb", [128, DCH, 1024], BF16)
    Wp_sb = sb("Wp_sb", [128, 8, 4096], BF16)
    Wf_sb = sb("Wf_sb", [128, 8, 3072], BF16)
    bp_sb = sb("bp_sb", [1, 4096], BF16)
    bf_sb = sb("bf_sb", [1, 3072], BF16)
    ident_sb = sb("ident_sb", [128, 128], BF16)
    ones_sb = sb("ones_sb", [1, 128], BF16)
    ost = sb("ost", [128, 512], BF16)
    lnst = sb("lnst", [128, NT, 8], F32)
    lncst = sb("lncst", [128, 2], F32)
    sqb = sb("sqb", [128, D], F32)
    if fused is None:
        ps = [ctx.enter_context(nc.psum_tensor(f"ps{i}", [128, 512], F32))[:]
              for i in range(2)]
        ps_tr = ctx.enter_context(nc.psum_tensor("ps_tr", [128, 128], BF16))[:]
        E = Em(nc)
    else:
        P8 = fused["P8"]
        ps = [P8[0][:], P8[1][:]]
        ps_tr = P8[7][:].bitcast(BF16)[:, 0:128]
        E = fused["E"]
    E.lnconsts = lncst
    E.ln_sq = sqb[:]
    t_id = E.dma("gpsimd", ident_sb[:], w_ident[:, :])
    t_wp = E.dma("gpsimd", Wp_sb[:], w_Wp[:, :, :])
    t_bp = E.dma("gpsimd", bp_sb[:], w_bp[:, :])
    t_wf = E.dma("gpsimd", Wf_sb[:], w_Wf[:, :, :])
    t_bf = E.dma("gpsimd", bf_sb[:], w_bf[:, :])
    t_pos = E.dma("gpsimd", posT_sb[:], w_pos[:, :, :])
    E.emit("vector", lambda v: v.memset(lncst[:, 0:1], 1e-5), inc="V")
    E.emit("vector", lambda v: v.memset(lncst[:, 1:2], -0.5), inc="V")
    t_on = E.emit("vector", lambda v: v.memset(ones_sb[:], 1.0), inc="V")
    ln_done = {}
    tr_free = {}
    for ti in range(NT):
        xl = E.dma("sync", xt[:], x_in[ti * 128:(ti + 1) * 128, :], waits=[ln_done.get(ti - 1)])
        t9 = emit_ln(E, xt[:], xhat[:], lnst[:, ti, :], ready=[xl],
                     war=[tr_free.get((ti - 1, DCH - 1))])
        ln_done[ti] = t9
        for ch in range(DCH):
            tt = E.emit("tensor",
                        (lambda t, _s=xhat[:, ch * 128:(ch + 1) * 128]:
                         t.transpose(ps_tr, _s, ident_sb[:])),
                        waits=[t9, t_id, tr_free.get((ti, ch - 1)), tr_free.get((ti - 1, DCH - 1))],
                        inc="T")
            ev = E.emit("vector",
                        (lambda v, _d=xhatT[:, ch, ti * 128:(ti + 1) * 128]:
                         v.tensor_copy(_d, ps_tr)),
                        waits=[tt], inc="V")
            tr_free[(ti, ch)] = ev
    xT_all = ev
    guard = {}
    outs = []
    for ti in range(NT):
        for cb in range(8):
            bank = ps[cb % 2]
            for dch in range(DCH):
                E.emit("tensor",
                       (lambda t, _l=xhatT[:, dch, ti * 128:(ti + 1) * 128],
                        _r=Wp_sb[:, dch, cb * 512:(cb + 1) * 512], _o=bank, _st=(dch == 0):
                        t.matmul(_o, _l, _r, start=_st, stop=False)),
                       waits=[xT_all, t_wp, guard.get(cb % 2)], inc="T")
            tb = E.emit("tensor",
                        (lambda t, _o=bank, _r=bp_sb[0:1, cb * 512:(cb + 1) * 512]:
                         t.matmul(_o, ones_sb[0:1, :], _r, start=False, stop=True)),
                        waits=[t_bp, t_on], inc="T")
            ev = E.emit("vector", (lambda v, _o=bank: v.tensor_copy(ost[:], _o)),
                        waits=[tb, guard.get("ost")], inc="V")
            if fused is None:
                dm = E.dma("sync", xp_out[ti * 128:(ti + 1) * 128, cb * 512:(cb + 1) * 512],
                           ost[:], waits=[ev])
            else:
                s_, mh = cb // 2, (cb % 2) * 4
                dm = E.dma("sync",
                           xpa2a_in[mh:mh + 4, s_, ti * 128:(ti + 1) * 128, :]
                           .rearrange("m p c -> p m c"),
                           ost[:].rearrange("p (m c) -> p m c", c=128), waits=[ev])
            guard[cb % 2] = ev
            guard["ost"] = dm
            outs.append(dm)
    for ti in range(8):
        for cb in range(6):
            bank = ps[cb % 2]
            for dch in range(DCH):
                E.emit("tensor",
                       (lambda t, _l=posT_sb[:, dch, ti * 128:(ti + 1) * 128],
                        _r=Wf_sb[:, dch, cb * 512:(cb + 1) * 512], _o=bank, _st=(dch == 0):
                        t.matmul(_o, _l, _r, start=_st, stop=False)),
                       waits=[t_pos, t_wf, guard.get(cb % 2)], inc="T")
            tb = E.emit("tensor",
                        (lambda t, _o=bank, _r=bf_sb[0:1, cb * 512:(cb + 1) * 512]:
                         t.matmul(_o, ones_sb[0:1, :], _r, start=False, stop=True)),
                        waits=[t_bf, t_on], inc="T")
            ev = E.emit("vector", (lambda v, _o=bank: v.tensor_copy(ost[:], _o)),
                        waits=[tb, guard.get("ost")], inc="V")
            if fused is None:
                dm = E.dma("sync", f_out[ti * 128:(ti + 1) * 128, cb * 512:(cb + 1) * 512],
                           ost[:], waits=[ev])
            else:
                s_, mh = cb // 2, (cb % 2) * 4
                dm = E.dma("sync",
                           fa2a_in[mh:mh + 4, s_, ti * 128:(ti + 1) * 128, :]
                           .rearrange("m p c -> p m c"),
                           ost[:].rearrange("p (m c) -> p m c", c=128), waits=[ev])
            guard[cb % 2] = ev
            guard["ost"] = dm
            outs.append(dm)
    E.emit("sync", lambda e: None, waits=outs)
    if fused is None:
        E.finalize(ctx)
        ctx.close()
    return nc


def build_conv(nc, dbg=False, maxph=99, fused=None):
    G = 64
    K2 = 65
    dio = _phase_io(nc, fused, [
        ("L1c", [128, 130], F32, "ExternalInput"),
        ("M2", [128, 65, 3, 128], F32, "ExternalInput"),
        ("I1", [128, 2, 256], F32, "ExternalInput"),
        ("I2", [65, 128, 2, 64], F32, "ExternalInput"),
    ] + ([("u_in", [4, 2, 128, 64, 128], BF16, "ExternalInput"),
          ("f_in", [3, 128, 64, 128], BF16, "ExternalInput"),
          ("y_out", [2, 128, 64, 128], BF16, "ExternalOutput")] if fused is None
         else [("xpa2a_out", [8, 4, TOK, 128], BF16, "Internal"),
               ("fa2a_out", [8, 3, 1024, 128], BF16, "Internal"),
               ("ya2a_in", [8, 128, TOK], BF16, "Internal")]))
    c_L1, c_M2, c_I1, c_I2 = dio["L1c"], dio["M2"], dio["I1"], dio["I2"]
    if fused is None:
        u_in, f_in, y_out = dio["u_in"], dio["f_in"], dio["y_out"]
    else:
        xpa2a_out, fa2a_out, ya2a_in = dio["xpa2a_out"], dio["fa2a_out"], dio["ya2a_in"]
    if dbg:
        d_A = nc.dram_tensor("d_A", [128, 2, 65, 64], F32, kind="ExternalOutput")
        d_X = nc.dram_tensor("d_X", [128, 2, 65, 64], F32, kind="ExternalOutput")
        d_G = nc.dram_tensor("d_G", [128, 2, 65, 64], F32, kind="ExternalOutput")
        d_Z = nc.dram_tensor("d_Z", [128, 2, 64, 65], F32, kind="ExternalOutput")
        d_At = nc.dram_tensor("d_At", [65, 2, 128, 64], F32, kind="ExternalOutput")
    if fused is None:
        ctx = ExitStack()
        sb = lambda name, shape, dt: ctx.enter_context(nc.sbuf_tensor(name, shape, dt))
    else:
        fused["ar"].reset()
        sb = fused["ar"].alloc
    M2_sb = sb("M2_sb", [128, 65, 3, 128], BF16)
    I2_sb = sb("I2_sb", [65, 128, 2, 64], BF16)
    I1_sb = sb("I1_sb", [128, 2, 256], BF16)
    L1_sb = sb("L1_sb", [128, 130], BF16)
    yD1 = sb("yD1", [64, G, 128], BF16)
    fD1 = sb("fD1", [64, G, 128], BF16)
    gD1 = sb("gD1", [64, G, 128], BF16) if fused is None else None
    gD1s = sb("gD1s", [64, 128, G], BF16)[:] if fused is not None else None
    Ab_t = sb("Ab_t", [128, 2, K2, G], BF16)
    Fb_t = sb("Fb_t", [128, 2, K2, G], BF16)
    arena = sb("arena", [128, 16640], BF16)
    Xb = arena[:, 0:8320].rearrange("p (x q c) -> p x q c", x=2, q=K2)
    Gb = arena[:, 8320:16640].rearrange("p (x q c) -> p x q c", x=2, q=K2)
    At = arena[0:65, 0:16384].rearrange("p (x j c) -> p x j c", x=2, j=128)
    Ab = Ab_t[:]
    Fb = Fb_t[:]
    Zb = Ab_t[:].rearrange("p x q c -> p x c q")
    Sc = Fb_t[:].rearrange("p x q c -> p x c q")

    if fused is None:
        psf = [ctx.enter_context(nc.psum_tensor(f"cps{i}", [128, 512], F32))[:]
               for i in range(8)]
        E = Em(nc)
    else:
        psf = [p[:] for p in fused["P8"]]
        E = fused["E"]
    uSt = (Ab_t[:].rearrange("p x q c -> p (x q c)")[0:64, 0:128 * G]
           .rearrange("p (a c) -> p a c", c=G)) if fused is not None else None
    t_m2 = E.dma("gpsimd", M2_sb[:], c_M2[:, :, :, :])
    t_i2 = E.dma("gpsimd", I2_sb[:], c_I2[:, :, :, :])
    t_i1 = E.dma("gpsimd", I1_sb[:], c_I1[:, :, :])
    t_l1 = E.dma("gpsimd", L1_sb[:], c_L1[:, :])
    gd = {}
    out_dmas = []

    def vs():
        return [("V", E.cnt["V"]), ("S", E.cnt["S"])]

    def fwd_l1(srcD1, Adst, tok_src, war):
        for c0 in range(0, G, 3):
            ncc = min(3, G - c0)
            bi = (c0 // 3) % 2
            bank = psf[bi]
            mm = None
            for j in range(ncc):
                mm = E.emit("tensor",
                            (lambda t, _l=srcD1[0:64, c0 + j, :],
                             _o=bank[:, j * 130:(j + 1) * 130]:
                             t.matmul(_o, _l, L1_sb[0:64, :], start=True, stop=True)),
                            waits=[tok_src, t_l1] + gd.get(("l1b", bi), []), inc="T")
            bv = bank[:, 0:ncc * 130].rearrange("p (c q) -> p c q", q=130)
            eng = "vector" if bi == 0 else "scalar"
            sem = "V" if bi == 0 else "S"
            def mkcopy(comp, _bv=bv, _ncc=ncc, _c0=c0, _Adst=Adst):
                s_ap = _bv[:, :, comp * K2:(comp + 1) * K2].rearrange("p c q -> p q c")
                d_ap = _Adst[:, comp, :, _c0:_c0 + _ncc]
                return lambda en: (en.tensor_copy(d_ap, s_ap) if hasattr(en, "tensor_copy")
                                   else en.copy(d_ap, s_ap))
            E.emit(eng, mkcopy(0), waits=[mm] + war, inc=sem)
            e2 = E.emit(eng, mkcopy(1), waits=[], inc=sem)
            gd[("l1b", bi)] = [e2]
        return vs()

    def fwd_l2(Asrc, Fsrc, Xdst, Gdst, tokAF, war):
        for o0 in range(0, K2, 8):
            nq = min(8, K2 - o0)
            par = (o0 // 8) % 2
            bks = [psf[4 * par + i] for i in range(4)]
            mm = None
            for qq in range(nq):
                q = o0 + qq
                outs = [b[:, qq * 64:(qq + 1) * 64] for b in bks]
                mm = E.emit("tensor",
                            (lambda t, _l=M2_sb[:, q, 0, :], _r=Asrc[:, 0, q, :], _o=outs[0]:
                             t.matmul(_o, _l, _r, start=True, stop=False)),
                            waits=[t_m2] + tokAF + gd.get(("l2b", par), []), inc="T")
                E.emit("tensor", (lambda t, _l=M2_sb[:, q, 0, :], _r=Asrc[:, 1, q, :], _o=outs[1]:
                                  t.matmul(_o, _l, _r, start=True, stop=False)), inc="T")
                E.emit("tensor", (lambda t, _l=M2_sb[:, q, 0, :], _r=Fsrc[:, 0, q, :], _o=outs[2]:
                                  t.matmul(_o, _l, _r, start=True, stop=False)), inc="T")
                E.emit("tensor", (lambda t, _l=M2_sb[:, q, 0, :], _r=Fsrc[:, 1, q, :], _o=outs[3]:
                                  t.matmul(_o, _l, _r, start=True, stop=False)), inc="T")
                E.emit("tensor", (lambda t, _l=M2_sb[:, q, 1, :], _r=Asrc[:, 1, q, :], _o=outs[0]:
                                  t.matmul(_o, _l, _r, start=False, stop=True)), inc="T")
                E.emit("tensor", (lambda t, _l=M2_sb[:, q, 1, :], _r=Fsrc[:, 1, q, :], _o=outs[2]:
                                  t.matmul(_o, _l, _r, start=False, stop=True)), inc="T")
                E.emit("tensor", (lambda t, _l=M2_sb[:, q, 2, :], _r=Asrc[:, 0, q, :], _o=outs[1]:
                                  t.matmul(_o, _l, _r, start=False, stop=True)), inc="T")
                mm = E.emit("tensor",
                            (lambda t, _l=M2_sb[:, q, 2, :], _r=Fsrc[:, 0, q, :], _o=outs[3]:
                             t.matmul(_o, _l, _r, start=False, stop=True)), inc="T")
            toks = []
            for i, (dst, comp) in enumerate(((Xdst, 0), (Xdst, 1), (Gdst, 0), (Gdst, 1))):
                eng = "vector" if i == 2 else "scalar"
                def bld(en, _b=bks[i], _d=dst[:, comp, o0:o0 + nq, :], _n=nq):
                    srcap = _b[:, 0:_n * 64].rearrange("p (q c) -> p q c", c=64)
                    return en.tensor_copy(_d, srcap) if hasattr(en, "tensor_copy") else en.copy(_d, srcap)
                toks.append(E.emit(eng, bld, waits=[mm] + war,
                                   inc="V" if eng == "vector" else "S"))
            gd[("l2b", par)] = toks
        return vs() + [E.tensor_tok()]

    def pointwise(war):
        t1 = E.emit("gpsimd", lambda v: v.tensor_tensor(
            Sc[:, 0], Xb[:, 0].rearrange("p q c -> p c q"),
            Gb[:, 0].rearrange("p q c -> p c q"), ALU.mult), waits=war, inc="P", strict=True)
        t2 = E.emit("vector", lambda v: v.tensor_tensor(
            Sc[:, 1], Xb[:, 1].rearrange("p q c -> p c q"),
            Gb[:, 1].rearrange("p q c -> p c q"), ALU.mult), waits=war, inc="V", strict=True)
        t3 = E.emit("vector", lambda v: v.tensor_tensor(
            Zb[:, 0], Sc[:, 0], Sc[:, 1], ALU.subtract), waits=[t1, t2], inc="V", strict=True)
        t4 = E.emit("gpsimd", lambda v: v.tensor_tensor(
            Sc[:, 0], Xb[:, 0].rearrange("p q c -> p c q"),
            Gb[:, 1].rearrange("p q c -> p c q"), ALU.mult), waits=[t3], inc="P", strict=True)
        t5 = E.emit("vector", lambda v: v.tensor_tensor(
            Sc[:, 1], Xb[:, 1].rearrange("p q c -> p c q"),
            Gb[:, 0].rearrange("p q c -> p c q"), ALU.mult), waits=[t3], inc="V", strict=True)
        t6 = E.emit("vector", lambda v: v.tensor_tensor(
            Zb[:, 1], Sc[:, 0], Sc[:, 1], ALU.add), waits=[t4, t5], inc="V", strict=True)
        return [t3, t6]

    def inv_l1(war):
        for c0 in range(0, G, 2):
            bi = (c0 // 2) % 4
            bank = psf[bi]
            mm = None
            for j in range(2):
                c = c0 + j
                E.emit("tensor",
                       (lambda t, _l=Zb[:, 0, c, :], _o=bank[0:65, j * 256:(j + 1) * 256]:
                        t.matmul(_o, _l, I1_sb[:, 0, :], start=True, stop=False)),
                       waits=[t_i1] + gd.get(("i1b", bi), []) + war, inc="T")
                mm = E.emit("tensor",
                            (lambda t, _l=Zb[:, 1, c, :], _o=bank[0:65, j * 256:(j + 1) * 256]:
                             t.matmul(_o, _l, I1_sb[:, 1, :], start=False, stop=True)),
                            inc="T")
            bv = bank[0:65, 0:512].rearrange("p (c x j) -> p c x j", x=2, j=128)
            eng = "vector" if bi % 2 == 0 else "scalar"
            sem = "V" if bi % 2 == 0 else "S"
            def mkcopy(comp, _bv=bv, _c0=c0):
                s_ap = _bv[:, :, comp, :].rearrange("p c j -> p j c")
                d_ap = At[:, comp, :, _c0:_c0 + 2]
                return lambda en: (en.tensor_copy(d_ap, s_ap) if hasattr(en, "tensor_copy")
                                   else en.copy(d_ap, s_ap))
            E.emit(eng, mkcopy(0), waits=[mm] + war, inc=sem)
            e2 = E.emit(eng, mkcopy(1), waits=[], inc=sem)
            gd[("i1b", bi)] = [e2]
        return vs()

    def inv_l2(gsrc, ydst, tokAt, tok_g):
        for j0 in range(0, 128, 8):
            par = (j0 // 8) % 2
            bank = psf[6 + par]
            mm = None
            for jj in range(8):
                jb = j0 + jj
                E.emit("tensor",
                       (lambda t, _l=I2_sb[:, jb, 0, :], _r=At[:, 0, jb, :],
                        _o=bank[0:64, jj * 64:(jj + 1) * 64]:
                        t.matmul(_o, _l, _r, start=True, stop=False)),
                       waits=[t_i2] + gd.get(("i2b", par), []) + tokAt, inc="T")
                mm = E.emit("tensor",
                            (lambda t, _l=I2_sb[:, jb, 1, :], _r=At[:, 1, jb, :],
                             _o=bank[0:64, jj * 64:(jj + 1) * 64]:
                             t.matmul(_o, _l, _r, start=False, stop=True)),
                            inc="T")
            bv = bank[0:64, :].rearrange("p (j c) -> p j c", c=64)
            e = E.emit("vector",
                       (lambda v, _s=bv.rearrange("p j c -> p c j"),
                        _g=gsrc[0:64, :, j0:j0 + 8], _d=ydst[0:64, :, j0:j0 + 8]:
                        v.tensor_tensor(_d, _s, _g, ALU.mult)),
                       waits=[mm, tok_g], inc="V")
            gd[("i2b", par)] = [e]
        return [("V", E.cnt["V"])]

    def load_D1(dst_name, dst, kind, s, b, ch0, waits):
        toks = []
        if fused is None:
            srct = {"u": u_in, "f": f_in}[kind if kind != "g" else "u"]
            if kind == "f":
                ap = f_in[s, ch0:ch0 + G, :, :].rearrange("c t x -> t c x")
            else:
                ap = u_in[s, b, ch0:ch0 + G, :, :].rearrange("c t x -> t c x")
            return [E.dma("sync", dst[:, :, :], ap, waits=waits)], None
        if kind in ("u", "g"):
            tgt = uSt if kind == "u" else dst
            for qi in range(4):
                toks.append(E.dma("sync", tgt[16 * qi:16 * qi + 16, :, :],
                                  xpa2a_out[b * 4 + qi, s, :, ch0:ch0 + G]
                                  .rearrange("(a j) c -> a j c", j=128), waits=waits))
        else:
            for si in range(8):
                toks.append(E.dma("sync", uSt[8 * si:8 * si + 8, :, :],
                                  fa2a_out[si, s, :, ch0:ch0 + G]
                                  .rearrange("(a j) c -> a j c", j=128), waits=waits))
        if kind == "g":
            return toks, None
        rv = E.emit("gpsimd",
                    (lambda v, _d=dst[:, :, :], _s=uSt[:].rearrange("t a c -> t c a")[0:64]:
                     v.tensor_copy(_d, _s)), waits=toks, inc="P")
        return [rv], rv

    for g in range(4 if maxph >= 99 else 1):
        b, ch0 = g // 2, (g % 2) * 64
        if fused is None:
            tok_ys, _ = load_D1("y", yD1, "u", 0, b, ch0, gd.get("yD1w", []))
            tok_y = tok_ys[0]
        else:
            tok_ys, _ = load_D1("y", yD1, "u", 0, b, ch0,
                                gd.get("yD1w", []) + gd.get("uStw", [])
                                + gd.get("Zread", []) + [E.tensor_tok()])
            gd["uStw"] = [tok_ys[0]]
            tok_y = tok_ys[0]
        for s in range(3 if maxph >= 99 else 1):
            if fused is None:
                tok_fs, _ = load_D1("f", fD1, "f", s, b, ch0, gd.get("fD1w", []))
                tok_f = tok_fs[0]
                tok_gs, _ = load_D1("g", gD1, "g", s + 1, b, ch0, gd.get("gD1w", []))
                tok_g = tok_gs[0]
                gview = gD1
            else:
                tok_fs, _ = load_D1("f", fD1, "f", s, b, ch0,
                                    gd.get("fD1w", []) + gd.get("uStw", [])
                                    + gd.get("Zread", []) + [E.tensor_tok()])
                gd["uStw"] = [tok_fs[0]]
                tok_f = tok_fs[0]
                tok_gs, _ = load_D1("g", gD1s, "g", s + 1, b, ch0, gd.get("gD1w", []))
                tok_g = tok_gs[-1]
                gview = gD1s.rearrange("t a c -> t c a")
            warA = gd.get("Zread", []) + (gd.get("uStw", []) if fused is not None else [])
            tokA = fwd_l1(yD1, Ab, tok_y, warA)
            if maxph <= 1:
                dm = E.dma("sync", y_out[0, 0:64, :, :].rearrange("c a j -> a c j"),
                           yD1[:, :, :], waits=tokA)
                E.emit("sync", lambda e: None, waits=[dm])
                E.finalize(ctx)
                ctx.close()
                return nc
            tokF = fwd_l1(fD1, Fb, tok_f, warA)
            warX = gd.get("Atread", [])
            if maxph <= 2:
                dm = E.dma("sync", y_out[0, 0:64, :, :].rearrange("c a j -> a c j"),
                           yD1[:, :, :], waits=tokA + tokF)
                E.emit("sync", lambda e: None, waits=[dm])
                E.finalize(ctx)
                ctx.close()
                return nc
            tokX = fwd_l2(Ab, Fb, Xb, Gb, tokA + tokF, warX)
            if dbg and g == 0 and s == 0:
                da = E.dma("gpsimd", d_A[:, :, :, :], Ab, waits=tokA)
                db = E.dma("gpsimd", d_X[:, :, :, :], Xb, waits=tokX)
                dc = E.dma("gpsimd", d_G[:, :, :, :], Gb, waits=tokX)
                tokX = tokX + [da, db, dc]
            if maxph <= 3:
                dm = E.dma("sync", y_out[0, 0:64, :, :].rearrange("c a j -> a c j"),
                           yD1[:, :, :], waits=tokX)
                E.emit("sync", lambda e: None, waits=[dm])
                E.finalize(ctx)
                ctx.close()
                return nc
            tokZ = pointwise(tokX)
            if dbg and g == 0 and s == 0:
                dz = E.dma("gpsimd", d_Z[:, :, :, :].rearrange("p x c q -> p (x c q)"),
                           Ab_t[:].rearrange("p x q c -> p (x q c)"), waits=tokZ)
                tokZ = tokZ + [dz]
            if maxph <= 4:
                dm = E.dma("sync", y_out[0, 0:64, :, :].rearrange("c a j -> a c j"),
                           yD1[:, :, :], waits=tokZ)
                E.emit("sync", lambda e: None, waits=[dm])
                E.finalize(ctx)
                ctx.close()
                return nc
            tokT = inv_l1(tokZ)
            gd["Zread"] = [E.tensor_tok()]
            if dbg and g == 0 and s == 0:
                dt_ = E.dma("gpsimd", d_At[:, :, :, :], At, waits=tokT)
                tokT = tokT + [dt_]
            if maxph <= 5:
                dm = E.dma("sync", y_out[0, 0:64, :, :].rearrange("c a j -> a c j"),
                           yD1[:, :, :], waits=tokT)
                E.emit("sync", lambda e: None, waits=[dm])
                E.finalize(ctx)
                ctx.close()
                return nc
            tok_yv = inv_l2(gview if fused is not None else gD1, yD1, tokT, tok_g)
            gd["Atread"] = [E.tensor_tok()]
            gd["fD1w"] = [E.tensor_tok()]
            gd["gD1w"] = tok_yv
            tok_y = tok_yv[0]
        if fused is None:
            dm = E.dma("sync", y_out[b, ch0:ch0 + G, :, :].rearrange("c a j -> a c j"),
                       yD1[:, :, :], waits=[tok_y])
            out_dmas.append(dm)
            gd["yD1w"] = [dm]
        else:
            dms = []
            for tq in range(4):
                dms.append(E.dma("sync",
                                 ya2a_in[b * 4 + tq, ch0:ch0 + G, :]
                                 .rearrange("c (a j) -> a c j", j=128),
                                 yD1[16 * tq:16 * tq + 16, :, :], waits=[tok_y]))
            out_dmas += dms
            gd["yD1w"] = dms
        if maxph <= 6:
            break
    E.emit("sync", lambda e: None, waits=out_dmas)
    if fused is None:
        E.finalize(ctx)
        ctx.close()
    return nc


def build_fused(nc, dbg=False):
    io = {}

    def dram(name, shape, dt, kind):
        io[name] = nc.dram_tensor(name, shape, dt, kind=kind)

    dram("x_in0", [TOK, D], F32, "ExternalInput")
    dram("x_out", [TOK, D], F32, "ExternalOutput")
    dram("ident", [128, 128], F32, "ExternalInput")
    for n, s in (("Wc", [128, 8, 128]), ("bc", [1, 128]), ("tie", [1, 512]),
                 ("W1", [128, 8, 128, 32]), ("b1", [32, 128]), ("W2", [32, 128, 1024]),
                 ("b2", [128, 1024]),
                 ("Wqkv", [128, 192]), ("bqkv", [1, 192]), ("masks", [128, 4, 512]),
                 ("Wr", [128, 8, 1024]), ("br", [1, 1024]),
                 ("rW1", [128, 8, 16, 128]), ("rb1", [128, 16]), ("rW2", [128, 16, 4]),
                 ("rb2", [1, 4]), ("We1", [128, 4, 8, 32, 128]), ("be1", [128, 4, 32]),
                 ("We2", [128, 4, 32, 1024]), ("be2", [1, 4, 1024]),
                 ("Wp", [128, 8, 4096]), ("bp", [1, 4096]), ("Wf", [128, 8, 3072]),
                 ("bf", [1, 3072]), ("posT", [128, 8, 1024]),
                 ("L1c", [128, 130]), ("M2", [128, 65, 3, 128]), ("I1", [128, 2, 256]),
                 ("I2", [65, 128, 2, 64])):
        dram(n, s, F32, "ExternalInput")
    for n in ("xb1", "xb2", "xb3", "xb4"):
        dram(n, [TOK, D], F32, "ExternalOutput" if dbg else "Internal")
    dram("Rd", [128, 64, 1024], BF16, "Internal")
    dram("xag_in", [128, DCH, TOK], BF16, "Internal")
    dram("xag_out", [8 * 128, DCH, TOK], BF16, "Internal")
    dram("oa2a_in", [8, 128, 2, TOK], BF16, "Internal")
    dram("oa2a_out", [8, 128, 2, TOK], BF16, "Internal")
    dram("xpa2a_in", [8, 4, TOK, 128], BF16, "Internal")
    dram("xpa2a_out", [8, 4, TOK, 128], BF16, "Internal")
    dram("fa2a_in", [8, 3, 1024, 128], BF16, "Internal")
    dram("fa2a_out", [8, 3, 1024, 128], BF16, "Internal")
    dram("ya2a_in", [8, 128, TOK], BF16, "Internal")
    dram("ya2a_out", [8, 128, TOK], BF16, "Internal")
    if dbg:
        dram("d_y", [8, 128, TOK], BF16, "ExternalOutput")
        dram("d_xp", [8, 4, TOK, 128], BF16, "ExternalOutput")
        dram("d_f", [8, 3, 1024, 128], BF16, "ExternalOutput")
        dram("d_yin", [8, 128, TOK], BF16, "ExternalOutput")
        dram("d_xpi", [8, 4, TOK, 128], BF16, "ExternalOutput")
        dram("d_fi", [8, 3, 1024, 128], BF16, "ExternalOutput")

    ctx = ExitStack()
    E = Em(nc)
    ar = Arena(nc)
    P8 = [nc.alloc_psum_tensor(f"PSB{i}", [128, 512], F32) for i in range(8)]
    pid = nc.partition_id(mybir.ALL_ENGINES)
    fused = {"E": E, "ar": ar, "P8": P8, "io": io,
             "pb": pid // 4, "ph": pid % 4}
    G4 = [[0, 1, 2, 3], [4, 5, 6, 7]]
    G8 = [[0, 1, 2, 3, 4, 5, 6, 7]]
    ncc = [0]

    def coll(kind, in_name, out_name, groups):
        E.phase_barrier()
        ncc[0] += 1
        sem = f"cc{ncc[0]}"
        E.semnames.add(sem)

        def flat(t):
            n = len(t.shape)
            if n == 4:
                return t.ap().rearrange("a b c d -> (a b c) d")
            if n == 3:
                return t.ap().rearrange("a b c -> (a b) c")
            return t.ap()
        iap, oap = flat(io[in_name]), flat(io[out_name])

        def bld(e, _i=iap, _o=oap, _g=groups, _k=kind):
            from concourse.aot_env import allow_ici_all_to_all
            with allow_ici_all_to_all():
                return e.collective_compute(_k, ALU.bypass, replica_groups=_g,
                                            ins=[_i.opt()], outs=[_o.opt()])
        tok = E.emit("gpsimd", bld, waits=E.all_toks(), inc=sem, inc_by=1)
        if not hasattr(E, "_dmaprev"):
            E._dmaprev = {}
        E._dmaprev[sem] = tok
        E.phase_barrier()

    build_l0ffn(nc, fused=fused, xin_name="x_in0", xout_name="xb1")
    E.phase_barrier()
    build_attn_pre(nc, fused=fused, xin_name="xb1")
    coll("AllGather", "xag_in", "xag_out", G8)
    build_attn_core(nc, fused=fused)
    coll("AllToAll", "oa2a_in", "oa2a_out", G8)
    build_attn_proj(nc, fused=fused, xin_name="xb1", xout_name="xb2")
    E.phase_barrier()
    build_l0ffn(nc, fused=fused, xin_name="xb2", xout_name="xb3")
    E.phase_barrier()
    build_l1ffn(nc, fused=fused, xin_name="xb3", xout_name="xb4")
    E.phase_barrier()
    build_xp(nc, fused=fused, xin_name="xb4")
    if dbg:
        E.phase_barrier()
        E.dma("sync", io["d_xpi"][:, :, :, :], io["xpa2a_in"][:, :, :, :],
              waits=E.all_toks())
        E.dma("sync", io["d_fi"][:, :, :, :], io["fa2a_in"][:, :, :, :],
              waits=E.all_toks())
        E.phase_barrier()
    coll("AllToAll", "xpa2a_in", "xpa2a_out", G8)
    coll("AllToAll", "fa2a_in", "fa2a_out", G8)
    if dbg:
        E.dma("sync", io["d_xp"][:, :, :, :], io["xpa2a_out"][:, :, :, :],
              waits=E.all_toks())
        E.dma("sync", io["d_f"][:, :, :, :], io["fa2a_out"][:, :, :, :],
              waits=E.all_toks())
        E.phase_barrier()
    build_conv(nc, fused=fused)
    if dbg:
        E.phase_barrier()
        E.dma("sync", io["d_yin"][:, :, :], io["ya2a_in"][:, :, :],
              waits=E.all_toks())
        E.phase_barrier()
    coll("AllToAll", "ya2a_in", "ya2a_out", G8)
    if dbg:
        E.dma("sync", io["d_y"][:, :, :], io["ya2a_out"][:, :, :],
              waits=E.all_toks())
        E.phase_barrier()
    build_l1ffn(nc, fused=fused, with_y=True, xin_name="xb4", xout_name="x_out")
    E.finalize(ctx)
    ctx.close()
    return nc


kernel_fused_dbg = [False]


def kernel_fused(inp):
    x = np.ascontiguousarray(inp["L0_attn_x"].reshape(16384, 1024), dtype=np.float32)
    H = {}
    H.update(host_prep_l0ffn(inp))
    H.update(host_prep_l0attn(inp))
    H.update(host_prep_l1ffn(inp))
    H.update(host_prep_conv(inp))
    nc = bass.Bass("TRN2", target_bir_lowering=False, debug=False)
    build_fused(nc, dbg=kernel_fused_dbg[0])
    names = ("Wc", "bc", "tie", "W1", "b1", "W2", "b2", "Wqkv", "bqkv", "masks",
             "Wr", "br", "rW1", "rb1", "rW2", "rb2", "We1", "be1", "We2", "be2",
             "Wp", "bp", "Wf", "bf", "L1c", "M2", "I1", "I2")
    in_maps = []
    for i in range(NCORE):
        m = {n: H[n] for n in names}
        m["ident"] = _IDENT
        m["x_in0"] = np.ascontiguousarray(x[i * TOK:(i + 1) * TOK])
        m["posT"] = np.ascontiguousarray(H["posT"][:, :, 1024 * i:1024 * (i + 1)])
        in_maps.append(m)
    from concourse.aot_env import allow_ici_all_to_all
    with allow_ici_all_to_all():
        res = run_bass_kernel_spmd(nc, in_maps, core_ids=list(range(NCORE)))
    out = np.concatenate([np.asarray(r["x_out"]) for r in res.results], axis=0)
    if kernel_fused_dbg[0]:
        kernel_fused_dbg.append(res.results)
    return out.reshape(2, 8192, 1024)


def _shard_rows(a):
    return [np.ascontiguousarray(a[i * TOK:(i + 1) * TOK]) for i in range(NCORE)]


_IDENT = np.eye(128, dtype=np.float32)


def _run(builder, in_maps, outs, **bkw):
    nc = bass.Bass("TRN2", target_bir_lowering=False, debug=False)
    builder(nc, **bkw)
    res = run_bass_kernel_spmd(nc, in_maps, core_ids=list(range(NCORE)))
    return [[np.asarray(r[o]) for o in outs] for r in res.results]


def kernel(**inp):
    inp = {k: np.asarray(v) for k, v in inp.items()}
    return kernel_fused(inp).astype(np.float32)


def kernel_multilaunch(**inp):
    inp = {k: np.asarray(v) for k, v in inp.items()}
    x = np.ascontiguousarray(inp["L0_attn_x"].reshape(16384, 1024), dtype=np.float32)
    H0f = host_prep_l0ffn(inp)
    H0a = host_prep_l0attn(inp)
    H1f = host_prep_l1ffn(inp)
    Hc = host_prep_conv(inp)

    def shards(a):
        return [np.ascontiguousarray(a[i * TOK:(i + 1) * TOK]) for i in range(NCORE)]

    def l0ffn(xcur):
        sh = shards(xcur)
        maps = [{"x_in": sh[i], "ident": _IDENT,
                 **{k: H0f[k] for k in ("Wc", "bc", "tie", "W1", "b1", "W2", "b2")}}
                for i in range(NCORE)]
        out = _run(build_l0ffn, maps, ["x_out"])
        return np.concatenate([o[0] for o in out], axis=0)

    def l0attn(xcur):
        sh = shards(xcur)
        out = _run(build_attn_pre, [{"x_in": sh[i], "ident": _IDENT} for i in range(NCORE)],
                   ["xhT_out"])
        xhT = [o[0] for o in out]
        xhT_full = [np.concatenate([xhT[4 * b + i] for i in range(4)], axis=2) for b in range(2)]
        maps = []
        for c in range(NCORE):
            b, hq = c // 4, c % 4
            maps.append({"xhT": np.ascontiguousarray(xhT_full[b][:, 2 * hq:2 * hq + 2, :]),
                         "Wqkv": H0a["Wqkv"], "bqkv": H0a["bqkv"], "masks": H0a["masks"],
                         "ident": _IDENT})
        out = _run(build_attn_core, maps, ["oT_out"])
        oT = [o[0] for o in out]
        oT_full = [np.concatenate([oT[4 * b + i] for i in range(4)], axis=1) for b in range(2)]
        maps = []
        for c in range(NCORE):
            b, tq = c // 4, c % 4
            maps.append({"x_in": sh[c],
                         "oT": np.ascontiguousarray(oT_full[b][:, :, tq * 2048:(tq + 1) * 2048]),
                         "Wr": H0a["Wr"], "br": H0a["br"]})
        out = _run(build_attn_proj, maps, ["x_out"])
        return np.concatenate([o[0] for o in out], axis=0)

    def l1ffn(xcur, y=None):
        sh = shards(xcur)
        ysh = shards(y) if y is not None else None
        maps = []
        for i in range(NCORE):
            m = {"x_in": sh[i], "ident": _IDENT,
                 **{k: H1f[k] for k in ("rW1", "rb1", "rW2", "rb2", "We1", "be1", "We2", "be2")}}
            if ysh is not None:
                m["y_in"] = ysh[i]
            maps.append(m)
        out = _run(build_l1ffn, maps, ["x_out"], with_y=(y is not None))
        return np.concatenate([o[0] for o in out], axis=0)

    def l1attn(xcur):
        sh = shards(xcur)
        maps = [{"x_in": sh[i], "Wp": Hc["Wp"], "bp": Hc["bp"], "Wf": Hc["Wf"],
                 "bf": Hc["bf"],
                 "posT": np.ascontiguousarray(Hc["posT"][:, :, 1024 * i:1024 * (i + 1)]),
                 "ident": _IDENT} for i in range(NCORE)]
        out = _run(build_xp, maps, ["xp_out", "f_out"])
        xp_full = np.concatenate([o[0] for o in out], axis=0)
        f_full = np.concatenate([o[1] for o in out], axis=0)
        uu = xp_full.reshape(2, 64, 128, 4, 1024).transpose(3, 0, 4, 1, 2)
        ff = f_full.reshape(64, 128, 3, 1024).transpose(2, 3, 0, 1)
        maps = [{"u_in": np.ascontiguousarray(uu[:, :, 128 * i:128 * (i + 1)]),
                 "f_in": np.ascontiguousarray(ff[:, 128 * i:128 * (i + 1)]),
                 "L1c": Hc["L1c"], "M2": Hc["M2"], "I1": Hc["I1"], "I2": Hc["I2"]}
                for i in range(NCORE)]
        out = _run(build_conv, maps, ["y_out"])
        y = np.stack([o[0] for o in out], axis=0)
        return np.ascontiguousarray(
            y.transpose(1, 3, 4, 0, 2).reshape(16384, 1024))

    x = l0ffn(x)
    x = l0attn(x)
    x = l0ffn(x)
    x = l1ffn(x)
    y = l1attn(x)
    x = l1ffn(x, y=y)
    return x.reshape(2, 8192, 1024)


# version 05
# speedup vs baseline: 1.0311x; 1.0311x over previous
import numpy as np
import ml_dtypes
from contextlib import ExitStack
from collections import defaultdict
import concourse.bass as bass
import concourse.mybir as mybir
from concourse.bass_utils import run_bass_kernel_spmd

AF = mybir.ActivationFunctionType
ALU = mybir.AluOpType
AX = mybir.AxisListType
F32, BF16 = mybir.dt.float32, mybir.dt.bfloat16

NCORE = 8
TOK = 2048
NT = TOK // 128
D = 1024
DCH = D // 128

ENGINES = ("tensor", "vector", "scalar", "gpsimd", "sync")


class Em:

    def __init__(self, nc):
        self.nc = nc
        self.q = {e: [] for e in ENGINES}
        self.cnt = defaultdict(int)
        self.lastw = {e: defaultdict(int) for e in ENGINES}
        self.own = {"T": "tensor", "V": "vector", "S": "scalar", "P": "gpsimd", "Y": "sync"}
        self.semnames = set(self.own)
        self.handles = {}

    def emit(self, eng, build, waits=(), inc=None, inc_by=1, strict=False):
        real = []
        for w in waits:
            if not w:
                continue
            s, v = w
            if self.own.get(s) == eng and not strict:
                continue
            if self.lastw[eng][s] < v:
                self.lastw[eng][s] = v
                real.append((s, v))
        tok = None
        if inc:
            self.semnames.add(inc)
            self.cnt[inc] += inc_by
            tok = (inc, self.cnt[inc])

        def fn(e, _real=tuple(real), _b=build, _inc=inc, _by=inc_by, _em=self):
            for (s, v) in _real:
                e.wait_ge(_em.handles[s], v)
            try:
                ins = _b(e)
            except Exception:
                import sys
                c = getattr(_b, "__code__", None)
                cl = getattr(_b, "__closure__", None)
                print("EM-OP FAILED:", getattr(_b, "_site", None) or
                      (f"{c.co_filename}:{c.co_firstlineno}" if c else "?"),
                      file=sys.stderr)
                if cl:
                    for cell in cl:
                        try:
                            print("  cell:", repr(cell.cell_contents)[:300], file=sys.stderr)
                        except Exception:
                            pass
                raise
            if _inc is not None:
                assert ins is not None, "inc on op that built nothing"
                ins.then_inc(_em.handles[_inc], _by)

        self.q[eng].append(fn)
        return tok

    def tensor_tok(self):
        return ("T", self.cnt["T"])

    DMA_POOL = 20

    def dma(self, eng, out, in_, waits=(), sem=None, **kw):
        if sem is None:
            tag = {"gpsimd": "dg", "sync": "dy", "scalar": "ds"}[eng]
            if not hasattr(self, "_dmarr"):
                self._dmarr = defaultdict(int)
                self._dmaprev = {}
            idx = self._dmarr[tag] % self.DMA_POOL
            self._dmarr[tag] += 1
            sem = f"{tag}{idx}"
            prev = self._dmaprev.get(sem)
            waits = list(waits) + ([prev] if prev else [])
        self.semnames.add(sem)
        import traceback as _tb
        _site = _tb.extract_stack()[-2]
        lam = lambda e: e.dma_start(out=out, in_=in_, **kw)
        lam._site = f"{_site.filename}:{_site.lineno}"
        tok = self.emit(eng, lam, waits=waits, inc=sem, inc_by=16)
        self._dmaprev[sem] = tok
        return tok

    def all_toks(self):
        toks = [(s, self.cnt[s]) for s in ("T", "V", "S", "P", "Y") if self.cnt[s]]
        if hasattr(self, "_dmaprev"):
            toks += list(self._dmaprev.values())
        return toks

    def phase_barrier(self):
        toks = self.all_toks()
        for eng in ENGINES:
            self.emit(eng, lambda e: None, waits=toks, strict=True)

    def finalize(self, ctx):
        for s in sorted(self.semnames):
            self.handles[s] = ctx.enter_context(self.nc.semaphore(s))
        block = ctx.enter_context(self.nc.Block())
        for engname in ENGINES:
            ops = self.q[engname]
            dec = getattr(block, engname)
            dec((lambda _ops: lambda e: [fn(e) for fn in _ops])(ops))


class Arena:
    _uid = [0]

    def __init__(self, nc, base=17408, top=229300):
        self.nc, self.base, self.top = nc, base, top
        self.off = base
        self.peak = base

    def reset(self):
        self.off = self.base

    def alloc(self, name, shape, dt):
        esz = {F32: 4, BF16: 2}[dt]
        nbytes = int(np.prod(shape[1:])) * esz
        nbytes = (nbytes + 31) // 32 * 32
        Arena._uid[0] += 1
        h = self.nc.alloc_sbuf_tensor_at(f"{name}_{Arena._uid[0]}", list(shape), dt,
                                         offset=self.off)
        self.off += nbytes
        self.peak = max(self.peak, self.off)
        assert self.off <= self.top, f"arena overflow at {name}: {self.off}"
        return h


def _dyn(ap_obj, extra):
    return bass.AP(ap_obj.tensor, ap_obj.offset + extra, ap_obj.ap)


def _phase_io(nc, fused, decls):
    if fused is None:
        return {n: nc.dram_tensor(n, s, d, kind=k) for (n, s, d, k) in decls}
    io = fused["io"]
    return {n: io[n] for (n, s, d, k) in decls}


def host_prep_l1ffn(inp):
    H = {}
    C = lambda a: np.ascontiguousarray(a, dtype=np.float32)
    H["ident"] = np.eye(128, dtype=np.float32)
    H["rW1"] = C(inp["L1_ffn_rW1"].reshape(8, 128, 16, 128).transpose(1, 0, 2, 3))
    H["rb1"] = C(np.asarray(inp["L1_ffn_rb1"]).reshape(16, 128).T)
    H["rW2"] = C(inp["L1_ffn_rW2"].reshape(16, 128, 4).transpose(1, 0, 2))
    H["rb2"] = C(np.asarray(inp["L1_ffn_rb2"]).reshape(1, 4))
    H["We1"] = C(inp["L1_ffn_We1"].reshape(4, 8, 128, 32, 128).transpose(2, 0, 1, 3, 4))
    H["be1"] = C(np.asarray(inp["L1_ffn_be1"]).reshape(4, 32, 128).transpose(2, 0, 1))
    H["We2"] = C(inp["L1_ffn_We2"].reshape(4, 32, 128, 1024).transpose(2, 0, 1, 3))
    H["be2"] = C(np.asarray(inp["L1_ffn_be2"]).reshape(1, 4, 1024))
    return H


def host_prep_l0ffn(inp):
    H = {}
    C = lambda a: np.ascontiguousarray(a, dtype=np.float32)
    Wc = np.asarray(inp["L0_ffn_Wc"]).reshape(1024, 128)
    H["Wc"] = C(Wc.reshape(8, 128, 128).transpose(1, 0, 2))
    H["bc"] = C(np.asarray(inp["L0_ffn_bc"]).reshape(1, 128))
    H["tie"] = np.tile(np.linspace(0.0, 1e-06, 32, dtype=np.float32), 16).reshape(1, 512).astype(ml_dtypes.bfloat16)
    W1 = np.asarray(inp["L0_ffn_W1"]).reshape(128, 1024, 32)
    H["W1"] = C(W1.transpose(1, 0, 2).reshape(8, 128, 128, 32).transpose(1, 0, 2, 3))
    H["b1"] = C(np.asarray(inp["L0_ffn_b1"]).reshape(128, 32).T)
    W2 = np.asarray(inp["L0_ffn_W2"]).reshape(128, 1024, 32)
    H["W2"] = C(W2.transpose(2, 0, 1))
    H["b2"] = C(np.asarray(inp["L0_ffn_b2"]).reshape(128, 1024))
    return H


def emit_ln(E, xtile, xhat_bf, st, ready=(), war=()):
    s1, s2, mean, var, rstd, t0, ex2, tln = (st[:, 0:1], st[:, 1:2], st[:, 2:3],
                                             st[:, 3:4], st[:, 4:5], st[:, 5:6],
                                             st[:, 6:7], st[:, 7:8])
    sq = E.ln_sq
    ta = E.emit("vector", lambda v: v.reduce_sum(s1, xtile, AX.X), waits=list(ready), inc="V")
    E.emit("vector", lambda v: v.tensor_tensor(sq, xtile, xtile, ALU.mult), inc="V")
    tb = E.emit("vector", lambda v: v.reduce_sum(s2, sq, AX.X), inc="V")
    tc = E.emit("vector", lambda v: v.tensor_scalar_mul(mean, s1, 1.0 / 1024.0),
                waits=[ta], inc="V", strict=True)
    td = E.emit("vector", lambda v: v.tensor_scalar_mul(ex2, s2, 1.0 / 1024.0),
                waits=[tb], inc="V", strict=True)
    te = E.emit("vector", lambda v: v.tensor_tensor(t0, mean, mean, ALU.mult),
                waits=[tc], inc="V", strict=True)
    t6 = E.emit("vector", lambda v: v.tensor_tensor(var, ex2, t0, ALU.subtract),
                waits=[td, te], inc="V", strict=True)
    cst = E.lnconsts
    E.emit("scalar", lambda s: s.activation(out=tln, in_=var, func=AF.Ln, bias=cst[:, 0:1]),
           waits=[t6], inc="S")
    t8 = E.emit("scalar", lambda s: s.activation(out=rstd, in_=tln, func=AF.Exp,
                                                 scale=cst[:, 1:2]), inc="S")
    t9 = E.emit("vector", lambda v: v.tensor_scalar(xhat_bf, xtile, mean, rstd,
                                                    ALU.subtract, ALU.mult),
                waits=[t8, tc] + list(war), inc="V", strict=True)
    return t9


def build_l1ffn(nc, dbg=False, with_y=False, fused=None, use_pc=False,
                xin_name="x_in", xout_name="x_out"):
    dio = _phase_io(nc, fused, [
        (xin_name, [TOK, D], F32, "ExternalInput"),
        (xout_name, [TOK, D], F32, "ExternalOutput"),
        ("rW1", [128, 8, 16, 128], F32, "ExternalInput"),
        ("rb1", [128, 16], F32, "ExternalInput"),
        ("rW2", [128, 16, 4], F32, "ExternalInput"),
        ("rb2", [1, 4], F32, "ExternalInput"),
        ("We1", [128, 4, 8, 32, 128], F32, "ExternalInput"),
        ("be1", [128, 4, 32], F32, "ExternalInput"),
        ("We2", [128, 4, 32, 1024], F32, "ExternalInput"),
        ("be2", [1, 4, 1024], F32, "ExternalInput"),
        ("ident", [128, 128], F32, "ExternalInput"),
    ] + ([("y_in", [TOK, D], BF16, "ExternalInput")] if (with_y and fused is None) else [])
      + ([("ya2a_out", [8, 128, TOK], BF16, "Internal")] if (with_y and fused is not None) else []))
    x_in, x_out = dio[xin_name], dio[xout_name]
    w_rW1, w_rb1, w_rW2, w_rb2 = dio["rW1"], dio["rb1"], dio["rW2"], dio["rb2"]
    w_We1, w_be1, w_We2, w_be2 = dio["We1"], dio["be1"], dio["We2"], dio["be2"]
    w_ident = dio["ident"]
    if use_pc and fused is not None:
        w_We1, w_We2 = fused["io"]["We1bf"], fused["io"]["We2bf"]
    we1_eng = "sync" if (use_pc and fused is not None) else "gpsimd"
    we2_eng = "scalar" if (use_pc and fused is not None) else "gpsimd"
    if with_y:
        if fused is None:
            y_in = dio["y_in"]
        else:
            ya2a_out = dio["ya2a_out"]
    if dbg:
        d_xhatT = nc.dram_tensor("d_xhatT", [128, DCH, TOK], F32, kind="ExternalOutput")
        d_cw = nc.dram_tensor("d_cw", [128, NT, 4], F32, kind="ExternalOutput")
        d_hT = nc.dram_tensor("d_hT", [128, 32, 512], F32, kind="ExternalOutput")
        d_rhT = nc.dram_tensor("d_rhT", [128, 32, 512], F32, kind="ExternalOutput")
        d_ln = nc.dram_tensor("d_ln", [128, NT, 8], F32, kind="ExternalOutput")

    if fused is None:
        ctx = ExitStack()
        sb = lambda name, shape, dt: ctx.enter_context(nc.sbuf_tensor(name, shape, dt))
    else:
        fused["ar"].reset()
        sb = fused["ar"].alloc

    xt = sb("xt", [128, D], F32)
    xhat = sb("xhat", [128, D], BF16)
    xhatT = sb("xhatT", [128, DCH, TOK], BF16)
    hT = sb("hT", [128, 32, 512], BF16)
    We1_sb = sb("We1_sb", [128, 8, 32, 128], BF16)
    We2_sb = sb("We2_sb", [128, 32, 1024], BF16)
    rW2_sb = sb("rW2_sb", [128, 16, 4], BF16)
    rb1_sb = sb("rb1_sb", [128, 16], F32)
    rb2_sb = sb("rb2_sb", [1, 4], BF16)
    be1_sb = sb("be1_sb", [128, 4, 32], F32)
    be2_sb = sb("be2_sb", [1, 1024], BF16)
    ident_sb = sb("ident_sb", [128, 128], BF16)
    ones_sb = sb("ones_sb", [1, 128], BF16)
    cw = sb("cw", [128, NT, 4], F32)
    lnst = sb("lnst", [128, NT, 8], F32)
    r4 = sb("r4", [128, 12], F32)
    r44 = sb("r44", [128, 16], F32)
    stage = sb("stage", [128, 2, 512], F32)

    if fused is None:
        ps_tr = [ctx.enter_context(nc.psum_tensor(f"ps_tr{i}", [128, 128], BF16))[:]
                 for i in range(2)]
        ps_h = [ctx.enter_context(nc.psum_tensor(f"ps_h{i}", [128, 512], F32))[:]
                for i in range(2)]
        ps_y = [ctx.enter_context(nc.psum_tensor(f"ps_y{i}", [128, 512], F32))[:]
                for i in range(2)]
        ps_l = ctx.enter_context(nc.psum_tensor("ps_l", [128, 512], F32))[:]
        ps_l4 = ctx.enter_context(nc.psum_tensor("ps_l4", [128, 4], F32))[:]
        E = Em(nc)
    else:
        P8 = fused["P8"]
        ps_tr = [P8[6][:].bitcast(BF16)[:, 0:128], P8[7][:].bitcast(BF16)[:, 0:128]]
        ps_h = [P8[0][:], P8[1][:]]
        ps_y = [P8[2][:], P8[3][:]]
        ps_l = P8[4][:]
        ps_l4 = P8[5][:, 0:4]
        E = fused["E"]

    t_cp = None if with_y else E.dma("scalar", x_out[:, :], x_in[:, :])
    t_id = E.dma("gpsimd", ident_sb[:], w_ident[:, :])
    t_rb1 = E.dma("gpsimd", rb1_sb[:], w_rb1[:, :])
    t_rW2 = E.dma("gpsimd", rW2_sb[:], w_rW2[:, :, :])
    t_rb2 = E.dma("gpsimd", rb2_sb[:], w_rb2[:, :])
    t_be1 = E.dma("gpsimd", be1_sb[:], w_be1[:, :, :])
    t_ones = E.emit("vector", lambda v: v.memset(ones_sb[:], 1.0), inc="V")
    lncst = sb("lncst", [128, 2], F32)
    E.emit("vector", lambda v: v.memset(lncst[:, 0:1], 1e-5), inc="V")
    E.emit("vector", lambda v: v.memset(lncst[:, 1:2], -0.5), inc="V")
    E.lnconsts = lncst
    E.ln_sq = stage[:].rearrange("p a b -> p (a b)")

    ln_done = {}
    tr_free = {}
    xload = {}
    basew = {}
    for ti in range(NT):
        xload[ti] = E.dma("sync", xt[:], x_in[ti * 128:(ti + 1) * 128, :],
                          waits=[ln_done.get(ti - 1), basew.get(ti - 1)])
        rdy = [xload[ti]]
        if with_y:
            yt = hT[:].rearrange("p a b -> p (a b)")[:, 0:D]
            if fused is None:
                yl = E.dma("scalar", yt[:], y_in[ti * 128:(ti + 1) * 128, :],
                           waits=[ln_done.get(ti - 1)])
            else:
                ycol = hT[:].rearrange("p a b -> p (a b)")[:, D:D + 256]
                for ch in range(DCH):
                    sl = ch % 2
                    lc = E.dma("scalar", ycol[:, sl * 128:(sl + 1) * 128],
                               ya2a_out[ch, :, ti * 128:(ti + 1) * 128],
                               waits=[tr_free.get(("y", ti, ch - 2)),
                                      ln_done.get(ti - 1)])
                    tt = E.emit("tensor",
                                (lambda t, _s=ycol[:, sl * 128:(sl + 1) * 128], _p=ps_tr[sl]:
                                 t.transpose(_p, _s, ident_sb[:])),
                                waits=[lc, t_id, tr_free.get((ti - 1, (sl) + 6)),
                                       tr_free.get(("y", ti, ch - 2))], inc="T")
                    yev = E.emit("vector",
                                 (lambda v, _p=ps_tr[sl], _d=yt[:, ch * 128:(ch + 1) * 128]:
                                  v.tensor_copy(_d, _p)),
                                 waits=[tt], inc="V")
                    tr_free[("y", ti, ch)] = yev
                tr_free[(ti - 1, 6)] = tr_free[("y", ti, DCH - 2)]
                tr_free[(ti - 1, 7)] = tr_free[("y", ti, DCH - 1)]
                yl = tr_free[("y", ti, DCH - 1)]
            ad = E.emit("vector", lambda v: v.tensor_tensor(xt[:], xt[:], yt[:], ALU.add),
                        waits=[xload[ti], yl], inc="V")
            basew[ti] = E.dma("sync", x_out[ti * 128:(ti + 1) * 128, :], xt[:], waits=[ad])
            rdy = [ad, basew[ti]]
        t9 = emit_ln(E, xt[:], xhat[:], lnst[:, ti, :], ready=rdy,
                     war=[tr_free.get((ti - 1, DCH - 1))])
        ln_done[ti] = t9
        for ch in range(DCH):
            pst = ps_tr[ch % 2]
            tt = E.emit("tensor",
                        (lambda t, _s=xhat[:, ch * 128:(ch + 1) * 128], _p=pst:
                         t.transpose(_p, _s, ident_sb[:])),
                        waits=[t9, t_id, tr_free.get((ti - 1, (ch % 2) + 6)),
                               tr_free.get((ti, ch - 2))], inc="T")
            ev = E.emit("vector",
                        (lambda v, _p=pst, _d=xhatT[:, ch, ti * 128:(ti + 1) * 128]:
                         v.tensor_copy(_d, _p)),
                        waits=[tt], inc="V")
            tr_free[(ti, ch)] = ev
    xT_all = ev
    if dbg:
        E.dma("gpsimd", d_xhatT[:, :, :], xhatT[:, :, :], waits=[xT_all])
        E.dma("gpsimd", d_ln[:, :, :], lnst[:, :, :], waits=[xT_all])

    NBLK = TOK // 512
    cw_tok = [None] * NT
    rhT = hT[:].rearrange("p a b -> p (a b)")
    ps_l_free = None
    ps_l4_free = None
    rw1_free = {}
    rhT_war = ("V", E.cnt["V"]) if with_y else None
    for blk in range(NBLK):
        t0c = blk * 512
        rh_last = None
        for hch in range(16):
            slot = hch % 2
            ld = E.dma("gpsimd", We1_sb[:, :, slot, :], w_rW1[:, :, hch, :],
                       waits=[rw1_free.get(slot)])
            tt = None
            for dch in range(DCH):
                tt = E.emit("tensor",
                            (lambda t, _l=We1_sb[:, dch, slot, :], _r=xhatT[:, dch, t0c:t0c + 512],
                             _st=(dch == 0), _sp=(dch == DCH - 1):
                             t.matmul(ps_l, _l, _r, start=_st, stop=_sp)),
                            waits=[ld, xT_all, ps_l_free], inc="T")
            rw1_free[slot] = tt
            ev = E.emit("scalar",
                        (lambda s, _o=rhT[:, hch * 512:(hch + 1) * 512], _b=rb1_sb[:, hch:hch + 1]:
                         s.activation(out=_o, in_=ps_l, func=AF.Gelu, bias=_b)),
                        waits=[tt, t_rb1, rhT_war, getattr(E, "_dbg_rhT_dump", None)], inc="S")
            ps_l_free = ev
            rh_last = ev
        for sub in range(4):
            ti = blk * 4 + sub
            for hch in range(16):
                E.emit("tensor",
                       (lambda t, _l=rhT[:, hch * 512 + sub * 128: hch * 512 + sub * 128 + 128],
                        _r=rW2_sb[:, hch, :], _st=(hch == 0):
                        t.matmul(ps_l4, _l, _r, start=_st, stop=False)),
                       waits=[rh_last, t_rW2, ps_l4_free], inc="T")
            tb = E.emit("tensor",
                        lambda t: t.matmul(ps_l4, ones_sb[0:1, :], rb2_sb[0:1, :],
                                           start=False, stop=True),
                        waits=[t_rb2, t_ones], inc="T")
            p4, m1, s4, rcp = r4[:, 0:4], r4[:, 4:5], r4[:, 5:6], r4[:, 6:7]
            cnt = r4[:, 8:12]
            cwt = cw[:, ti, :]
            c1 = E.emit("vector", lambda v: v.reduce_max(m1, ps_l4, AX.X), waits=[tb], inc="V")
            c2 = E.emit("vector", lambda v: v.tensor_scalar_sub(p4, ps_l4, m1),
                        waits=[c1], inc="V", strict=True)
            ps_l4_free = c2
            e3 = E.emit("vector", lambda v: v.tensor_copy(r44[:, 0:4], p4),
                        waits=[c2], inc="V", strict=True)
            e3b = E.emit("scalar", lambda s: s.activation(out=p4, in_=r44[:, 0:4], func=AF.Exp),
                         waits=[e3], inc="S")
            c4 = E.emit("vector", lambda v: v.reduce_sum(s4, p4, AX.X), waits=[e3b], inc="V")
            c5 = E.emit("vector", lambda v: v.reciprocal(rcp, s4),
                        waits=[c4], inc="V", strict=True)
            c6 = E.emit("vector", lambda v: v.tensor_scalar_mul(p4, p4, rcp),
                        waits=[c5], inc="V", strict=True)
            pa = p4.unsqueeze(1).broadcast_to([128, 4, 4])
            pb = p4.unsqueeze(2).broadcast_to([128, 4, 4])
            c7 = E.emit("vector", lambda v: v.tensor_tensor(
                r44[:].rearrange("p (a b) -> p a b", b=4), pa, pb, ALU.is_gt),
                waits=[c6], inc="V", strict=True)
            c8 = E.emit("vector", lambda v: v.reduce_sum(
                cnt, r44[:].rearrange("p (a b) -> p a b", b=4), AX.X),
                waits=[c7], inc="V", strict=True)
            c9 = E.emit("vector", lambda v: v.tensor_single_scalar(cnt, cnt, 1.5, ALU.is_lt),
                        waits=[c8], inc="V", strict=True)
            c10 = E.emit("vector", (lambda v, _c=cwt: v.tensor_tensor(_c, p4, cnt, ALU.mult)),
                         waits=[c9], inc="V", strict=True)
            cw_tok[ti] = E.emit("vector", (lambda v, _c=cwt: v.tensor_scalar_mul(_c, _c, 0.5)),
                                waits=[c10], inc="V", strict=True)
        rhT_war = E.tensor_tok()
        if dbg and blk == 0:
            dmp = E.dma("gpsimd", d_rhT[:, :, :], hT[:, :, :], waits=[rh_last])
            E._dbg_rhT_dump = dmp

    acc_dma = [None] * NT
    stage_free = {}
    ps_h_free = {}
    ps_y_free = {}
    nstage = 0
    router_T = E.tensor_tok()
    if dbg:
        E.dma("gpsimd", d_cw[:, :, :], cw[:, :, :], waits=[cw_tok[NT - 1]])
    we1_free = router_T
    we2_free = None
    be2_free = None
    hT_war = rhT_war
    for e in range(4):
        l1 = E.dma(we1_eng, We1_sb[:], w_We1[:, e, :, :, :], waits=[we1_free])
        l2 = E.dma(we2_eng, We2_sb[:], w_We2[:, e, :, :], waits=[we2_free])
        t_be2 = E.dma("gpsimd", be2_sb[:], w_be2[0:1, e, :], waits=[be2_free])
        for blk in range(NBLK):
            t0c = blk * 512
            h_last = None
            for hch in range(32):
                bank = ps_h[hch % 2]
                tt = None
                for dch in range(DCH):
                    tt = E.emit("tensor",
                                (lambda t, _l=We1_sb[:, dch, hch, :], _r=xhatT[:, dch, t0c:t0c + 512],
                                 _o=bank, _st=(dch == 0), _sp=(dch == DCH - 1):
                                 t.matmul(_o, _l, _r, start=_st, stop=_sp)),
                                waits=[l1, xT_all, ps_h_free.get(hch % 2)], inc="T")
                ev = E.emit("scalar",
                            (lambda s, _o=hT[:, hch, :], _b=be1_sb[:, e, hch:hch + 1], _p=bank:
                             s.activation(out=_o, in_=_p, func=AF.Gelu, bias=_b)),
                            waits=[tt, t_be1, hT_war, getattr(E, "_dbg_hT_dump", None)], inc="S")
                ps_h_free[hch % 2] = ev
                h_last = ev
            if dbg and e == 0 and blk == 0:
                dmp = E.dma("gpsimd", d_hT[:, :, :], hT[:, :, :], waits=[h_last])
                E._dbg_hT_dump = dmp
            if blk == NBLK - 1:
                we1_free = E.tensor_tok()
            for sub in range(4):
                ti = blk * 4 + sub
                for half in range(2):
                    bank = ps_y[half]
                    for hch in range(32):
                        E.emit("tensor",
                               (lambda t, _l=hT[:, hch, sub * 128:(sub + 1) * 128],
                                _r=We2_sb[:, hch, half * 512:(half + 1) * 512],
                                _o=bank, _st=(hch == 0):
                                t.matmul(_o, _l, _r, start=_st, stop=False)),
                               waits=[l2, h_last, ps_y_free.get(half)], inc="T")
                    tb = E.emit("tensor",
                                (lambda t, _o=bank, _r=be2_sb[0:1, half * 512:(half + 1) * 512]:
                                 t.matmul(_o, ones_sb[0:1, :], _r, start=False, stop=True)),
                                waits=[t_be2], inc="T")
                    sslot = nstage % 2
                    nstage += 1
                    evt = E.emit("vector",
                                 (lambda v, _p=bank, _s=stage[:, sslot, :], _c=cw[:, ti, e:e + 1]:
                                  v.tensor_scalar_mul(_s, _p, _c)),
                                 waits=[tb, cw_tok[ti], stage_free.get(sslot)], inc="V")
                    ps_y_free[half] = evt
                    dmat = E.dma("gpsimd",
                                 x_out[ti * 128:(ti + 1) * 128, half * 512:(half + 1) * 512],
                                 stage[:, sslot, :],
                                 waits=[evt, t_cp if not with_y else basew[ti],
                                        acc_dma[ti]], accum_op=ALU.add)
                    acc_dma[ti] = dmat
                    stage_free[sslot] = dmat
            hT_war = E.tensor_tok()
        we2_free = E.tensor_tok()
        be2_free = we2_free

    E.emit("sync", lambda e: None, waits=[t for t in acc_dma] + ([t_cp] if t_cp else []))
    if fused is None:
        E.finalize(ctx)
        ctx.close()
    return nc


def build_l0ffn(nc, dbg=False, fused=None, use_pc=False, xin_name="x_in", xout_name="x_out"):
    dio = _phase_io(nc, fused, [
        (xin_name, [TOK, D], F32, "ExternalInput"),
        (xout_name, [TOK, D], F32, "ExternalOutput"),
        ("Wc", [128, 8, 128], F32, "ExternalInput"),
        ("bc", [1, 128], F32, "ExternalInput"),
        ("tie", [1, 512], F32, "ExternalInput"),
        ("W1", [128, 8, 128, 32], F32, "ExternalInput"),
        ("b1", [32, 128], F32, "ExternalInput"),
        ("W2", [32, 128, 1024], F32, "ExternalInput"),
        ("b2", [128, 1024], F32, "ExternalInput"),
        ("ident", [128, 128], F32, "ExternalInput"),
        ("Rd", [128, 64, 1024], BF16, "Internal"),
    ])
    x_in, x_out = dio[xin_name], dio[xout_name]
    w_Wc, w_bc, w_tie, w_ident = dio["Wc"], dio["bc"], dio["tie"], dio["ident"]
    w_W1, w_b1, w_W2, w_b2, Rd = dio["W1"], dio["b1"], dio["W2"], dio["b2"], dio["Rd"]
    if use_pc and fused is not None:
        w_W1, w_W2 = fused["io"]["W1bf"], fused["io"]["W2bf"]
    w1_eng = "sync" if (use_pc and fused is not None) else "gpsimd"
    w2_eng = "scalar" if (use_pc and fused is not None) else "gpsimd"
    if dbg:
        d_lg = nc.dram_tensor("d_lg", [128, TOK], F32, kind="ExternalOutput")
        d_pt = nc.dram_tensor("d_pt", [128, TOK], F32, kind="ExternalOutput")
        d_xt = nc.dram_tensor("d_xt", [128, 8, 64, 64], F32, kind="ExternalOutput")

    if fused is None:
        ctx = ExitStack()
        sb = lambda name, shape, dt: ctx.enter_context(nc.sbuf_tensor(name, shape, dt))
    else:
        fused["ar"].reset()
        sb = fused["ar"].alloc

    xt = sb("xt", [128, D], F32)
    xhat = sb("xhat", [128, NT, D], BF16)
    xhatT = sb("xhatT", [128, DCH, TOK], BF16)
    XT = sb("XT", [128, 8, 64, 64], BF16)
    W1_sb = sb("W1_sb", [128, 8, 64, 32], BF16)
    W2_sb = sb("W2_sb", [32, 2, 1024], BF16)
    lg_sb = sb("lg_sb", [128, 512], F32)
    pt_sb = sb("pt_sb", [128, TOK], BF16)
    pbig = sb("pbig", [128, NT, 512], BF16)
    gmax = sb("gmax", [128, 64], F32)
    inner = sb("inner", [32, 2, 64], BF16)
    Wc_sb = sb("Wc_sb", [128, 8, 128], BF16)
    bc_sb = sb("bc_sb", [1, 128], BF16)
    tie_sb = sb("tie_sb", [1, 512], BF16)
    b1_sb = sb("b1_sb", [32, 128], F32)
    b2_sb = sb("b2_sb", [128, 1024], BF16)
    ident_sb = sb("ident_sb", [128, 128], BF16)
    ones_sb = sb("ones_sb", [1, 512], BF16)
    lnst = sb("lnst", [128, NT, 8], F32)
    lncst = sb("lncst", [128, 2], F32)
    rstage = sb("rstage", [64, 2, 512], BF16)
    rsb = sb("rsb", [128, 1024], BF16)

    if fused is None:
        psf = [ctx.enter_context(nc.psum_tensor(f"psf{i}", [128, 512], F32))[:]
               for i in range(6)]
        ps_bf = ctx.enter_context(nc.psum_tensor("ps_bf", [128, 512], BF16))[:]
        ps_tr = ctx.enter_context(nc.psum_tensor("ps_tr", [128, 128], BF16))[:]
        E = Em(nc)
    else:
        P8 = fused["P8"]
        psf = [P8[i][:] for i in range(6)]
        ps_bf = P8[6][:].bitcast(BF16)[:, 0:512]
        ps_tr = P8[7][:].bitcast(BF16)[:, 0:128]
        E = fused["E"]
    E.lnconsts = lncst
    E.ln_sq = XT[:].rearrange("p a b c -> p (a b c)")[:, 0:2048].bitcast(F32)

    t_id = E.dma("gpsimd", ident_sb[:], w_ident[:, :])
    t_wc = E.dma("gpsimd", Wc_sb[:], w_Wc[:, :, :])
    t_bc = E.dma("gpsimd", bc_sb[:], w_bc[:, :])
    t_tie = E.dma("gpsimd", tie_sb[:], w_tie[:, :])
    t_b1 = E.dma("gpsimd", b1_sb[:], w_b1[:, :])
    t_b2 = E.dma("gpsimd", b2_sb[:], w_b2[:, :])
    E.emit("vector", lambda v: v.memset(ones_sb[:], 1.0), inc="V")
    E.emit("vector", lambda v: v.memset(lncst[:, 0:1], 1e-5), inc="V")
    E.emit("vector", lambda v: v.memset(lncst[:, 1:2], -0.5), inc="V")

    ln_done = {}
    tr_free = {}
    for ti in range(NT):
        xl = E.dma("sync", xt[:], x_in[ti * 128:(ti + 1) * 128, :], waits=[ln_done.get(ti - 1)])
        t9 = emit_ln(E, xt[:], xhat[:, ti, :], lnst[:, ti, :], ready=[xl], war=[])
        ln_done[ti] = t9
        for ch in range(DCH):
            tt = E.emit("tensor",
                        (lambda t, _s=xhat[:, ti, ch * 128:(ch + 1) * 128]:
                         t.transpose(ps_tr, _s, ident_sb[:])),
                        waits=[t9, t_id, tr_free.get((ti, ch - 1)), tr_free.get((ti - 1, DCH - 1))],
                        inc="T")
            ev = E.emit("vector",
                        (lambda v, _d=xhatT[:, ch, ti * 128:(ti + 1) * 128]:
                         v.tensor_copy(_d, ps_tr)),
                        waits=[tt], inc="V")
            tr_free[(ti, ch)] = ev
    xT_all = ev

    NBLK = TOK // 512
    lg_evac = {}
    route_done = []
    for blk in range(NBLK):
        t0c = blk * 512
        bank = psf[blk % 2]
        for dch in range(DCH):
            E.emit("tensor",
                   (lambda t, _l=Wc_sb[:, dch, :], _r=xhatT[:, dch, t0c:t0c + 512],
                    _o=bank, _st=(dch == 0):
                    t.matmul(_o, _l, _r, start=_st, stop=False)),
                   waits=[t_wc, xT_all, lg_evac.get(blk % 2)], inc="T")
        E.emit("tensor", (lambda t, _o=bank: t.matmul(_o, bc_sb[0:1, :], ones_sb[0:1, 0:512],
                                                      start=False, stop=False)),
               waits=[t_bc], inc="T")
        tt = E.emit("tensor", (lambda t, _o=bank: t.matmul(_o, ones_sb[0:1, 0:128], tie_sb[0:1, :],
                                                           start=False, stop=True)),
                    waits=[t_tie], inc="T")
        ev = E.emit("vector", (lambda v, _o=bank, _d=lg_sb[:, :]:
                               v.tensor_copy(_d, _o)), waits=[tt, lg_evac.get("route")],
                    inc="V")
        lg_evac[blk % 2] = ev
        lgv = lg_sb[:, :].rearrange("p (g w) -> p g w", w=32)
        g16 = gmax[:, blk * 16:(blk + 1) * 16]
        r1 = E.emit("vector", (lambda v, _l=lgv, _g=g16: v.tensor_reduce(_g, _l, AX.X, ALU.max)),
                    waits=[ev], inc="V", strict=True)
        gb = g16.unsqueeze(2).broadcast_to([128, 16, 32])
        r2 = E.emit("vector", (lambda v, _l=lgv, _g=gb,
                               _d=pt_sb[:, t0c:t0c + 512].rearrange("p (g w) -> p g w", w=32):
                               v.tensor_tensor(_d, _l, _g, ALU.is_equal)),
                    waits=[r1], inc="V", strict=True)
        lg_evac["route"] = r2
        if dbg:
            lg_evac["route"] = E.dma("gpsimd", d_lg[:, t0c:t0c + 512], lg_sb[:, :], waits=[r2])
        route_done.append(r2)
    logits_V = ("V", E.cnt["V"])

    pb_toks = []
    for ti in range(NT):
        mz = E.emit("vector", lambda v: v.memset(ps_bf.bitcast(F32), 0.0),
                    waits=[route_done[ti // 4], pb_toks[-1] if pb_toks else None],
                    inc="V", strict=True)
        trs = []
        for j in range(4):
            tt = E.emit("tensor",
                        (lambda t, _i=pt_sb[:, ti * 128 + 32 * j: ti * 128 + 32 * j + 32],
                         _o=ps_bf[32 * j:32 * j + 32, j * 128:(j + 1) * 128], _j=j:
                         t.transpose(_o, _i, ident_sb[:], tile_position=(0, 32 * _j))),
                        waits=[mz, route_done[-1] if ti == 0 else None], inc="T")
            trs.append(tt)
        ev = E.emit("vector", (lambda v, _d=pbig[:, ti, :]: v.tensor_copy(_d, ps_bf)),
                    waits=trs, inc="V")
        pb_toks.append(ev)
    pbig_all = pb_toks[-1]

    w1_free = None
    gather_ps_free = {}
    prev_phase_acts = None
    w1_bank_free = {}
    w2_bank_free = {}
    w2_slot_free = {}
    inner_slot_free = {}
    rstage_free = {}
    rd_done = []
    for eb in range(2):
        l_w1 = E.dma(w1_eng, W1_sb[:], w_W1[:, :, eb * 64:(eb + 1) * 64, :], waits=[w1_free])
        gather_evacs = []
        for ti in range(NT):
            for ch in range(DCH):
                bi = 2 + (ch % 2)
                bank = psf[bi]
                rhs = pbig[:, ti, :].rearrange("p (g e) -> p g e", e=128)[:, :, eb * 64:(eb + 1) * 64]
                tt = E.emit("tensor",
                            (lambda t, _l=xhat[:, ti, ch * 128:(ch + 1) * 128], _r=rhs,
                             _o=bank[:, 0:256]:
                             t.matmul(_o, _l, _r, start=True, stop=True)),
                            waits=[pbig_all, gather_ps_free.get(bi), prev_phase_acts], inc="T")
                dst = XT[:, ch, :, :].rearrange("p e g -> p g e")[:, ti * 4:ti * 4 + 4, :]
                eng = "vector" if (ti + ch) % 2 == 0 else "scalar"
                xw = w1_free if eb == 1 else None
                if eng == "vector":
                    ev = E.emit("vector",
                                (lambda v, _s=bank[:, 0:256].rearrange("p (g e) -> p g e", e=64),
                                 _d=dst: v.tensor_copy(_d, _s)), waits=[tt, xw], inc="V")
                else:
                    ev = E.emit("scalar",
                                (lambda s, _s=bank[:, 0:256].rearrange("p (g e) -> p g e", e=64),
                                 _d=dst: s.copy(_d, _s)), waits=[tt, xw], inc="S")
                gather_ps_free[bi] = ev
                gather_evacs.append(ev)
        gV, gS = ("V", E.cnt["V"]), ("S", E.cnt["S"])
        rd_last = None
        for ei in range(64):
            es = eb * 64 + ei
            slot = ei % 2
            lw2 = E.dma(w2_eng, W2_sb[:, slot, :], w_W2[:, es, :], waits=[w2_slot_free.get(slot)])
            w1b = psf[slot][0:32, 0:64]
            tt = None
            for ch in range(DCH):
                tt = E.emit("tensor",
                            (lambda t, _l=W1_sb[:, ch, ei, :], _r=XT[:, ch, ei, :],
                             _o=w1b, _st=(ch == 0), _sp=(ch == DCH - 1):
                             t.matmul(_o, _l, _r, start=_st, stop=_sp)),
                            waits=[l_w1, gV, gS, logits_V, w1_bank_free.get(slot)], inc="T")
            iv = E.emit("scalar",
                        (lambda s, _o=inner[:, slot, :], _b=b1_sb[:, es:es + 1], _p=w1b:
                         s.activation(out=_o, in_=_p, func=AF.Relu, bias=_b)),
                        waits=[tt, t_b1, inner_slot_free.get(slot)], inc="S")
            w1_bank_free[slot] = iv
            for half in range(2):
                yb = psf[4 + half][0:64, :]
                tt2 = E.emit("tensor",
                             (lambda t, _l=inner[:, slot, :],
                              _r=W2_sb[:, slot, half * 512:(half + 1) * 512], _o=yb:
                              t.matmul(_o, _l, _r, start=True, stop=True)),
                             waits=[iv, lw2, w2_bank_free.get(half)], inc="T")
                sev = E.emit("vector",
                             (lambda v, _s=rstage[:, half, :], _p=yb: v.tensor_copy(_s, _p)),
                             waits=[tt2, rstage_free.get(half)], inc="V")
                w2_bank_free[half] = sev
                dm = E.dma("sync", Rd[es, :, half * 512:(half + 1) * 512], rstage[:, half, :],
                           waits=[sev])
                rstage_free[half] = dm
                rd_last = dm
            inner_slot_free[slot] = E.tensor_tok()
            w2_slot_free[slot] = E.tensor_tok()
        w1_free = E.tensor_tok()
        prev_phase_acts = ("S", E.cnt["S"])
        rd_done.append(rd_last)

    sc_ps_free = {}
    rsb_free = None
    xt_free2 = ln_done[NT - 1]
    w1_acts = ("S", E.cnt["S"])
    out_done = []
    ost = lg_sb[:, :]
    for ti in range(NT):
        xl = E.dma("scalar", xt[:], x_in[ti * 128:(ti + 1) * 128, :], waits=[xt_free2])
        mms = []
        for j in range(4):
            lr = E.dma("sync", rsb[:, :], Rd[:, ti * 4 + j, :],
                       waits=rd_done + ([rsb_free] if rsb_free else []))
            for half in range(2):
                E.emit("tensor",
                       (lambda t, _l=pt_sb[:, (ti * 4 + j) * 32:(ti * 4 + j) * 32 + 32],
                        _r=rsb[:, half * 512:(half + 1) * 512],
                        _o=psf[2 + half][32 * j:32 * j + 32, :], _j=j:
                        t.matmul(_o, _l, _r, start=True, stop=False,
                                 tile_position=(0, 32 * _j))),
                       waits=[lr, route_done[-1], sc_ps_free.get(half), w1_acts], inc="T")
                tt = E.emit("tensor",
                            (lambda t, _l=pt_sb[:, (ti * 4 + j) * 32:(ti * 4 + j) * 32 + 32],
                             _r=b2_sb[:, half * 512:(half + 1) * 512],
                             _o=psf[2 + half][32 * j:32 * j + 32, :], _j=j:
                             t.matmul(_o, _l, _r, start=False, stop=True,
                                      tile_position=(0, 32 * _j))),
                            waits=[t_b2], inc="T")
                mms.append(tt)
            rsb_free = E.tensor_tok()
        prevh = sc_ps_free.get("ost")
        for half in range(2):
            e1 = E.emit("vector",
                        (lambda v, _p=psf[2 + half]: v.tensor_scalar_mul(ost, _p[:], 0.5)),
                        waits=mms + ([prevh] if prevh else []), inc="V", strict=True)
            e3 = E.emit("vector",
                        (lambda v, _x=xt[:, half * 512:(half + 1) * 512]:
                         v.tensor_tensor(ost, ost, _x, ALU.add)),
                        waits=[e1, xl], inc="V", strict=True)
            sc_ps_free[half] = e3
            dm = E.dma("sync",
                       x_out[ti * 128:(ti + 1) * 128, half * 512:(half + 1) * 512], ost,
                       waits=[e3])
            prevh = dm
            out_done.append(dm)
        sc_ps_free["ost"] = prevh
        xt_free2 = sc_ps_free[1]

    if dbg:
        E.dma("gpsimd", d_pt[:, :], pt_sb[:, :], waits=[route_done[-1]])
        E.dma("gpsimd", d_xt[:, :, :, :], XT[:, :, :, :], waits=[gV, gS])
    E.emit("sync", lambda e: None, waits=out_done)
    if fused is None:
        E.finalize(ctx)
        ctx.close()
    return nc


def host_prep_l0attn(inp):
    H = {}
    C = lambda a: np.ascontiguousarray(a, dtype=np.float32)
    Wq = np.asarray(inp["L0_attn_Wqkv"])
    H["Wqkv"] = C(np.concatenate([Wq, Wq], axis=0))
    H["bqkv"] = C(np.asarray(inp["L0_attn_bqkv"]).reshape(1, 192))
    H["Wr"] = C(np.asarray(inp["L0_attn_Wr"]).reshape(8, 128, 1024).transpose(1, 0, 2))
    H["br"] = C(np.asarray(inp["L0_attn_br"]).reshape(1, 1024))
    mj = np.zeros((128, 4, 512), np.float32)
    for j in range(4):
        kk = np.arange(128)[:, None] + 128 * j
        qq = np.arange(512)[None, :]
        mj[:, j, :] = (kk < qq).astype(np.float32)
    H["masks"] = mj.astype(ml_dtypes.bfloat16)
    return H


def build_attn_pre(nc, fused=None, xin_name="x_in"):
    dio = _phase_io(nc, fused, [
        (xin_name, [TOK, D], F32, "ExternalInput"),
        ("ident", [128, 128], F32, "ExternalInput"),
    ] + ([("xhT_out", [128, DCH, TOK], BF16, "ExternalOutput")] if fused is None
         else [("xag_in", [128, DCH, TOK], BF16, "Internal")]))
    x_in, w_ident = dio[xin_name], dio["ident"]
    if fused is None:
        xhT_out = dio["xhT_out"]
        ctx = ExitStack()
        sb = lambda name, shape, dt: ctx.enter_context(nc.sbuf_tensor(name, shape, dt))
    else:
        xag_in = dio["xag_in"]
        fused["ar"].reset()
        sb = fused["ar"].alloc
    xt = sb("xt", [128, D], F32)
    xhat = sb("xhat", [128, D], BF16)
    xhatT = sb("xhatT", [128, DCH, TOK], BF16)
    ident_sb = sb("ident_sb", [128, 128], BF16)
    lnst = sb("lnst", [128, NT, 8], F32)
    lncst = sb("lncst", [128, 2], F32)
    sqb = sb("sqb", [128, D], F32)
    if fused is None:
        ps_tr = ctx.enter_context(nc.psum_tensor("ps_tr", [128, 128], BF16))[:]
        E = Em(nc)
    else:
        ps_tr = fused["P8"][7][:].bitcast(BF16)[:, 0:128]
        E = fused["E"]
    E.lnconsts = lncst
    E.ln_sq = sqb[:]
    t_id = E.dma("gpsimd", ident_sb[:], w_ident[:, :])
    E.emit("vector", lambda v: v.memset(lncst[:, 0:1], 1e-5), inc="V")
    E.emit("vector", lambda v: v.memset(lncst[:, 1:2], -0.5), inc="V")
    ln_done = {}
    tr_free = {}
    for ti in range(NT):
        xl = E.dma("sync", xt[:], x_in[ti * 128:(ti + 1) * 128, :], waits=[ln_done.get(ti - 1)])
        t9 = emit_ln(E, xt[:], xhat[:], lnst[:, ti, :], ready=[xl],
                     war=[tr_free.get((ti - 1, DCH - 1))])
        ln_done[ti] = t9
        for ch in range(DCH):
            tt = E.emit("tensor",
                        (lambda t, _s=xhat[:, ch * 128:(ch + 1) * 128]:
                         t.transpose(ps_tr, _s, ident_sb[:])),
                        waits=[t9, t_id, tr_free.get((ti, ch - 1)), tr_free.get((ti - 1, DCH - 1))],
                        inc="T")
            ev = E.emit("vector",
                        (lambda v, _d=xhatT[:, ch, ti * 128:(ti + 1) * 128]:
                         v.tensor_copy(_d, ps_tr)),
                        waits=[tt], inc="V")
            tr_free[(ti, ch)] = ev
    fin = tr_free[(NT - 1, DCH - 1)]
    dms = []
    if fused is None:
        dms.append(E.dma("sync", xhT_out[:, :, :], xhatT[:, :, :], waits=[fin]))
    else:
        dms.append(E.dma("sync", xag_in[:, :, :], xhatT[:, :, :], waits=[fin]))
    E.emit("sync", lambda e: None, waits=dms)
    if fused is None:
        E.finalize(ctx)
        ctx.close()
    return nc


def build_attn_core(nc, dbg=False, fused=None):
    S = 8192
    NKT = S // 128
    NQB = S // 512
    dio = _phase_io(nc, fused, [
        ("Wqkv", [128, 192], F32, "ExternalInput"),
        ("bqkv", [1, 192], F32, "ExternalInput"),
        ("masks", [128, 4, 512], F32, "ExternalInput"),
        ("ident", [128, 128], F32, "ExternalInput"),
    ] + ([("xhT", [128, 2, S], BF16, "ExternalInput"),
          ("oT_out", [128, 2, S], BF16, "ExternalOutput")] if fused is None
         else [("xag_out", [8 * 128, DCH, TOK], BF16, "Internal"),
               ("oa2a_in", [8, 128, 2, TOK], BF16, "Internal")]))
    w_Wqkv, w_bqkv, w_masks, w_ident = dio["Wqkv"], dio["bqkv"], dio["masks"], dio["ident"]
    if fused is None:
        xhT, oT_out = dio["xhT"], dio["oT_out"]
    else:
        xag_out, oa2a_in = dio["xag_out"], dio["oa2a_in"]
        pb_, ph_ = fused["pb"], fused["ph"]
    if dbg:
        d_qT = nc.dram_tensor("d_qT", [64, S], F32, kind="ExternalOutput")
        d_kT = nc.dram_tensor("d_kT", [64, S], F32, kind="ExternalOutput")
        d_vp = nc.dram_tensor("d_vp", [128, 32, 130], F32, kind="ExternalOutput")
        d_ts = nc.dram_tensor("d_ts", [64, 65], F32, kind="ExternalOutput")
        d_bo = nc.dram_tensor("d_bo", [128, 512], F32, kind="ExternalOutput")
        d_pb = nc.dram_tensor("d_pb", [128, 512], F32, kind="ExternalOutput")
    if fused is None:
        ctx = ExitStack()
        sb = lambda name, shape, dt: ctx.enter_context(nc.sbuf_tensor(name, shape, dt))
    else:
        fused["ar"].reset()
        sb = fused["ar"].alloc
    ident_sb = sb("ident_sb", [128, 128], BF16)
    xh = sb("xh", [128, 2, S], BF16)
    qT = sb("qT", [64, S], BF16)
    kT = sb("kT", [64, S], BF16)
    vp = sb("vp", [128, NKT // 2, 130], BF16)
    pb = sb("pb", [128, 4, 512], BF16)
    oh = sb("oh", [64, 4, S], BF16)
    masks = sb("masks_sb", [128, 4, 512], BF16)
    ones_sb = sb("ones_sb", [128, 512], BF16)
    tsT = sb("tsT", [64, 65], BF16)
    ts_col = sb("ts_col", [65, 64], BF16)
    bq_sb = sb("bq_sb", [1, 192], BF16)
    sums = sb("sums", [128, S], F32)
    onesf = sb("onesf", [128, 64], F32)
    Wq_sb = sb("Wq_sb", [128, 192], BF16)

    if fused is None:
        S4 = [ctx.enter_context(nc.psum_tensor(f"ps_s{i}", [128, 512], F32))[:]
              for i in range(4)]
        ps_o = [ctx.enter_context(nc.psum_tensor(f"ps_o{i}", [128, 512], F32))[:]
                for i in range(2)]
        ps_q = ctx.enter_context(nc.psum_tensor("ps_q", [128, 512], F32))[:]
        ps_tr = ctx.enter_context(nc.psum_tensor("ps_tr", [128, 128], BF16))[:]
        E = Em(nc)
    else:
        P8 = fused["P8"]
        S4 = [P8[0][:], P8[1][:], P8[5][:], P8[6][:]]
        ps_o = [P8[2][:], P8[3][:]]
        ps_q = P8[4][:]
        ps_tr = P8[7][:].bitcast(BF16)[:, 0:128]
        E = fused["E"]
    if fused is None:
        t_x = E.dma("sync", xh[:, :, :], xhT[:, :, :])
    else:
        for ch in range(2):
            v = (xag_out.ap().rearrange("(s p) c t -> p c s t", p=128)
                 [:, ch:ch + 1, 0:4, :].squeeze(1))
            vin = _dyn(v, pb_ * (4 * 128 * DCH * TOK) + ph_ * (2 * TOK))
            t_x = E.dma("sync", xh[:, ch, :].rearrange("p (r t) -> p r t", r=4), vin)
    t_w = E.dma("gpsimd", Wq_sb[:], w_Wqkv[:, :])
    t_b = E.dma("gpsimd", bq_sb[:], w_bqkv[:, :])
    t_m = E.dma("sync", masks[:], w_masks[:, :, :])
    t_id = E.dma("sync", ident_sb[:], w_ident[:, :])
    if fused is not None:
        iom = fused["io"]
        for e in range(4):
            E.dma("gpsimd", iom["We1bf"][:, e, :, :, :], iom["We1"][:, e, :, :, :])
            E.dma("gpsimd", iom["We2bf"][:, e, :, :], iom["We2"][:, e, :, :])
        E.dma("gpsimd", iom["W1bf"][:, :, :, :], iom["W1"][:, :, :, :])
        E.dma("gpsimd", iom["W2bf"][:, :, :], iom["W2"][:, :, :])
        E.dma("gpsimd", iom["Wpbf"][:, :, :], iom["Wp"][:, :, :])
        E.dma("gpsimd", iom["Wfbf"][:, :, :], iom["Wf"][:, :, :])
    E.emit("vector", lambda v: v.memset(ones_sb[:], 1.0), inc="V")
    E.emit("vector", lambda v: v.memset(onesf[:], 1.0), inc="V")
    ones_tok = ("V", E.cnt["V"])

    oh_dmas = []
    bg = {}
    pb_guard = {}
    og = {}
    for hh in range(4):
        chunk = hh // 2
        base = 64 * (hh % 2)
        tp = (base, 0)
        nb = 0
        qk_evac = None
        for qc in range(2):
            dstT = qT if qc == 0 else kT
            for tt_i in range(NQB):
                c0 = tt_i * 512
                idx = nb % 4
                bank = S4[idx]
                nb += 1
                E.emit("tensor",
                       (lambda t, _l=Wq_sb[base:base + 64, qc * 64:(qc + 1) * 64],
                        _r=xh[base:base + 64, chunk, c0:c0 + 512], _tp=tp, _o=bank:
                        t.matmul(_o[0:64, :], _l, _r, start=True, stop=False,
                                 tile_position=_tp)),
                       waits=[t_x, t_w, bg.get(idx)], inc="T")
                m2 = E.emit("tensor",
                            (lambda t, _l=bq_sb[0:1, qc * 64:(qc + 1) * 64], _o=bank:
                             t.matmul(_o[0:64, :], _l, ones_sb[0:1, :],
                                      start=False, stop=True)),
                            waits=[t_b, ones_tok], inc="T")
                ev = E.emit("vector",
                            (lambda v, _d=dstT[0:64, c0:c0 + 512], _o=bank:
                             v.tensor_copy(_d, _o[0:64, :])),
                            waits=[m2], inc="V")
                bg[idx] = ev
                qk_evac = ev
        v_evac = None
        for kt in range(NKT):
            c0 = kt * 128
            idx = nb % 4
            bank = S4[idx]
            nb += 1
            E.emit("tensor",
                   (lambda t, _l=xh[base:base + 64, chunk, c0:c0 + 128],
                    _r=Wq_sb[base:base + 64, 128:192], _tp=tp, _o=bank:
                    t.matmul(_o[:, 0:64], _l, _r, start=True, stop=False, tile_position=_tp)),
                   waits=[t_x, t_w, bg.get(idx)], inc="T")
            E.emit("tensor",
                   (lambda t, _o=bank:
                    t.matmul(_o[:, 0:64], ones_sb[0:1, 0:128], bq_sb[0:1, 128:192],
                             start=False, stop=False)),
                   waits=[t_b], inc="T")
            m3 = E.emit("tensor",
                        (lambda t, _o=bank:
                         t.matmul(_o[:, 64:65], ones_sb[0:1, 0:128],
                                  ones_sb[0:1, 0:1], start=False, stop=True)),
                        inc="T")
            ev = E.emit("vector",
                        (lambda v, _d=vp[:, kt // 2, (kt % 2) * 65:(kt % 2) * 65 + 65], _o=bank:
                         v.tensor_copy(_d, _o[:, 0:65])),
                        waits=[m3], inc="V")
            bg[idx] = ev
            v_evac = ev
        ts_mm = None
        for kt in range(NKT):
            ts_mm = E.emit("tensor",
                           (lambda t, _l=vp[:, kt // 2, (kt % 2) * 65:(kt % 2) * 65 + 65],
                            _c=63 - kt:
                            t.matmul(ps_q[0:65, _c:_c + 1], _l, ones_sb[:, 0:1],
                                     start=True, stop=True)),
                           waits=[v_evac, bg.get("q")], inc="T")
        e1 = E.emit("vector", lambda v: v.tensor_copy(ts_col[:, :], ps_q[0:65, 0:64]),
                    waits=[ts_mm], inc="V")
        bg["q"] = e1
        tr1 = E.emit("tensor", lambda t: t.transpose(ps_tr[0:64, 0:65], ts_col[:, :],
                                                     ident_sb[0:65, 0:65]),
                     waits=[e1, t_id, bg.get("tr")], inc="T")
        e2 = E.emit("vector", lambda v: v.tensor_copy(tsT[:, :], ps_tr[0:64, 0:65]),
                    waits=[tr1], inc="V")
        bg["tr"] = e2

        seq = [(B, kt) for B in range(NQB) for kt in range(4 * B + 4)]
        exp_tok = {}

        def emit_qk(i, _seq=seq, _hh=hh, _qk=qk_evac):
            B, kt = _seq[i]
            idx = i % 4
            bank = S4[idx]
            q0 = B * 512
            sm = E.emit("tensor",
                        (lambda t, _l=kT[0:64, kt * 128:(kt + 1) * 128],
                         _r=qT[0:64, q0:q0 + 512], _o=bank:
                         t.matmul(_o, _l, _r, start=True, stop=True)),
                        waits=[_qk, bg.get(idx)], inc="T")
            if kt >= 4 * B:
                j = kt - 4 * B
                sm = E.emit("vector",
                            (lambda v, _o=bank, _m=masks[:, j, :]:
                             v.tensor_tensor(_o, _o, _m, ALU.mult)),
                            waits=[sm, t_m], inc="V")
            ex = E.emit("scalar",
                        (lambda s, _o=bank, _d=pb[:, idx, :]:
                         s.activation(out=_d, in_=_o, func=AF.Exp, scale=0.125)),
                        waits=[sm, pb_guard.get(idx)], inc="S")
            exp_tok[i] = ex
            bg[idx] = ex

        LOOK = 4
        for i in range(min(LOOK, len(seq))):
            emit_qk(i)
        qnext = LOOK
        for i, (B, kt) in enumerate(seq):
            bo = ps_o[B % 2]
            if kt == 0 and B < NQB - 1:
                ksuf = 4 * (NQB - 1 - B)
                E.emit("tensor",
                       (lambda t, _k=ksuf, _o=bo: t.matmul(_o[0:65, :], tsT[0:_k, :],
                                                           ones_sb[0:_k, :],
                                                           start=True, stop=False)),
                       waits=[e2, og.get(B % 2)], inc="T")
            av = E.emit("tensor",
                        (lambda t, _l=vp[:, kt // 2, (kt % 2) * 65:(kt % 2) * 65 + 65],
                         _r=pb[:, i % 4, :], _o=bo,
                         _st=(B == NQB - 1 and kt == 0), _sp=(kt == 4 * B + 3):
                         t.matmul(_o[0:65, :], _l, _r, start=_st, stop=_sp)),
                        waits=[exp_tok.pop(i), v_evac, og.get(B % 2)], inc="T")
            pb_guard[i % 4] = av
            if qnext < len(seq):
                emit_qk(qnext)
                qnext += 1
            if kt == 4 * B + 3:
                q0 = B * 512
                E.emit("vector",
                       (lambda v, _d=oh[0:64, hh, q0:q0 + 512], _o=bo:
                        v.tensor_copy(_d, _o[0:64, :])),
                       waits=[av], inc="V")
                evs = E.emit("vector",
                             (lambda v, _d=sums[64:65, q0:q0 + 512], _o=bo:
                              v.tensor_copy(_d, _o[64:65, :])),
                             waits=[av], inc="V")
                og[B % 2] = evs
        rcp = E.emit("vector", lambda v: v.reciprocal(sums[64:65, :], sums[64:65, :]),
                     waits=[og[0], og[1]], inc="V", strict=True)
        nm = None
        for j in range(NQB):
            idx = j % 4
            bank = S4[idx]
            bc = E.emit("tensor",
                        (lambda t, _o=bank, _r=sums[64:65, j * 512:(j + 1) * 512]:
                         t.matmul(_o[0:64, :], onesf[64:65, 0:64], _r, start=True, stop=True,
                                  tile_position=(64, 0))),
                        waits=[rcp, bg.get(idx)], inc="T")
            nm = E.emit("vector",
                        (lambda v, _d=oh[0:64, hh, j * 512:(j + 1) * 512], _o=bank:
                         v.tensor_tensor(_d, _d, _o[0:64, :], ALU.mult)),
                        waits=[bc], inc="V", strict=True)
            bg[idx] = nm
        if dbg and hh == 0:
            E.dma("gpsimd", d_qT[:, :], qT[:, :], waits=[qk_evac])
            E.dma("gpsimd", d_kT[:, :], kT[:, :], waits=[qk_evac])
            E.dma("gpsimd", d_vp[:, :, :], vp[:, :, :], waits=[v_evac])
            E.dma("gpsimd", d_ts[:, :], tsT[:, :], waits=[e2])
        odone = [nm]
        if fused is None:
            oh_dmas.append(E.dma("sync", oT_out[base:base + 64, chunk, :],
                                 oh[0:64, hh, :], waits=odone))
        else:
            for m in range(4):
                ov = oa2a_in[m, base:base + 64, chunk, :]
                ov = _dyn(ov, pb_ * (4 * 128 * 2 * TOK))
                oh_dmas.append(E.dma("sync", ov,
                                     oh[0:64, hh, m * TOK:(m + 1) * TOK], waits=odone))
    E.emit("sync", lambda e: None, waits=oh_dmas)
    if fused is None:
        E.finalize(ctx)
        ctx.close()
    return nc


def build_attn_proj(nc, fused=None, xin_name="x_in", xout_name="x_out"):
    dio = _phase_io(nc, fused, [
        (xin_name, [TOK, D], F32, "ExternalInput"),
        (xout_name, [TOK, D], F32, "ExternalOutput"),
        ("Wr", [128, 8, 1024], F32, "ExternalInput"),
        ("br", [1, 1024], F32, "ExternalInput"),
    ] + ([("oT", [128, DCH, TOK], BF16, "ExternalInput")] if fused is None
         else [("oa2a_out", [8, 128, 2, TOK], BF16, "Internal")]))
    x_in, x_out, w_Wr, w_br = dio[xin_name], dio[xout_name], dio["Wr"], dio["br"]
    if fused is None:
        oT = dio["oT"]
        ctx = ExitStack()
        sb = lambda name, shape, dt: ctx.enter_context(nc.sbuf_tensor(name, shape, dt))
    else:
        oa2a_out = dio["oa2a_out"]
        fused["ar"].reset()
        sb = fused["ar"].alloc
    o_sb = sb("o_sb", [128, DCH, TOK], BF16)
    Wr_sb = sb("Wr_sb", [128, 8, 1024], BF16)
    br_sb = sb("br_sb", [1, 1024], BF16)
    xt = sb("xt", [128, D], F32)
    ost = sb("ost", [128, D], F32)
    ones_sb = sb("ones_sb", [1, 128], BF16)
    if fused is None:
        ps = [ctx.enter_context(nc.psum_tensor(f"ps{i}", [128, 512], F32))[:]
              for i in range(2)]
        E = Em(nc)
        t_o = E.dma("sync", o_sb[:, :, :], oT[:, :, :])
    else:
        P8 = fused["P8"]
        ps = [P8[0][:], P8[1][:]]
        E = fused["E"]
        pb_ = fused["pb"]
        for r_ in range(4):
            v = oa2a_out[r_, :, :, :]
            vin = _dyn(v, pb_ * (4 * 128 * 2 * TOK))
            t_o = E.dma("gpsimd", o_sb[:, 2 * r_:2 * r_ + 2, :], vin)
    t_w = E.dma("gpsimd", Wr_sb[:], w_Wr[:, :, :])
    t_b = E.dma("gpsimd", br_sb[:], w_br[:, :])
    ot1 = E.emit("vector", lambda v: v.memset(ones_sb[:], 1.0), inc="V")
    guard = {}
    outs = []
    for ti in range(NT):
        xl = E.dma("scalar", xt[:], x_in[ti * 128:(ti + 1) * 128, :], waits=[guard.get("xt")])
        for half in range(2):
            bank = ps[half]
            for och in range(DCH):
                E.emit("tensor",
                       (lambda t, _l=o_sb[:, och, ti * 128:(ti + 1) * 128],
                        _r=Wr_sb[:, och, half * 512:(half + 1) * 512], _o=bank,
                        _st=(och == 0):
                        t.matmul(_o, _l, _r, start=_st, stop=False)),
                       waits=[t_o, t_w, guard.get(("ps", half))], inc="T")
            tb = E.emit("tensor",
                        (lambda t, _o=bank, _r=br_sb[0:1, half * 512:(half + 1) * 512]:
                         t.matmul(_o, ones_sb[0:1, :], _r, start=False, stop=True)),
                        waits=[t_b, ot1], inc="T")
            e1 = E.emit("vector",
                        (lambda v, _o=bank, _d=ost[:, half * 512:(half + 1) * 512],
                         _x=xt[:, half * 512:(half + 1) * 512]:
                         v.tensor_tensor(_d, _o, _x, ALU.add)),
                        waits=[tb, xl, guard.get("ost")], inc="V")
            guard[("ps", half)] = e1
        guard["xt"] = e1
        dm = E.dma("sync", x_out[ti * 128:(ti + 1) * 128, :], ost[:], waits=[e1])
        guard["ost"] = dm
        outs.append(dm)
    E.emit("sync", lambda e: None, waits=outs)
    if fused is None:
        E.finalize(ctx)
        ctx.close()
    return nc


def host_prep_conv(inp):
    H = {}
    C = lambda a: np.ascontiguousarray(a, dtype=np.float32).astype(ml_dtypes.bfloat16)
    N1 = N2 = 128
    NFULL = N1 * N2
    K2 = 65
    t2 = np.arange(N2)[:, None]
    k2 = np.arange(K2)[None, :]
    H["L1c"] = C(np.concatenate([np.cos(2 * np.pi * t2 * k2 / N2),
                                 -np.sin(2 * np.pi * t2 * k2 / N2)], axis=1))
    t1 = np.arange(N1)[:, None]
    k1 = np.arange(N1)[None, :]
    M2 = np.zeros((128, K2, 3, 128), np.float32)
    for q in range(K2):
        ang = 2 * np.pi * t1 * (q + 128 * k1) / NFULL
        M2[:, q, 0, :] = np.cos(ang)
        M2[:, q, 1, :] = np.sin(ang)
        M2[:, q, 2, :] = -np.sin(ang)
    H["M2"] = C(M2)
    k1c = np.arange(N1)[:, None]
    jb = np.arange(N1)[None, :]
    H["I1"] = C(np.stack([
        np.concatenate([np.cos(2 * np.pi * k1c * jb / N1),
                        np.sin(2 * np.pi * k1c * jb / N1)], axis=1),
        np.concatenate([-np.sin(2 * np.pi * k1c * jb / N1),
                        np.cos(2 * np.pi * k1c * jb / N1)], axis=1)], axis=1))
    alpha = np.full(K2, 2.0)
    alpha[0] = 1.0
    alpha[64] = 1.0
    ja = np.arange(64)[None, :]
    k2c = np.arange(K2)[:, None]
    I2 = np.zeros((65, 128, 2, 64), np.float32)
    for b in range(N1):
        ang = 2 * np.pi * (b + 128 * ja) * k2c / NFULL
        I2[:, b, 0, :] = alpha[:, None] * np.cos(ang) / NFULL
        I2[:, b, 1, :] = -alpha[:, None] * np.sin(ang) / NFULL
    H["I2"] = C(I2)
    F = lambda a: np.ascontiguousarray(a, dtype=np.float32)
    H["Wp"] = F(np.asarray(inp["L1_attn_Wp"]).reshape(8, 128, 4096).transpose(1, 0, 2))
    H["bp"] = F(np.asarray(inp["L1_attn_bp"]).reshape(1, 4096))
    H["Wf"] = F(np.asarray(inp["L1_attn_Wf"]).reshape(8, 128, 3072).transpose(1, 0, 2))
    H["bf"] = F(np.asarray(inp["L1_attn_bf"]).reshape(1, 3072))
    T, Dm = 8192, 1024
    pos = np.arange(T, dtype=np.float64)[:, None]
    i = np.arange(Dm)[None, :]
    angle = pos / np.power(10000.0, (2 * (i // 2)).astype(np.float64) / Dm)
    pe = np.where(i % 2 == 0, np.sin(angle), np.cos(angle)).astype(np.float32)
    H["posT"] = C(pe.reshape(8192, 8, 128).transpose(2, 1, 0))
    return H


def build_xp(nc, fused=None, xin_name="x_in"):
    dio = _phase_io(nc, fused, [
        (xin_name, [TOK, D], F32, "ExternalInput"),
        ("Wp", [128, 8, 4096], F32, "ExternalInput"),
        ("bp", [1, 4096], F32, "ExternalInput"),
        ("Wf", [128, 8, 3072], F32, "ExternalInput"),
        ("bf", [1, 3072], F32, "ExternalInput"),
        ("posT", [128, 8, 1024], F32, "ExternalInput"),
        ("ident", [128, 128], F32, "ExternalInput"),
    ] + ([("xp_out", [TOK, 4096], BF16, "ExternalOutput"),
          ("f_out", [1024, 3072], BF16, "ExternalOutput")] if fused is None
         else [("xpa2a_in", [8, 4, TOK, 128], BF16, "Internal"),
               ("fa2a_in", [8, 3, 1024, 128], BF16, "Internal")]))
    x_in, w_ident, w_pos = dio[xin_name], dio["ident"], dio["posT"]
    w_Wp, w_bp, w_Wf, w_bf = dio["Wp"], dio["bp"], dio["Wf"], dio["bf"]
    if fused is not None:
        w_Wp, w_Wf = fused["io"]["Wpbf"], fused["io"]["Wfbf"]
    wp_eng = "scalar" if fused is not None else "gpsimd"
    wf_eng = "sync" if fused is not None else "gpsimd"
    if fused is None:
        xp_out, f_out = dio["xp_out"], dio["f_out"]
        ctx = ExitStack()
        sb = lambda name, shape, dt: ctx.enter_context(nc.sbuf_tensor(name, shape, dt))
    else:
        xpa2a_in, fa2a_in = dio["xpa2a_in"], dio["fa2a_in"]
        fused["ar"].reset()
        sb = fused["ar"].alloc
    xt = sb("xt", [128, D], F32)
    xhat = sb("xhat", [128, D], BF16)
    xhatT = sb("xhatT", [128, DCH, TOK], BF16)
    posT_sb = sb("posT_sb", [128, DCH, 1024], BF16)
    Wp_sb = sb("Wp_sb", [128, 8, 4096], BF16)
    Wf_sb = sb("Wf_sb", [128, 8, 3072], BF16)
    bp_sb = sb("bp_sb", [1, 4096], BF16)
    bf_sb = sb("bf_sb", [1, 3072], BF16)
    ident_sb = sb("ident_sb", [128, 128], BF16)
    ones_sb = sb("ones_sb", [1, 128], BF16)
    ost = sb("ost", [128, 512], BF16)
    lnst = sb("lnst", [128, NT, 8], F32)
    lncst = sb("lncst", [128, 2], F32)
    sqb = sb("sqb", [128, D], F32)
    if fused is None:
        ps = [ctx.enter_context(nc.psum_tensor(f"ps{i}", [128, 512], F32))[:]
              for i in range(2)]
        ps_tr = ctx.enter_context(nc.psum_tensor("ps_tr", [128, 128], BF16))[:]
        E = Em(nc)
    else:
        P8 = fused["P8"]
        ps = [P8[0][:], P8[1][:]]
        ps_tr = P8[7][:].bitcast(BF16)[:, 0:128]
        E = fused["E"]
    E.lnconsts = lncst
    E.ln_sq = sqb[:]
    t_id = E.dma("gpsimd", ident_sb[:], w_ident[:, :])
    t_wp = E.dma(wp_eng, Wp_sb[:], w_Wp[:, :, :])
    t_bp = E.dma("gpsimd", bp_sb[:], w_bp[:, :])
    t_wf = E.dma(wf_eng, Wf_sb[:], w_Wf[:, :, :])
    t_bf = E.dma("gpsimd", bf_sb[:], w_bf[:, :])
    t_pos = E.dma("gpsimd", posT_sb[:], w_pos[:, :, :])
    E.emit("vector", lambda v: v.memset(lncst[:, 0:1], 1e-5), inc="V")
    E.emit("vector", lambda v: v.memset(lncst[:, 1:2], -0.5), inc="V")
    t_on = E.emit("vector", lambda v: v.memset(ones_sb[:], 1.0), inc="V")
    ln_done = {}
    tr_free = {}
    for ti in range(NT):
        xl = E.dma("sync", xt[:], x_in[ti * 128:(ti + 1) * 128, :], waits=[ln_done.get(ti - 1)])
        t9 = emit_ln(E, xt[:], xhat[:], lnst[:, ti, :], ready=[xl],
                     war=[tr_free.get((ti - 1, DCH - 1))])
        ln_done[ti] = t9
        for ch in range(DCH):
            tt = E.emit("tensor",
                        (lambda t, _s=xhat[:, ch * 128:(ch + 1) * 128]:
                         t.transpose(ps_tr, _s, ident_sb[:])),
                        waits=[t9, t_id, tr_free.get((ti, ch - 1)), tr_free.get((ti - 1, DCH - 1))],
                        inc="T")
            ev = E.emit("vector",
                        (lambda v, _d=xhatT[:, ch, ti * 128:(ti + 1) * 128]:
                         v.tensor_copy(_d, ps_tr)),
                        waits=[tt], inc="V")
            tr_free[(ti, ch)] = ev
    xT_all = ev
    guard = {}
    outs = []
    for ti in range(NT):
        for cb in range(8):
            bank = ps[cb % 2]
            for dch in range(DCH):
                E.emit("tensor",
                       (lambda t, _l=xhatT[:, dch, ti * 128:(ti + 1) * 128],
                        _r=Wp_sb[:, dch, cb * 512:(cb + 1) * 512], _o=bank, _st=(dch == 0):
                        t.matmul(_o, _l, _r, start=_st, stop=False)),
                       waits=[xT_all, t_wp, guard.get(cb % 2)], inc="T")
            tb = E.emit("tensor",
                        (lambda t, _o=bank, _r=bp_sb[0:1, cb * 512:(cb + 1) * 512]:
                         t.matmul(_o, ones_sb[0:1, :], _r, start=False, stop=True)),
                        waits=[t_bp, t_on], inc="T")
            ev = E.emit("vector", (lambda v, _o=bank: v.tensor_copy(ost[:], _o)),
                        waits=[tb, guard.get("ost")], inc="V")
            if fused is None:
                dm = E.dma("sync", xp_out[ti * 128:(ti + 1) * 128, cb * 512:(cb + 1) * 512],
                           ost[:], waits=[ev])
            else:
                s_, mh = cb // 2, (cb % 2) * 4
                dm = E.dma("sync",
                           xpa2a_in[mh:mh + 4, s_, ti * 128:(ti + 1) * 128, :]
                           .rearrange("m p c -> p m c"),
                           ost[:].rearrange("p (m c) -> p m c", c=128), waits=[ev])
            guard[cb % 2] = ev
            guard["ost"] = dm
            outs.append(dm)
    for ti in range(8):
        for cb in range(6):
            bank = ps[cb % 2]
            for dch in range(DCH):
                E.emit("tensor",
                       (lambda t, _l=posT_sb[:, dch, ti * 128:(ti + 1) * 128],
                        _r=Wf_sb[:, dch, cb * 512:(cb + 1) * 512], _o=bank, _st=(dch == 0):
                        t.matmul(_o, _l, _r, start=_st, stop=False)),
                       waits=[t_pos, t_wf, guard.get(cb % 2)], inc="T")
            tb = E.emit("tensor",
                        (lambda t, _o=bank, _r=bf_sb[0:1, cb * 512:(cb + 1) * 512]:
                         t.matmul(_o, ones_sb[0:1, :], _r, start=False, stop=True)),
                        waits=[t_bf, t_on], inc="T")
            ev = E.emit("vector", (lambda v, _o=bank: v.tensor_copy(ost[:], _o)),
                        waits=[tb, guard.get("ost")], inc="V")
            if fused is None:
                dm = E.dma("sync", f_out[ti * 128:(ti + 1) * 128, cb * 512:(cb + 1) * 512],
                           ost[:], waits=[ev])
            else:
                s_, mh = cb // 2, (cb % 2) * 4
                dm = E.dma("sync",
                           fa2a_in[mh:mh + 4, s_, ti * 128:(ti + 1) * 128, :]
                           .rearrange("m p c -> p m c"),
                           ost[:].rearrange("p (m c) -> p m c", c=128), waits=[ev])
            guard[cb % 2] = ev
            guard["ost"] = dm
            outs.append(dm)
    E.emit("sync", lambda e: None, waits=outs)
    if fused is None:
        E.finalize(ctx)
        ctx.close()
    return nc


def build_conv(nc, dbg=False, maxph=99, fused=None):
    G = 64
    K2 = 65
    dio = _phase_io(nc, fused, [
        ("L1c", [128, 130], F32, "ExternalInput"),
        ("M2", [128, 65, 3, 128], F32, "ExternalInput"),
        ("I1", [128, 2, 256], F32, "ExternalInput"),
        ("I2", [65, 128, 2, 64], F32, "ExternalInput"),
    ] + ([("u_in", [4, 2, 128, 64, 128], BF16, "ExternalInput"),
          ("f_in", [3, 128, 64, 128], BF16, "ExternalInput"),
          ("y_out", [2, 128, 64, 128], BF16, "ExternalOutput")] if fused is None
         else [("xpa2a_out", [8, 4, TOK, 128], BF16, "Internal"),
               ("fa2a_out", [8, 3, 1024, 128], BF16, "Internal"),
               ("ya2a_in", [8, 128, TOK], BF16, "Internal")]))
    c_L1, c_M2, c_I1, c_I2 = dio["L1c"], dio["M2"], dio["I1"], dio["I2"]
    if fused is None:
        u_in, f_in, y_out = dio["u_in"], dio["f_in"], dio["y_out"]
    else:
        xpa2a_out, fa2a_out, ya2a_in = dio["xpa2a_out"], dio["fa2a_out"], dio["ya2a_in"]
    if dbg:
        d_A = nc.dram_tensor("d_A", [128, 2, 65, 64], F32, kind="ExternalOutput")
        d_X = nc.dram_tensor("d_X", [128, 2, 65, 64], F32, kind="ExternalOutput")
        d_G = nc.dram_tensor("d_G", [128, 2, 65, 64], F32, kind="ExternalOutput")
        d_Z = nc.dram_tensor("d_Z", [128, 2, 64, 65], F32, kind="ExternalOutput")
        d_At = nc.dram_tensor("d_At", [65, 2, 128, 64], F32, kind="ExternalOutput")
    if fused is None:
        ctx = ExitStack()
        sb = lambda name, shape, dt: ctx.enter_context(nc.sbuf_tensor(name, shape, dt))
    else:
        fused["ar"].reset()
        sb = fused["ar"].alloc
    M2_sb = sb("M2_sb", [128, 65, 3, 128], BF16)
    I2_sb = sb("I2_sb", [65, 128, 2, 64], BF16)
    I1_sb = sb("I1_sb", [128, 2, 256], BF16)
    L1_sb = sb("L1_sb", [128, 130], BF16)
    yD1 = sb("yD1", [64, G, 128], BF16)
    fD1 = sb("fD1", [64, G, 128], BF16)
    gD1 = sb("gD1", [64, G, 128], BF16) if fused is None else None
    gD1s = sb("gD1s", [64, 128, G], BF16)[:] if fused is not None else None
    Ab_t = sb("Ab_t", [128, 2, K2, G], BF16)
    Fb_t = sb("Fb_t", [128, 2, K2, G], BF16)
    arena = sb("arena", [128, 16640], BF16)
    Xb = arena[:, 0:8320].rearrange("p (x q c) -> p x q c", x=2, q=K2)
    Gb = arena[:, 8320:16640].rearrange("p (x q c) -> p x q c", x=2, q=K2)
    At = arena[0:65, 0:16384].rearrange("p (x j c) -> p x j c", x=2, j=128)
    Ab = Ab_t[:]
    Fb = Fb_t[:]
    Zb = Ab_t[:].rearrange("p x q c -> p x c q")
    Sc = Fb_t[:].rearrange("p x q c -> p x c q")

    if fused is None:
        psf = [ctx.enter_context(nc.psum_tensor(f"cps{i}", [128, 512], F32))[:]
               for i in range(8)]
        E = Em(nc)
    else:
        psf = [p[:] for p in fused["P8"]]
        E = fused["E"]
    uSt = (Ab_t[:].rearrange("p x q c -> p (x q c)")[0:64, 0:128 * G]
           .rearrange("p (a c) -> p a c", c=G)) if fused is not None else None
    t_m2 = E.dma("scalar" if fused is not None else "gpsimd", M2_sb[:], c_M2[:, :, :, :])
    t_i2 = E.dma("scalar" if fused is not None else "gpsimd", I2_sb[:], c_I2[:, :, :, :])
    t_i1 = E.dma("gpsimd", I1_sb[:], c_I1[:, :, :])
    t_l1 = E.dma("gpsimd", L1_sb[:], c_L1[:, :])
    gd = {}
    out_dmas = []

    def vs():
        return [("V", E.cnt["V"]), ("S", E.cnt["S"])]

    def fwd_l1(srcD1, Adst, tok_src, war):
        for c0 in range(0, G, 3):
            ncc = min(3, G - c0)
            bi = (c0 // 3) % 2
            bank = psf[bi]
            mm = None
            for j in range(ncc):
                mm = E.emit("tensor",
                            (lambda t, _l=srcD1[0:64, c0 + j, :],
                             _o=bank[:, j * 130:(j + 1) * 130]:
                             t.matmul(_o, _l, L1_sb[0:64, :], start=True, stop=True)),
                            waits=[tok_src, t_l1] + gd.get(("l1b", bi), []), inc="T")
            bv = bank[:, 0:ncc * 130].rearrange("p (c q) -> p c q", q=130)
            eng = "vector" if bi == 0 else "scalar"
            sem = "V" if bi == 0 else "S"
            def mkcopy(comp, _bv=bv, _ncc=ncc, _c0=c0, _Adst=Adst):
                s_ap = _bv[:, :, comp * K2:(comp + 1) * K2].rearrange("p c q -> p q c")
                d_ap = _Adst[:, comp, :, _c0:_c0 + _ncc]
                return lambda en: (en.tensor_copy(d_ap, s_ap) if hasattr(en, "tensor_copy")
                                   else en.copy(d_ap, s_ap))
            E.emit(eng, mkcopy(0), waits=[mm] + war, inc=sem)
            e2 = E.emit(eng, mkcopy(1), waits=[], inc=sem)
            gd[("l1b", bi)] = [e2]
        return vs()

    def fwd_l2(Asrc, Fsrc, Xdst, Gdst, tokAF, war):
        for o0 in range(0, K2, 8):
            nq = min(8, K2 - o0)
            par = (o0 // 8) % 2
            bks = [psf[4 * par + i] for i in range(4)]
            mm = None
            for qq in range(nq):
                q = o0 + qq
                outs = [b[:, qq * 64:(qq + 1) * 64] for b in bks]
                mm = E.emit("tensor",
                            (lambda t, _l=M2_sb[:, q, 0, :], _r=Asrc[:, 0, q, :], _o=outs[0]:
                             t.matmul(_o, _l, _r, start=True, stop=False)),
                            waits=[t_m2] + tokAF + gd.get(("l2b", par), []), inc="T")
                E.emit("tensor", (lambda t, _l=M2_sb[:, q, 0, :], _r=Asrc[:, 1, q, :], _o=outs[1]:
                                  t.matmul(_o, _l, _r, start=True, stop=False)), inc="T")
                E.emit("tensor", (lambda t, _l=M2_sb[:, q, 0, :], _r=Fsrc[:, 0, q, :], _o=outs[2]:
                                  t.matmul(_o, _l, _r, start=True, stop=False)), inc="T")
                E.emit("tensor", (lambda t, _l=M2_sb[:, q, 0, :], _r=Fsrc[:, 1, q, :], _o=outs[3]:
                                  t.matmul(_o, _l, _r, start=True, stop=False)), inc="T")
                E.emit("tensor", (lambda t, _l=M2_sb[:, q, 1, :], _r=Asrc[:, 1, q, :], _o=outs[0]:
                                  t.matmul(_o, _l, _r, start=False, stop=True)), inc="T")
                E.emit("tensor", (lambda t, _l=M2_sb[:, q, 1, :], _r=Fsrc[:, 1, q, :], _o=outs[2]:
                                  t.matmul(_o, _l, _r, start=False, stop=True)), inc="T")
                E.emit("tensor", (lambda t, _l=M2_sb[:, q, 2, :], _r=Asrc[:, 0, q, :], _o=outs[1]:
                                  t.matmul(_o, _l, _r, start=False, stop=True)), inc="T")
                mm = E.emit("tensor",
                            (lambda t, _l=M2_sb[:, q, 2, :], _r=Fsrc[:, 0, q, :], _o=outs[3]:
                             t.matmul(_o, _l, _r, start=False, stop=True)), inc="T")
            toks = []
            for i, (dst, comp) in enumerate(((Xdst, 0), (Xdst, 1), (Gdst, 0), (Gdst, 1))):
                eng = "vector" if i % 2 == 0 else "scalar"
                def bld(en, _b=bks[i], _d=dst[:, comp, o0:o0 + nq, :], _n=nq):
                    srcap = _b[:, 0:_n * 64].rearrange("p (q c) -> p q c", c=64)
                    return en.tensor_copy(_d, srcap) if hasattr(en, "tensor_copy") else en.copy(_d, srcap)
                toks.append(E.emit(eng, bld, waits=[mm] + war,
                                   inc="V" if eng == "vector" else "S"))
            gd[("l2b", par)] = toks
        return vs() + [E.tensor_tok()]

    def pointwise(war):
        t1 = E.emit("vector", lambda v: v.tensor_tensor(
            Sc[:, 0], Xb[:, 0].rearrange("p q c -> p c q"),
            Gb[:, 0].rearrange("p q c -> p c q"), ALU.mult), waits=war, inc="V", strict=True)
        t2 = E.emit("vector", lambda v: v.tensor_tensor(
            Sc[:, 1], Xb[:, 1].rearrange("p q c -> p c q"),
            Gb[:, 1].rearrange("p q c -> p c q"), ALU.mult), inc="V")
        t3 = E.emit("vector", lambda v: v.tensor_tensor(
            Zb[:, 0], Sc[:, 0], Sc[:, 1], ALU.subtract), waits=[t1, t2], inc="V", strict=True)
        t4 = E.emit("vector", lambda v: v.tensor_tensor(
            Sc[:, 0], Xb[:, 0].rearrange("p q c -> p c q"),
            Gb[:, 1].rearrange("p q c -> p c q"), ALU.mult), waits=[t3], inc="V", strict=True)
        t5 = E.emit("vector", lambda v: v.tensor_tensor(
            Sc[:, 1], Xb[:, 1].rearrange("p q c -> p c q"),
            Gb[:, 0].rearrange("p q c -> p c q"), ALU.mult), inc="V")
        t6 = E.emit("vector", lambda v: v.tensor_tensor(
            Zb[:, 1], Sc[:, 0], Sc[:, 1], ALU.add), waits=[t4, t5], inc="V", strict=True)
        return [t3, t6]

    def inv_l1(war):
        for c0 in range(0, G, 2):
            bi = (c0 // 2) % 4
            bank = psf[bi]
            mm = None
            for j in range(2):
                c = c0 + j
                E.emit("tensor",
                       (lambda t, _l=Zb[:, 0, c, :], _o=bank[0:65, j * 256:(j + 1) * 256]:
                        t.matmul(_o, _l, I1_sb[:, 0, :], start=True, stop=False)),
                       waits=[t_i1] + gd.get(("i1b", bi), []) + war, inc="T")
                mm = E.emit("tensor",
                            (lambda t, _l=Zb[:, 1, c, :], _o=bank[0:65, j * 256:(j + 1) * 256]:
                             t.matmul(_o, _l, I1_sb[:, 1, :], start=False, stop=True)),
                            inc="T")
            bv = bank[0:65, 0:512].rearrange("p (c x j) -> p c x j", x=2, j=128)
            eng = "vector" if bi % 2 == 0 else "scalar"
            sem = "V" if bi % 2 == 0 else "S"
            def mkcopy(comp, _bv=bv, _c0=c0):
                s_ap = _bv[:, :, comp, :].rearrange("p c j -> p j c")
                d_ap = At[:, comp, :, _c0:_c0 + 2]
                return lambda en: (en.tensor_copy(d_ap, s_ap) if hasattr(en, "tensor_copy")
                                   else en.copy(d_ap, s_ap))
            E.emit(eng, mkcopy(0), waits=[mm] + war, inc=sem)
            e2 = E.emit(eng, mkcopy(1), waits=[], inc=sem)
            gd[("i1b", bi)] = [e2]
        return vs()

    def inv_l2(gsrc, ydst, tokAt, tok_g):
        for j0 in range(0, 128, 8):
            par = (j0 // 8) % 2
            bank = psf[6 + par]
            mm = None
            for jj in range(8):
                jb = j0 + jj
                E.emit("tensor",
                       (lambda t, _l=I2_sb[:, jb, 0, :], _r=At[:, 0, jb, :],
                        _o=bank[0:64, jj * 64:(jj + 1) * 64]:
                        t.matmul(_o, _l, _r, start=True, stop=False)),
                       waits=[t_i2] + gd.get(("i2b", par), []) + tokAt, inc="T")
                mm = E.emit("tensor",
                            (lambda t, _l=I2_sb[:, jb, 1, :], _r=At[:, 1, jb, :],
                             _o=bank[0:64, jj * 64:(jj + 1) * 64]:
                             t.matmul(_o, _l, _r, start=False, stop=True)),
                            inc="T")
            bv = bank[0:64, :].rearrange("p (j c) -> p j c", c=64)
            e = E.emit("vector",
                       (lambda v, _s=bv.rearrange("p j c -> p c j"),
                        _g=gsrc[0:64, :, j0:j0 + 8], _d=ydst[0:64, :, j0:j0 + 8]:
                        v.tensor_tensor(_d, _s, _g, ALU.mult)),
                       waits=[mm, tok_g], inc="V")
            gd[("i2b", par)] = [e]
        return [("V", E.cnt["V"])]

    def load_D1(dst_name, dst, kind, s, b, ch0, waits):
        toks = []
        if fused is None:
            srct = {"u": u_in, "f": f_in}[kind if kind != "g" else "u"]
            if kind == "f":
                ap = f_in[s, ch0:ch0 + G, :, :].rearrange("c t x -> t c x")
            else:
                ap = u_in[s, b, ch0:ch0 + G, :, :].rearrange("c t x -> t c x")
            return [E.dma("sync", dst[:, :, :], ap, waits=waits)], None
        if kind in ("u", "g"):
            tgt = uSt if kind == "u" else dst
            for qi in range(4):
                toks.append(E.dma("sync", tgt[16 * qi:16 * qi + 16, :, :],
                                  xpa2a_out[b * 4 + qi, s, :, ch0:ch0 + G]
                                  .rearrange("(a j) c -> a j c", j=128), waits=waits))
        else:
            for si in range(8):
                toks.append(E.dma("sync", uSt[8 * si:8 * si + 8, :, :],
                                  fa2a_out[si, s, :, ch0:ch0 + G]
                                  .rearrange("(a j) c -> a j c", j=128), waits=waits))
        if kind == "g":
            return toks, None
        rv = E.emit("vector",
                    (lambda v, _d=dst[:, :, :], _s=uSt[:].rearrange("t a c -> t c a")[0:64]:
                     v.tensor_copy(_d, _s)), waits=toks, inc="V")
        return [rv], rv

    for g in range(4 if maxph >= 99 else 1):
        b, ch0 = g // 2, (g % 2) * 64
        if fused is None:
            tok_ys, _ = load_D1("y", yD1, "u", 0, b, ch0, gd.get("yD1w", []))
            tok_y = tok_ys[0]
        else:
            tok_ys, _ = load_D1("y", yD1, "u", 0, b, ch0,
                                gd.get("yD1w", []) + gd.get("uStw", [])
                                + gd.get("Zread", []) + [E.tensor_tok()])
            gd["uStw"] = [tok_ys[0]]
            tok_y = tok_ys[0]
        for s in range(3 if maxph >= 99 else 1):
            if fused is None:
                tok_fs, _ = load_D1("f", fD1, "f", s, b, ch0, gd.get("fD1w", []))
                tok_f = tok_fs[0]
                tok_gs, _ = load_D1("g", gD1, "g", s + 1, b, ch0, gd.get("gD1w", []))
                tok_g = tok_gs[0]
                gview = gD1
            else:
                tok_fs, _ = load_D1("f", fD1, "f", s, b, ch0,
                                    gd.get("fD1w", []) + gd.get("uStw", [])
                                    + gd.get("Zread", []) + [E.tensor_tok()])
                gd["uStw"] = [tok_fs[0]]
                tok_f = tok_fs[0]
                tok_gs, _ = load_D1("g", gD1s, "g", s + 1, b, ch0, gd.get("gD1w", []))
                tok_g = tok_gs[-1]
                gview = gD1s.rearrange("t a c -> t c a")
            warA = gd.get("Zread", []) + (gd.get("uStw", []) if fused is not None else [])
            tokA = fwd_l1(yD1, Ab, tok_y, warA)
            if maxph <= 1:
                dm = E.dma("sync", y_out[0, 0:64, :, :].rearrange("c a j -> a c j"),
                           yD1[:, :, :], waits=tokA)
                E.emit("sync", lambda e: None, waits=[dm])
                E.finalize(ctx)
                ctx.close()
                return nc
            tokF = fwd_l1(fD1, Fb, tok_f, warA)
            warX = gd.get("Atread", [])
            if maxph <= 2:
                dm = E.dma("sync", y_out[0, 0:64, :, :].rearrange("c a j -> a c j"),
                           yD1[:, :, :], waits=tokA + tokF)
                E.emit("sync", lambda e: None, waits=[dm])
                E.finalize(ctx)
                ctx.close()
                return nc
            tokX = fwd_l2(Ab, Fb, Xb, Gb, tokA + tokF, warX)
            if dbg and g == 0 and s == 0:
                da = E.dma("gpsimd", d_A[:, :, :, :], Ab, waits=tokA)
                db = E.dma("gpsimd", d_X[:, :, :, :], Xb, waits=tokX)
                dc = E.dma("gpsimd", d_G[:, :, :, :], Gb, waits=tokX)
                tokX = tokX + [da, db, dc]
            if maxph <= 3:
                dm = E.dma("sync", y_out[0, 0:64, :, :].rearrange("c a j -> a c j"),
                           yD1[:, :, :], waits=tokX)
                E.emit("sync", lambda e: None, waits=[dm])
                E.finalize(ctx)
                ctx.close()
                return nc
            tokZ = pointwise(tokX)
            if dbg and g == 0 and s == 0:
                dz = E.dma("gpsimd", d_Z[:, :, :, :].rearrange("p x c q -> p (x c q)"),
                           Ab_t[:].rearrange("p x q c -> p (x q c)"), waits=tokZ)
                tokZ = tokZ + [dz]
            if maxph <= 4:
                dm = E.dma("sync", y_out[0, 0:64, :, :].rearrange("c a j -> a c j"),
                           yD1[:, :, :], waits=tokZ)
                E.emit("sync", lambda e: None, waits=[dm])
                E.finalize(ctx)
                ctx.close()
                return nc
            tokT = inv_l1(tokZ)
            gd["Zread"] = [E.tensor_tok()]
            if dbg and g == 0 and s == 0:
                dt_ = E.dma("gpsimd", d_At[:, :, :, :], At, waits=tokT)
                tokT = tokT + [dt_]
            if maxph <= 5:
                dm = E.dma("sync", y_out[0, 0:64, :, :].rearrange("c a j -> a c j"),
                           yD1[:, :, :], waits=tokT)
                E.emit("sync", lambda e: None, waits=[dm])
                E.finalize(ctx)
                ctx.close()
                return nc
            tok_yv = inv_l2(gview if fused is not None else gD1, yD1, tokT, tok_g)
            gd["Atread"] = [E.tensor_tok()]
            gd["fD1w"] = [E.tensor_tok()]
            gd["gD1w"] = tok_yv
            tok_y = tok_yv[0]
        if fused is None:
            dm = E.dma("sync", y_out[b, ch0:ch0 + G, :, :].rearrange("c a j -> a c j"),
                       yD1[:, :, :], waits=[tok_y])
            out_dmas.append(dm)
            gd["yD1w"] = [dm]
        else:
            dms = []
            for tq in range(4):
                dms.append(E.dma("sync",
                                 ya2a_in[b * 4 + tq, ch0:ch0 + G, :]
                                 .rearrange("c (a j) -> a c j", j=128),
                                 yD1[16 * tq:16 * tq + 16, :, :], waits=[tok_y]))
            out_dmas += dms
            gd["yD1w"] = dms
        if maxph <= 6:
            break
    E.emit("sync", lambda e: None, waits=out_dmas)
    if fused is None:
        E.finalize(ctx)
        ctx.close()
    return nc


def build_fused(nc, dbg=False):
    io = {}

    def dram(name, shape, dt, kind):
        io[name] = nc.dram_tensor(name, shape, dt, kind=kind)

    dram("x_in0", [TOK, D], F32, "ExternalInput")
    dram("x_out", [TOK, D], F32, "ExternalOutput")
    for n, s in (("Wc", [128, 8, 128]), ("bc", [1, 128]),
                 ("W1", [128, 8, 128, 32]), ("b1", [32, 128]), ("W2", [32, 128, 1024]),
                 ("b2", [128, 1024]),
                 ("Wqkv", [128, 192]), ("bqkv", [1, 192]),
                 ("Wr", [128, 8, 1024]), ("br", [1, 1024]),
                 ("rW1", [128, 8, 16, 128]), ("rb1", [128, 16]), ("rW2", [128, 16, 4]),
                 ("rb2", [1, 4]), ("We1", [128, 4, 8, 32, 128]), ("be1", [128, 4, 32]),
                 ("We2", [128, 4, 32, 1024]), ("be2", [1, 4, 1024]),
                 ("Wp", [128, 8, 4096]), ("bp", [1, 4096]), ("Wf", [128, 8, 3072]),
                 ("bf", [1, 3072])):
        dram(n, s, F32, "ExternalInput")
    for n, s in (("ident", [128, 128]), ("masks", [128, 4, 512]), ("tie", [1, 512]),
                 ("posT", [128, 8, 1024]),
                 ("L1c", [128, 130]), ("M2", [128, 65, 3, 128]), ("I1", [128, 2, 256]),
                 ("I2", [65, 128, 2, 64])):
        dram(n, s, BF16, "ExternalInput")
    for n, s in (("We1bf", [128, 4, 8, 32, 128]), ("We2bf", [128, 4, 32, 1024]),
                 ("W1bf", [128, 8, 128, 32]), ("W2bf", [32, 128, 1024]),
                 ("Wpbf", [128, 8, 4096]), ("Wfbf", [128, 8, 3072])):
        dram(n, s, BF16, "Internal")
    for n in ("xb1", "xb2", "xb3", "xb4"):
        dram(n, [TOK, D], F32, "ExternalOutput" if dbg else "Internal")
    dram("Rd", [128, 64, 1024], BF16, "Internal")
    dram("xag_in", [128, DCH, TOK], BF16, "Internal")
    dram("xag_out", [8 * 128, DCH, TOK], BF16, "Internal")
    dram("oa2a_in", [8, 128, 2, TOK], BF16, "Internal")
    dram("oa2a_out", [8, 128, 2, TOK], BF16, "Internal")
    dram("xpa2a_in", [8, 4, TOK, 128], BF16, "Internal")
    dram("xpa2a_out", [8, 4, TOK, 128], BF16, "Internal")
    dram("fa2a_in", [8, 3, 1024, 128], BF16, "Internal")
    dram("fa2a_out", [8, 3, 1024, 128], BF16, "Internal")
    dram("ya2a_in", [8, 128, TOK], BF16, "Internal")
    dram("ya2a_out", [8, 128, TOK], BF16, "Internal")
    if dbg:
        dram("d_y", [8, 128, TOK], BF16, "ExternalOutput")
        dram("d_xp", [8, 4, TOK, 128], BF16, "ExternalOutput")
        dram("d_f", [8, 3, 1024, 128], BF16, "ExternalOutput")
        dram("d_yin", [8, 128, TOK], BF16, "ExternalOutput")
        dram("d_xpi", [8, 4, TOK, 128], BF16, "ExternalOutput")
        dram("d_fi", [8, 3, 1024, 128], BF16, "ExternalOutput")

    ctx = ExitStack()
    E = Em(nc)
    ar = Arena(nc)
    P8 = [nc.alloc_psum_tensor(f"PSB{i}", [128, 512], F32) for i in range(8)]
    pid = nc.partition_id(mybir.ALL_ENGINES)
    fused = {"E": E, "ar": ar, "P8": P8, "io": io,
             "pb": pid // 4, "ph": pid % 4}
    G4 = [[0, 1, 2, 3], [4, 5, 6, 7]]
    G8 = [[0, 1, 2, 3, 4, 5, 6, 7]]
    ncc = [0]

    def coll(kind, in_name, out_name, groups):
        E.phase_barrier()
        ncc[0] += 1
        sem = f"cc{ncc[0]}"
        E.semnames.add(sem)

        def flat(t):
            n = len(t.shape)
            if n == 4:
                return t.ap().rearrange("a b c d -> (a b c) d")
            if n == 3:
                return t.ap().rearrange("a b c -> (a b) c")
            return t.ap()
        iap, oap = flat(io[in_name]), flat(io[out_name])

        def bld(e, _i=iap, _o=oap, _g=groups, _k=kind):
            from concourse.aot_env import allow_ici_all_to_all
            with allow_ici_all_to_all():
                return e.collective_compute(_k, ALU.bypass, replica_groups=_g,
                                            ins=[_i.opt()], outs=[_o.opt()])
        tok = E.emit("gpsimd", bld, waits=E.all_toks(), inc=sem, inc_by=1)
        if not hasattr(E, "_dmaprev"):
            E._dmaprev = {}
        E._dmaprev[sem] = tok
        E.phase_barrier()

    build_l0ffn(nc, fused=fused, xin_name="x_in0", xout_name="xb1")
    E.phase_barrier()
    build_attn_pre(nc, fused=fused, xin_name="xb1")
    coll("AllGather", "xag_in", "xag_out", G8)
    build_attn_core(nc, fused=fused)
    coll("AllToAll", "oa2a_in", "oa2a_out", G8)
    build_attn_proj(nc, fused=fused, xin_name="xb1", xout_name="xb2")
    E.phase_barrier()
    build_l0ffn(nc, fused=fused, use_pc=True, xin_name="xb2", xout_name="xb3")
    E.phase_barrier()
    build_l1ffn(nc, fused=fused, use_pc=True, xin_name="xb3", xout_name="xb4")
    E.phase_barrier()
    build_xp(nc, fused=fused, xin_name="xb4")
    if dbg:
        E.phase_barrier()
        E.dma("sync", io["d_xpi"][:, :, :, :], io["xpa2a_in"][:, :, :, :],
              waits=E.all_toks())
        E.dma("sync", io["d_fi"][:, :, :, :], io["fa2a_in"][:, :, :, :],
              waits=E.all_toks())
        E.phase_barrier()
    coll("AllToAll", "xpa2a_in", "xpa2a_out", G8)
    coll("AllToAll", "fa2a_in", "fa2a_out", G8)
    if dbg:
        E.dma("sync", io["d_xp"][:, :, :, :], io["xpa2a_out"][:, :, :, :],
              waits=E.all_toks())
        E.dma("sync", io["d_f"][:, :, :, :], io["fa2a_out"][:, :, :, :],
              waits=E.all_toks())
        E.phase_barrier()
    build_conv(nc, fused=fused)
    if dbg:
        E.phase_barrier()
        E.dma("sync", io["d_yin"][:, :, :], io["ya2a_in"][:, :, :],
              waits=E.all_toks())
        E.phase_barrier()
    coll("AllToAll", "ya2a_in", "ya2a_out", G8)
    if dbg:
        E.dma("sync", io["d_y"][:, :, :], io["ya2a_out"][:, :, :],
              waits=E.all_toks())
        E.phase_barrier()
    build_l1ffn(nc, fused=fused, with_y=True, use_pc=True, xin_name="xb4", xout_name="x_out")
    E.finalize(ctx)
    ctx.close()
    return nc


kernel_fused_dbg = [False]


def kernel_fused(inp):
    x = np.ascontiguousarray(inp["L0_attn_x"].reshape(16384, 1024), dtype=np.float32)
    H = {}
    H.update(host_prep_l0ffn(inp))
    H.update(host_prep_l0attn(inp))
    H.update(host_prep_l1ffn(inp))
    H.update(host_prep_conv(inp))
    nc = bass.Bass("TRN2", target_bir_lowering=False, debug=False)
    build_fused(nc, dbg=kernel_fused_dbg[0])
    names = ("Wc", "bc", "tie", "W1", "b1", "W2", "b2", "Wqkv", "bqkv", "masks",
             "Wr", "br", "rW1", "rb1", "rW2", "rb2", "We1", "be1", "We2", "be2",
             "Wp", "bp", "Wf", "bf", "L1c", "M2", "I1", "I2")
    in_maps = []
    for i in range(NCORE):
        m = {n: H[n] for n in names}
        m["ident"] = _IDENTB
        m["x_in0"] = np.ascontiguousarray(x[i * TOK:(i + 1) * TOK])
        m["posT"] = np.ascontiguousarray(H["posT"][:, :, 1024 * i:1024 * (i + 1)])
        in_maps.append(m)
    from concourse.aot_env import allow_ici_all_to_all
    with allow_ici_all_to_all():
        res = run_bass_kernel_spmd(nc, in_maps, core_ids=list(range(NCORE)))
    out = np.concatenate([np.asarray(r["x_out"]) for r in res.results], axis=0)
    if kernel_fused_dbg[0]:
        kernel_fused_dbg.append(res.results)
    return out.reshape(2, 8192, 1024)


def _shard_rows(a):
    return [np.ascontiguousarray(a[i * TOK:(i + 1) * TOK]) for i in range(NCORE)]


_IDENT = np.eye(128, dtype=np.float32)
_IDENTB = np.eye(128, dtype=ml_dtypes.bfloat16)


def _run(builder, in_maps, outs, **bkw):
    nc = bass.Bass("TRN2", target_bir_lowering=False, debug=False)
    builder(nc, **bkw)
    res = run_bass_kernel_spmd(nc, in_maps, core_ids=list(range(NCORE)))
    return [[np.asarray(r[o]) for o in outs] for r in res.results]


def kernel(**inp):
    inp = {k: np.asarray(v) for k, v in inp.items()}
    return kernel_fused(inp).astype(np.float32)


def kernel_multilaunch(**inp):
    inp = {k: np.asarray(v) for k, v in inp.items()}
    x = np.ascontiguousarray(inp["L0_attn_x"].reshape(16384, 1024), dtype=np.float32)
    H0f = host_prep_l0ffn(inp)
    H0a = host_prep_l0attn(inp)
    H1f = host_prep_l1ffn(inp)
    Hc = host_prep_conv(inp)

    def shards(a):
        return [np.ascontiguousarray(a[i * TOK:(i + 1) * TOK]) for i in range(NCORE)]

    def l0ffn(xcur):
        sh = shards(xcur)
        maps = [{"x_in": sh[i], "ident": _IDENT,
                 **{k: H0f[k] for k in ("Wc", "bc", "tie", "W1", "b1", "W2", "b2")}}
                for i in range(NCORE)]
        out = _run(build_l0ffn, maps, ["x_out"])
        return np.concatenate([o[0] for o in out], axis=0)

    def l0attn(xcur):
        sh = shards(xcur)
        out = _run(build_attn_pre, [{"x_in": sh[i], "ident": _IDENT} for i in range(NCORE)],
                   ["xhT_out"])
        xhT = [o[0] for o in out]
        xhT_full = [np.concatenate([xhT[4 * b + i] for i in range(4)], axis=2) for b in range(2)]
        maps = []
        for c in range(NCORE):
            b, hq = c // 4, c % 4
            maps.append({"xhT": np.ascontiguousarray(xhT_full[b][:, 2 * hq:2 * hq + 2, :]),
                         "Wqkv": H0a["Wqkv"], "bqkv": H0a["bqkv"], "masks": H0a["masks"],
                         "ident": _IDENT})
        out = _run(build_attn_core, maps, ["oT_out"])
        oT = [o[0] for o in out]
        oT_full = [np.concatenate([oT[4 * b + i] for i in range(4)], axis=1) for b in range(2)]
        maps = []
        for c in range(NCORE):
            b, tq = c // 4, c % 4
            maps.append({"x_in": sh[c],
                         "oT": np.ascontiguousarray(oT_full[b][:, :, tq * 2048:(tq + 1) * 2048]),
                         "Wr": H0a["Wr"], "br": H0a["br"]})
        out = _run(build_attn_proj, maps, ["x_out"])
        return np.concatenate([o[0] for o in out], axis=0)

    def l1ffn(xcur, y=None):
        sh = shards(xcur)
        ysh = shards(y) if y is not None else None
        maps = []
        for i in range(NCORE):
            m = {"x_in": sh[i], "ident": _IDENT,
                 **{k: H1f[k] for k in ("rW1", "rb1", "rW2", "rb2", "We1", "be1", "We2", "be2")}}
            if ysh is not None:
                m["y_in"] = ysh[i]
            maps.append(m)
        out = _run(build_l1ffn, maps, ["x_out"], with_y=(y is not None))
        return np.concatenate([o[0] for o in out], axis=0)

    def l1attn(xcur):
        sh = shards(xcur)
        maps = [{"x_in": sh[i], "Wp": Hc["Wp"], "bp": Hc["bp"], "Wf": Hc["Wf"],
                 "bf": Hc["bf"],
                 "posT": np.ascontiguousarray(Hc["posT"][:, :, 1024 * i:1024 * (i + 1)]),
                 "ident": _IDENT} for i in range(NCORE)]
        out = _run(build_xp, maps, ["xp_out", "f_out"])
        xp_full = np.concatenate([o[0] for o in out], axis=0)
        f_full = np.concatenate([o[1] for o in out], axis=0)
        uu = xp_full.reshape(2, 64, 128, 4, 1024).transpose(3, 0, 4, 1, 2)
        ff = f_full.reshape(64, 128, 3, 1024).transpose(2, 3, 0, 1)
        maps = [{"u_in": np.ascontiguousarray(uu[:, :, 128 * i:128 * (i + 1)]),
                 "f_in": np.ascontiguousarray(ff[:, 128 * i:128 * (i + 1)]),
                 "L1c": Hc["L1c"], "M2": Hc["M2"], "I1": Hc["I1"], "I2": Hc["I2"]}
                for i in range(NCORE)]
        out = _run(build_conv, maps, ["y_out"])
        y = np.stack([o[0] for o in out], axis=0)
        return np.ascontiguousarray(
            y.transpose(1, 3, 4, 0, 2).reshape(16384, 1024))

    x = l0ffn(x)
    x = l0attn(x)
    x = l0ffn(x)
    x = l1ffn(x)
    y = l1attn(x)
    x = l1ffn(x, y=y)
    return x.reshape(2, 8192, 1024)


# version 06
# speedup vs baseline: 1.0458x; 1.0142x over previous
import numpy as np
import ml_dtypes
from contextlib import ExitStack
from collections import defaultdict
import concourse.bass as bass
import concourse.mybir as mybir
from concourse.bass_utils import run_bass_kernel_spmd

AF = mybir.ActivationFunctionType
ALU = mybir.AluOpType
AX = mybir.AxisListType
F32, BF16 = mybir.dt.float32, mybir.dt.bfloat16

NCORE = 8
TOK = 2048
NT = TOK // 128
D = 1024
DCH = D // 128

ENGINES = ("tensor", "vector", "scalar", "gpsimd", "sync")


class Em:

    def __init__(self, nc):
        self.nc = nc
        self.q = {e: [] for e in ENGINES}
        self.cnt = defaultdict(int)
        self.lastw = {e: defaultdict(int) for e in ENGINES}
        self.own = {"T": "tensor", "V": "vector", "S": "scalar", "P": "gpsimd", "Y": "sync"}
        self.semnames = set(self.own)
        self.handles = {}

    def emit(self, eng, build, waits=(), inc=None, inc_by=1, strict=False):
        real = []
        for w in waits:
            if not w:
                continue
            s, v = w
            if self.own.get(s) == eng and not strict:
                continue
            if self.lastw[eng][s] < v:
                self.lastw[eng][s] = v
                real.append((s, v))
        tok = None
        if inc:
            self.semnames.add(inc)
            self.cnt[inc] += inc_by
            tok = (inc, self.cnt[inc])

        def fn(e, _real=tuple(real), _b=build, _inc=inc, _by=inc_by, _em=self):
            for (s, v) in _real:
                e.wait_ge(_em.handles[s], v)
            try:
                ins = _b(e)
            except Exception:
                import sys
                c = getattr(_b, "__code__", None)
                cl = getattr(_b, "__closure__", None)
                print("EM-OP FAILED:", getattr(_b, "_site", None) or
                      (f"{c.co_filename}:{c.co_firstlineno}" if c else "?"),
                      file=sys.stderr)
                if cl:
                    for cell in cl:
                        try:
                            print("  cell:", repr(cell.cell_contents)[:300], file=sys.stderr)
                        except Exception:
                            pass
                raise
            if _inc is not None:
                assert ins is not None, "inc on op that built nothing"
                ins.then_inc(_em.handles[_inc], _by)

        self.q[eng].append(fn)
        return tok

    def tensor_tok(self):
        return ("T", self.cnt["T"])

    DMA_POOL = 20

    def dma(self, eng, out, in_, waits=(), sem=None, **kw):
        if sem is None:
            tag = {"gpsimd": "dg", "sync": "dy", "scalar": "ds"}[eng]
            if not hasattr(self, "_dmarr"):
                self._dmarr = defaultdict(int)
                self._dmaprev = {}
            idx = self._dmarr[tag] % self.DMA_POOL
            self._dmarr[tag] += 1
            sem = f"{tag}{idx}"
            prev = self._dmaprev.get(sem)
            waits = list(waits) + ([prev] if prev else [])
        self.semnames.add(sem)
        import traceback as _tb
        _site = _tb.extract_stack()[-2]
        lam = lambda e: e.dma_start(out=out, in_=in_, **kw)
        lam._site = f"{_site.filename}:{_site.lineno}"
        tok = self.emit(eng, lam, waits=waits, inc=sem, inc_by=16)
        self._dmaprev[sem] = tok
        return tok

    def all_toks(self):
        toks = [(s, self.cnt[s]) for s in ("T", "V", "S", "P", "Y") if self.cnt[s]]
        if hasattr(self, "_dmaprev"):
            toks += list(self._dmaprev.values())
        return toks

    def phase_barrier(self):
        toks = self.all_toks()
        for eng in ENGINES:
            self.emit(eng, lambda e: None, waits=toks, strict=True)

    def finalize(self, ctx):
        for s in sorted(self.semnames):
            self.handles[s] = ctx.enter_context(self.nc.semaphore(s))
        block = ctx.enter_context(self.nc.Block())
        for engname in ENGINES:
            ops = self.q[engname]
            dec = getattr(block, engname)
            dec((lambda _ops: lambda e: [fn(e) for fn in _ops])(ops))


class Arena:
    _uid = [0]

    def __init__(self, nc, base=17408, top=229300):
        self.nc, self.base, self.top = nc, base, top
        self.off = base
        self.peak = base

    def reset(self):
        self.off = self.base

    def alloc(self, name, shape, dt):
        esz = {F32: 4, BF16: 2}[dt]
        nbytes = int(np.prod(shape[1:])) * esz
        nbytes = (nbytes + 31) // 32 * 32
        Arena._uid[0] += 1
        h = self.nc.alloc_sbuf_tensor_at(f"{name}_{Arena._uid[0]}", list(shape), dt,
                                         offset=self.off)
        self.off += nbytes
        self.peak = max(self.peak, self.off)
        assert self.off <= self.top, f"arena overflow at {name}: {self.off}"
        return h


def _dyn(ap_obj, extra):
    return bass.AP(ap_obj.tensor, ap_obj.offset + extra, ap_obj.ap)


def _phase_io(nc, fused, decls):
    if fused is None:
        return {n: nc.dram_tensor(n, s, d, kind=k) for (n, s, d, k) in decls}
    io = fused["io"]
    return {n: io[n] for (n, s, d, k) in decls}


def host_prep_l1ffn(inp):
    H = {}
    C = lambda a: np.ascontiguousarray(a, dtype=np.float32)
    H["ident"] = np.eye(128, dtype=np.float32)
    H["rW1"] = C(inp["L1_ffn_rW1"].reshape(8, 128, 16, 128).transpose(1, 0, 2, 3))
    H["rb1"] = C(np.asarray(inp["L1_ffn_rb1"]).reshape(16, 128).T)
    H["rW2"] = C(inp["L1_ffn_rW2"].reshape(16, 128, 4).transpose(1, 0, 2))
    H["rb2"] = C(np.asarray(inp["L1_ffn_rb2"]).reshape(1, 4))
    H["We1"] = C(inp["L1_ffn_We1"].reshape(4, 8, 128, 32, 128).transpose(2, 0, 1, 3, 4))
    H["be1"] = C(np.asarray(inp["L1_ffn_be1"]).reshape(4, 32, 128).transpose(2, 0, 1))
    H["We2"] = C(inp["L1_ffn_We2"].reshape(4, 32, 128, 1024).transpose(2, 0, 1, 3))
    H["be2"] = C(np.asarray(inp["L1_ffn_be2"]).reshape(1, 4, 1024))
    return H


def host_prep_l0ffn(inp):
    H = {}
    C = lambda a: np.ascontiguousarray(a, dtype=np.float32)
    Wc = np.asarray(inp["L0_ffn_Wc"]).reshape(1024, 128)
    H["Wc"] = C(Wc.reshape(8, 128, 128).transpose(1, 0, 2))
    H["bc"] = C(np.asarray(inp["L0_ffn_bc"]).reshape(1, 128))
    H["tie"] = np.tile(np.linspace(0.0, 1e-06, 32, dtype=np.float32), 16).reshape(1, 512).astype(ml_dtypes.bfloat16)
    W1 = np.asarray(inp["L0_ffn_W1"]).reshape(128, 1024, 32)
    H["W1"] = C(W1.transpose(1, 0, 2).reshape(8, 128, 128, 32).transpose(1, 0, 2, 3))
    H["b1"] = C(np.asarray(inp["L0_ffn_b1"]).reshape(128, 32).T)
    W2 = np.asarray(inp["L0_ffn_W2"]).reshape(128, 1024, 32)
    H["W2"] = C(W2.transpose(2, 0, 1))
    H["b2"] = C(np.asarray(inp["L0_ffn_b2"]).reshape(128, 1024))
    return H


def emit_ln(E, xtile, xhat_bf, st, ready=(), war=()):
    s1, s2, mean, var, rstd, t0, ex2, tln = (st[:, 0:1], st[:, 1:2], st[:, 2:3],
                                             st[:, 3:4], st[:, 4:5], st[:, 5:6],
                                             st[:, 6:7], st[:, 7:8])
    sq = E.ln_sq
    ta = E.emit("vector", lambda v: v.reduce_sum(s1, xtile, AX.X), waits=list(ready), inc="V")
    E.emit("vector", lambda v: v.tensor_tensor(sq, xtile, xtile, ALU.mult), inc="V")
    tb = E.emit("vector", lambda v: v.reduce_sum(s2, sq, AX.X), inc="V")
    tc = E.emit("vector", lambda v: v.tensor_scalar_mul(mean, s1, 1.0 / 1024.0),
                waits=[ta], inc="V", strict=True)
    td = E.emit("vector", lambda v: v.tensor_scalar_mul(ex2, s2, 1.0 / 1024.0),
                waits=[tb], inc="V", strict=True)
    te = E.emit("vector", lambda v: v.tensor_tensor(t0, mean, mean, ALU.mult),
                waits=[tc], inc="V", strict=True)
    t6 = E.emit("vector", lambda v: v.tensor_tensor(var, ex2, t0, ALU.subtract),
                waits=[td, te], inc="V", strict=True)
    cst = E.lnconsts
    E.emit("scalar", lambda s: s.activation(out=tln, in_=var, func=AF.Ln, bias=cst[:, 0:1]),
           waits=[t6], inc="S")
    t8 = E.emit("scalar", lambda s: s.activation(out=rstd, in_=tln, func=AF.Exp,
                                                 scale=cst[:, 1:2]), inc="S")
    t9 = E.emit("vector", lambda v: v.tensor_scalar(xhat_bf, xtile, mean, rstd,
                                                    ALU.subtract, ALU.mult),
                waits=[t8, tc] + list(war), inc="V", strict=True)
    return t9


def build_l1ffn(nc, dbg=False, with_y=False, fused=None, use_pc=False,
                xin_name="x_in", xout_name="x_out"):
    dio = _phase_io(nc, fused, [
        (xin_name, [TOK, D], F32, "ExternalInput"),
        (xout_name, [TOK, D], F32, "ExternalOutput"),
        ("rW1", [128, 8, 16, 128], F32, "ExternalInput"),
        ("rb1", [128, 16], F32, "ExternalInput"),
        ("rW2", [128, 16, 4], F32, "ExternalInput"),
        ("rb2", [1, 4], F32, "ExternalInput"),
        ("We1", [128, 4, 8, 32, 128], F32, "ExternalInput"),
        ("be1", [128, 4, 32], F32, "ExternalInput"),
        ("We2", [128, 4, 32, 1024], F32, "ExternalInput"),
        ("be2", [1, 4, 1024], F32, "ExternalInput"),
        ("ident", [128, 128], F32, "ExternalInput"),
    ] + ([("y_in", [TOK, D], BF16, "ExternalInput")] if (with_y and fused is None) else [])
      + ([("ya2a_out", [8, 128, TOK], BF16, "Internal")] if (with_y and fused is not None) else []))
    x_in, x_out = dio[xin_name], dio[xout_name]
    w_rW1, w_rb1, w_rW2, w_rb2 = dio["rW1"], dio["rb1"], dio["rW2"], dio["rb2"]
    w_We1, w_be1, w_We2, w_be2 = dio["We1"], dio["be1"], dio["We2"], dio["be2"]
    w_ident = dio["ident"]
    if use_pc and fused is not None:
        w_We1, w_We2 = fused["io"]["We1bf"], fused["io"]["We2bf"]
    we1_eng = "sync" if (use_pc and fused is not None) else "gpsimd"
    we2_eng = "scalar" if (use_pc and fused is not None) else "gpsimd"
    if with_y:
        if fused is None:
            y_in = dio["y_in"]
        else:
            ya2a_out = dio["ya2a_out"]
    if dbg:
        d_xhatT = nc.dram_tensor("d_xhatT", [128, DCH, TOK], F32, kind="ExternalOutput")
        d_cw = nc.dram_tensor("d_cw", [128, NT, 4], F32, kind="ExternalOutput")
        d_hT = nc.dram_tensor("d_hT", [128, 32, 512], F32, kind="ExternalOutput")
        d_rhT = nc.dram_tensor("d_rhT", [128, 32, 512], F32, kind="ExternalOutput")
        d_ln = nc.dram_tensor("d_ln", [128, NT, 8], F32, kind="ExternalOutput")

    if fused is None:
        ctx = ExitStack()
        sb = lambda name, shape, dt: ctx.enter_context(nc.sbuf_tensor(name, shape, dt))
    else:
        fused["ar"].reset()
        sb = fused["ar"].alloc

    xt = sb("xt", [128, D], F32)
    xhat = sb("xhat", [128, D], BF16)
    xhatT = sb("xhatT", [128, DCH, TOK], BF16)
    hT = sb("hT", [128, 32, 512], BF16)
    We1_sb = sb("We1_sb", [128, 8, 32, 128], BF16)
    We2_sb = sb("We2_sb", [128, 32, 1024], BF16)
    rW2_sb = sb("rW2_sb", [128, 16, 4], BF16)
    rb1_sb = sb("rb1_sb", [128, 16], F32)
    rb2_sb = sb("rb2_sb", [1, 4], BF16)
    be1_sb = sb("be1_sb", [128, 4, 32], F32)
    be2_sb = sb("be2_sb", [1, 1024], BF16)
    ident_sb = sb("ident_sb", [128, 128], BF16)
    ones_sb = sb("ones_sb", [1, 128], BF16)
    cw = sb("cw", [128, NT, 4], F32)
    lnst = sb("lnst", [128, NT, 8], F32)
    r4 = sb("r4", [128, 12], F32)
    r44 = sb("r44", [128, 16], F32)
    stage = sb("stage", [128, 2, 512], F32)

    if fused is None:
        ps_tr = [ctx.enter_context(nc.psum_tensor(f"ps_tr{i}", [128, 128], BF16))[:]
                 for i in range(2)]
        ps_h = [ctx.enter_context(nc.psum_tensor(f"ps_h{i}", [128, 512], F32))[:]
                for i in range(2)]
        ps_y = [ctx.enter_context(nc.psum_tensor(f"ps_y{i}", [128, 512], F32))[:]
                for i in range(2)]
        ps_l = ctx.enter_context(nc.psum_tensor("ps_l", [128, 512], F32))[:]
        ps_l4 = ctx.enter_context(nc.psum_tensor("ps_l4", [128, 4], F32))[:]
        E = Em(nc)
    else:
        P8 = fused["P8"]
        ps_tr = [P8[6][:].bitcast(BF16)[:, 0:128], P8[7][:].bitcast(BF16)[:, 0:128]]
        ps_h = [P8[0][:], P8[1][:]]
        ps_y = [P8[2][:], P8[3][:]]
        ps_l = P8[4][:]
        ps_l4 = P8[5][:, 0:4]
        E = fused["E"]

    t_cp = None if with_y else E.dma("scalar", x_out[:, :], x_in[:, :])
    t_id = E.dma("gpsimd", ident_sb[:], w_ident[:, :])
    t_rb1 = E.dma("gpsimd", rb1_sb[:], w_rb1[:, :])
    t_rW2 = E.dma("gpsimd", rW2_sb[:], w_rW2[:, :, :])
    t_rb2 = E.dma("gpsimd", rb2_sb[:], w_rb2[:, :])
    t_be1 = E.dma("gpsimd", be1_sb[:], w_be1[:, :, :])
    t_ones = E.emit("vector", lambda v: v.memset(ones_sb[:], 1.0), inc="V")
    lncst = sb("lncst", [128, 2], F32)
    E.emit("vector", lambda v: v.memset(lncst[:, 0:1], 1e-5), inc="V")
    E.emit("vector", lambda v: v.memset(lncst[:, 1:2], -0.5), inc="V")
    E.lnconsts = lncst
    E.ln_sq = stage[:].rearrange("p a b -> p (a b)")

    ln_done = {}
    tr_free = {}
    xload = {}
    basew = {}
    for ti in range(NT):
        xload[ti] = E.dma("sync", xt[:], x_in[ti * 128:(ti + 1) * 128, :],
                          waits=[ln_done.get(ti - 1), basew.get(ti - 1)])
        rdy = [xload[ti]]
        if with_y:
            yt = hT[:].rearrange("p a b -> p (a b)")[:, 0:D]
            if fused is None:
                yl = E.dma("scalar", yt[:], y_in[ti * 128:(ti + 1) * 128, :],
                           waits=[ln_done.get(ti - 1)])
            else:
                ycol = hT[:].rearrange("p a b -> p (a b)")[:, D:D + 256]
                for ch in range(DCH):
                    sl = ch % 2
                    lc = E.dma("scalar", ycol[:, sl * 128:(sl + 1) * 128],
                               ya2a_out[ch, :, ti * 128:(ti + 1) * 128],
                               waits=[tr_free.get(("y", ti, ch - 2)),
                                      ln_done.get(ti - 1)])
                    tt = E.emit("tensor",
                                (lambda t, _s=ycol[:, sl * 128:(sl + 1) * 128], _p=ps_tr[sl]:
                                 t.transpose(_p, _s, ident_sb[:])),
                                waits=[lc, t_id, tr_free.get((ti - 1, (sl) + 6)),
                                       tr_free.get(("y", ti, ch - 2))], inc="T")
                    yev = E.emit("vector",
                                 (lambda v, _p=ps_tr[sl], _d=yt[:, ch * 128:(ch + 1) * 128]:
                                  v.tensor_copy(_d, _p)),
                                 waits=[tt], inc="V")
                    tr_free[("y", ti, ch)] = yev
                tr_free[(ti - 1, 6)] = tr_free[("y", ti, DCH - 2)]
                tr_free[(ti - 1, 7)] = tr_free[("y", ti, DCH - 1)]
                yl = tr_free[("y", ti, DCH - 1)]
            ad = E.emit("vector", lambda v: v.tensor_tensor(xt[:], xt[:], yt[:], ALU.add),
                        waits=[xload[ti], yl], inc="V")
            basew[ti] = E.dma("sync", x_out[ti * 128:(ti + 1) * 128, :], xt[:], waits=[ad])
            rdy = [ad, basew[ti]]
        t9 = emit_ln(E, xt[:], xhat[:], lnst[:, ti, :], ready=rdy,
                     war=[tr_free.get((ti - 1, DCH - 1))])
        ln_done[ti] = t9
        for ch in range(DCH):
            pst = ps_tr[ch % 2]
            tt = E.emit("tensor",
                        (lambda t, _s=xhat[:, ch * 128:(ch + 1) * 128], _p=pst:
                         t.transpose(_p, _s, ident_sb[:])),
                        waits=[t9, t_id, tr_free.get((ti - 1, (ch % 2) + 6)),
                               tr_free.get((ti, ch - 2))], inc="T")
            ev = E.emit("vector",
                        (lambda v, _p=pst, _d=xhatT[:, ch, ti * 128:(ti + 1) * 128]:
                         v.tensor_copy(_d, _p)),
                        waits=[tt], inc="V")
            tr_free[(ti, ch)] = ev
    xT_all = ev
    if dbg:
        E.dma("gpsimd", d_xhatT[:, :, :], xhatT[:, :, :], waits=[xT_all])
        E.dma("gpsimd", d_ln[:, :, :], lnst[:, :, :], waits=[xT_all])

    NBLK = TOK // 512
    cw_tok = [None] * NT
    rhT = hT[:].rearrange("p a b -> p (a b)")
    ps_l_free = None
    ps_l4_free = None
    rw1_free = {}
    rhT_war = ("V", E.cnt["V"]) if with_y else None
    for blk in range(NBLK):
        t0c = blk * 512
        rh_last = None
        for hch in range(16):
            slot = hch % 2
            ld = E.dma("gpsimd", We1_sb[:, :, slot, :], w_rW1[:, :, hch, :],
                       waits=[rw1_free.get(slot)])
            tt = None
            for dch in range(DCH):
                tt = E.emit("tensor",
                            (lambda t, _l=We1_sb[:, dch, slot, :], _r=xhatT[:, dch, t0c:t0c + 512],
                             _st=(dch == 0), _sp=(dch == DCH - 1):
                             t.matmul(ps_l, _l, _r, start=_st, stop=_sp)),
                            waits=[ld, xT_all, ps_l_free], inc="T")
            rw1_free[slot] = tt
            ev = E.emit("scalar",
                        (lambda s, _o=rhT[:, hch * 512:(hch + 1) * 512], _b=rb1_sb[:, hch:hch + 1]:
                         s.activation(out=_o, in_=ps_l, func=AF.Gelu, bias=_b)),
                        waits=[tt, t_rb1, rhT_war, getattr(E, "_dbg_rhT_dump", None)], inc="S")
            ps_l_free = ev
            rh_last = ev
        for sub in range(4):
            ti = blk * 4 + sub
            for hch in range(16):
                E.emit("tensor",
                       (lambda t, _l=rhT[:, hch * 512 + sub * 128: hch * 512 + sub * 128 + 128],
                        _r=rW2_sb[:, hch, :], _st=(hch == 0):
                        t.matmul(ps_l4, _l, _r, start=_st, stop=False)),
                       waits=[rh_last, t_rW2, ps_l4_free], inc="T")
            tb = E.emit("tensor",
                        lambda t: t.matmul(ps_l4, ones_sb[0:1, :], rb2_sb[0:1, :],
                                           start=False, stop=True),
                        waits=[t_rb2, t_ones], inc="T")
            p4, m1, s4, rcp = r4[:, 0:4], r4[:, 4:5], r4[:, 5:6], r4[:, 6:7]
            cnt = r4[:, 8:12]
            cwt = cw[:, ti, :]
            c1 = E.emit("vector", lambda v: v.reduce_max(m1, ps_l4, AX.X), waits=[tb], inc="V")
            c2 = E.emit("vector", lambda v: v.tensor_scalar_sub(p4, ps_l4, m1),
                        waits=[c1], inc="V", strict=True)
            ps_l4_free = c2
            e3 = E.emit("vector", lambda v: v.tensor_copy(r44[:, 0:4], p4),
                        waits=[c2], inc="V", strict=True)
            e3b = E.emit("scalar", lambda s: s.activation(out=p4, in_=r44[:, 0:4], func=AF.Exp),
                         waits=[e3], inc="S")
            c4 = E.emit("vector", lambda v: v.reduce_sum(s4, p4, AX.X), waits=[e3b], inc="V")
            c5 = E.emit("vector", lambda v: v.reciprocal(rcp, s4),
                        waits=[c4], inc="V", strict=True)
            c6 = E.emit("vector", lambda v: v.tensor_scalar_mul(p4, p4, rcp),
                        waits=[c5], inc="V", strict=True)
            pa = p4.unsqueeze(1).broadcast_to([128, 4, 4])
            pb = p4.unsqueeze(2).broadcast_to([128, 4, 4])
            c7 = E.emit("vector", lambda v: v.tensor_tensor(
                r44[:].rearrange("p (a b) -> p a b", b=4), pa, pb, ALU.is_gt),
                waits=[c6], inc="V", strict=True)
            c8 = E.emit("vector", lambda v: v.reduce_sum(
                cnt, r44[:].rearrange("p (a b) -> p a b", b=4), AX.X),
                waits=[c7], inc="V", strict=True)
            c9 = E.emit("vector", lambda v: v.tensor_single_scalar(cnt, cnt, 1.5, ALU.is_lt),
                        waits=[c8], inc="V", strict=True)
            c10 = E.emit("vector", (lambda v, _c=cwt: v.tensor_tensor(_c, p4, cnt, ALU.mult)),
                         waits=[c9], inc="V", strict=True)
            cw_tok[ti] = E.emit("vector", (lambda v, _c=cwt: v.tensor_scalar_mul(_c, _c, 0.5)),
                                waits=[c10], inc="V", strict=True)
        rhT_war = E.tensor_tok()
        if dbg and blk == 0:
            dmp = E.dma("gpsimd", d_rhT[:, :, :], hT[:, :, :], waits=[rh_last])
            E._dbg_rhT_dump = dmp

    acc_dma = [None] * NT
    stage_free = {}
    ps_h_free = {}
    ps_y_free = {}
    nstage = 0
    router_T = E.tensor_tok()
    if dbg:
        E.dma("gpsimd", d_cw[:, :, :], cw[:, :, :], waits=[cw_tok[NT - 1]])
    we1_free = router_T
    we2_free = None
    be2_free = None
    hT_war = rhT_war
    for e in range(4):
        l1 = E.dma(we1_eng, We1_sb[:], w_We1[:, e, :, :, :], waits=[we1_free])
        l2 = E.dma(we2_eng, We2_sb[:], w_We2[:, e, :, :], waits=[we2_free])
        t_be2 = E.dma("gpsimd", be2_sb[:], w_be2[0:1, e, :], waits=[be2_free])
        for blk in range(NBLK):
            t0c = blk * 512
            h_last = None
            for hch in range(32):
                bank = ps_h[hch % 2]
                tt = None
                for dch in range(DCH):
                    tt = E.emit("tensor",
                                (lambda t, _l=We1_sb[:, dch, hch, :], _r=xhatT[:, dch, t0c:t0c + 512],
                                 _o=bank, _st=(dch == 0), _sp=(dch == DCH - 1):
                                 t.matmul(_o, _l, _r, start=_st, stop=_sp)),
                                waits=[l1, xT_all, ps_h_free.get(hch % 2)], inc="T")
                ev = E.emit("scalar",
                            (lambda s, _o=hT[:, hch, :], _b=be1_sb[:, e, hch:hch + 1], _p=bank:
                             s.activation(out=_o, in_=_p, func=AF.Gelu, bias=_b)),
                            waits=[tt, t_be1, hT_war, getattr(E, "_dbg_hT_dump", None)], inc="S")
                ps_h_free[hch % 2] = ev
                h_last = ev
            if dbg and e == 0 and blk == 0:
                dmp = E.dma("gpsimd", d_hT[:, :, :], hT[:, :, :], waits=[h_last])
                E._dbg_hT_dump = dmp
            if blk == NBLK - 1:
                we1_free = E.tensor_tok()
            for sub in range(4):
                ti = blk * 4 + sub
                for half in range(2):
                    bank = ps_y[half]
                    for hch in range(32):
                        E.emit("tensor",
                               (lambda t, _l=hT[:, hch, sub * 128:(sub + 1) * 128],
                                _r=We2_sb[:, hch, half * 512:(half + 1) * 512],
                                _o=bank, _st=(hch == 0):
                                t.matmul(_o, _l, _r, start=_st, stop=False)),
                               waits=[l2, h_last, ps_y_free.get(half)], inc="T")
                    tb = E.emit("tensor",
                                (lambda t, _o=bank, _r=be2_sb[0:1, half * 512:(half + 1) * 512]:
                                 t.matmul(_o, ones_sb[0:1, :], _r, start=False, stop=True)),
                                waits=[t_be2], inc="T")
                    sslot = nstage % 2
                    nstage += 1
                    evt = E.emit("vector",
                                 (lambda v, _p=bank, _s=stage[:, sslot, :], _c=cw[:, ti, e:e + 1]:
                                  v.tensor_scalar_mul(_s, _p, _c)),
                                 waits=[tb, cw_tok[ti], stage_free.get(sslot)], inc="V")
                    ps_y_free[half] = evt
                    dmat = E.dma("gpsimd",
                                 x_out[ti * 128:(ti + 1) * 128, half * 512:(half + 1) * 512],
                                 stage[:, sslot, :],
                                 waits=[evt, t_cp if not with_y else basew[ti],
                                        acc_dma[ti]], accum_op=ALU.add)
                    acc_dma[ti] = dmat
                    stage_free[sslot] = dmat
            hT_war = E.tensor_tok()
        we2_free = E.tensor_tok()
        be2_free = we2_free

    E.emit("sync", lambda e: None, waits=[t for t in acc_dma] + ([t_cp] if t_cp else []))
    if fused is None:
        E.finalize(ctx)
        ctx.close()
    return nc


def build_l0ffn(nc, dbg=False, fused=None, use_pc=False, xin_name="x_in", xout_name="x_out"):
    dio = _phase_io(nc, fused, [
        (xin_name, [TOK, D], F32, "ExternalInput"),
        (xout_name, [TOK, D], F32, "ExternalOutput"),
        ("Wc", [128, 8, 128], F32, "ExternalInput"),
        ("bc", [1, 128], F32, "ExternalInput"),
        ("tie", [1, 512], F32, "ExternalInput"),
        ("W1", [128, 8, 128, 32], F32, "ExternalInput"),
        ("b1", [32, 128], F32, "ExternalInput"),
        ("W2", [32, 128, 1024], F32, "ExternalInput"),
        ("b2", [128, 1024], F32, "ExternalInput"),
        ("ident", [128, 128], F32, "ExternalInput"),
        ("Rd", [128, 64, 1024], BF16, "Internal"),
    ])
    x_in, x_out = dio[xin_name], dio[xout_name]
    w_Wc, w_bc, w_tie, w_ident = dio["Wc"], dio["bc"], dio["tie"], dio["ident"]
    w_W1, w_b1, w_W2, w_b2, Rd = dio["W1"], dio["b1"], dio["W2"], dio["b2"], dio["Rd"]
    if use_pc and fused is not None:
        w_W1, w_W2 = fused["io"]["W1bf"], fused["io"]["W2bf"]
    w1_eng = "sync" if (use_pc and fused is not None) else "gpsimd"
    w2_eng = "scalar" if (use_pc and fused is not None) else "gpsimd"
    if dbg:
        d_lg = nc.dram_tensor("d_lg", [128, TOK], F32, kind="ExternalOutput")
        d_pt = nc.dram_tensor("d_pt", [128, TOK], F32, kind="ExternalOutput")
        d_xt = nc.dram_tensor("d_xt", [128, 8, 64, 64], F32, kind="ExternalOutput")

    if fused is None:
        ctx = ExitStack()
        sb = lambda name, shape, dt: ctx.enter_context(nc.sbuf_tensor(name, shape, dt))
    else:
        fused["ar"].reset()
        sb = fused["ar"].alloc

    xt = sb("xt", [128, D], F32)
    xhat = sb("xhat", [128, NT, D], BF16)
    xhatT = sb("xhatT", [128, DCH, TOK], BF16)
    XT = sb("XT", [128, 8, 64, 64], BF16)
    W1_sb = sb("W1_sb", [128, 8, 64, 32], BF16)
    W2_sb = sb("W2_sb", [32, 2, 1024], BF16)
    lg_sb = sb("lg_sb", [128, 512], F32)
    pt_sb = sb("pt_sb", [128, TOK], BF16)
    pbig = sb("pbig", [128, NT, 512], BF16)
    gmax = sb("gmax", [128, 64], F32)
    inner = sb("inner", [32, 2, 64], BF16)
    Wc_sb = sb("Wc_sb", [128, 8, 128], BF16)
    bc_sb = sb("bc_sb", [1, 128], BF16)
    tie_sb = sb("tie_sb", [1, 512], BF16)
    b1_sb = sb("b1_sb", [32, 128], F32)
    b2_sb = sb("b2_sb", [128, 1024], BF16)
    ident_sb = sb("ident_sb", [128, 128], BF16)
    ones_sb = sb("ones_sb", [1, 512], BF16)
    lnst = sb("lnst", [128, NT, 8], F32)
    lncst = sb("lncst", [128, 2], F32)
    rstage = sb("rstage", [64, 2, 512], BF16)
    rsb = sb("rsb", [128, 1024], BF16)

    if fused is None:
        psf = [ctx.enter_context(nc.psum_tensor(f"psf{i}", [128, 512], F32))[:]
               for i in range(6)]
        ps_bf = ctx.enter_context(nc.psum_tensor("ps_bf", [128, 512], BF16))[:]
        ps_tr = ctx.enter_context(nc.psum_tensor("ps_tr", [128, 128], BF16))[:]
        E = Em(nc)
    else:
        P8 = fused["P8"]
        psf = [P8[i][:] for i in range(6)]
        ps_bf = P8[6][:].bitcast(BF16)[:, 0:512]
        ps_tr = P8[7][:].bitcast(BF16)[:, 0:128]
        E = fused["E"]
    E.lnconsts = lncst
    E.ln_sq = XT[:].rearrange("p a b c -> p (a b c)")[:, 0:2048].bitcast(F32)

    t_id = E.dma("gpsimd", ident_sb[:], w_ident[:, :])
    t_wc = E.dma("gpsimd", Wc_sb[:], w_Wc[:, :, :])
    t_bc = E.dma("gpsimd", bc_sb[:], w_bc[:, :])
    t_tie = E.dma("gpsimd", tie_sb[:], w_tie[:, :])
    t_b1 = E.dma("gpsimd", b1_sb[:], w_b1[:, :])
    t_b2 = E.dma("gpsimd", b2_sb[:], w_b2[:, :])
    E.emit("vector", lambda v: v.memset(ones_sb[:], 1.0), inc="V")
    E.emit("vector", lambda v: v.memset(lncst[:, 0:1], 1e-5), inc="V")
    E.emit("vector", lambda v: v.memset(lncst[:, 1:2], -0.5), inc="V")

    TB4 = [ps_tr] if fused is None else [psf[i].bitcast(BF16)[:, 0:128] for i in (2, 3, 4, 5)]
    NB4 = len(TB4)
    ln_done = {}
    tr_free = {}
    for ti in range(NT):
        xl = E.dma("sync", xt[:], x_in[ti * 128:(ti + 1) * 128, :], waits=[ln_done.get(ti - 1)])
        t9 = emit_ln(E, xt[:], xhat[:, ti, :], lnst[:, ti, :], ready=[xl], war=[])
        ln_done[ti] = t9
        for ch in range(DCH):
            k = ti * DCH + ch
            bank = TB4[k % NB4]
            tt = E.emit("tensor",
                        (lambda t, _s=xhat[:, ti, ch * 128:(ch + 1) * 128], _o=bank:
                         t.transpose(_o, _s, ident_sb[:])),
                        waits=[t9, t_id, tr_free.get(k - NB4)], inc="T")
            ev = E.emit("vector",
                        (lambda v, _d=xhatT[:, ch, ti * 128:(ti + 1) * 128], _o=bank:
                         v.tensor_copy(_d, _o)),
                        waits=[tt], inc="V")
            tr_free[k] = ev
    xT_all = ev

    NBLK = TOK // 512
    lg_evac = {}
    route_done = []
    for blk in range(NBLK):
        t0c = blk * 512
        bank = psf[blk % 2]
        for dch in range(DCH):
            E.emit("tensor",
                   (lambda t, _l=Wc_sb[:, dch, :], _r=xhatT[:, dch, t0c:t0c + 512],
                    _o=bank, _st=(dch == 0):
                    t.matmul(_o, _l, _r, start=_st, stop=False)),
                   waits=[t_wc, xT_all, lg_evac.get(blk % 2)], inc="T")
        E.emit("tensor", (lambda t, _o=bank: t.matmul(_o, bc_sb[0:1, :], ones_sb[0:1, 0:512],
                                                      start=False, stop=False)),
               waits=[t_bc], inc="T")
        tt = E.emit("tensor", (lambda t, _o=bank: t.matmul(_o, ones_sb[0:1, 0:128], tie_sb[0:1, :],
                                                           start=False, stop=True)),
                    waits=[t_tie], inc="T")
        ev = E.emit("vector", (lambda v, _o=bank, _d=lg_sb[:, :]:
                               v.tensor_copy(_d, _o)), waits=[tt, lg_evac.get("route")],
                    inc="V")
        lg_evac[blk % 2] = ev
        lgv = lg_sb[:, :].rearrange("p (g w) -> p g w", w=32)
        g16 = gmax[:, blk * 16:(blk + 1) * 16]
        r1 = E.emit("vector", (lambda v, _l=lgv, _g=g16: v.tensor_reduce(_g, _l, AX.X, ALU.max)),
                    waits=[ev], inc="V", strict=True)
        gb = g16.unsqueeze(2).broadcast_to([128, 16, 32])
        r2 = E.emit("vector", (lambda v, _l=lgv, _g=gb,
                               _d=pt_sb[:, t0c:t0c + 512].rearrange("p (g w) -> p g w", w=32):
                               v.tensor_tensor(_d, _l, _g, ALU.is_equal)),
                    waits=[r1], inc="V", strict=True)
        lg_evac["route"] = r2
        if dbg:
            lg_evac["route"] = E.dma("gpsimd", d_lg[:, t0c:t0c + 512], lg_sb[:, :], waits=[r2])
        route_done.append(r2)
    logits_V = ("V", E.cnt["V"])

    pb_toks = []
    for ti in range(NT):
        mz = E.emit("vector", lambda v: v.memset(ps_bf.bitcast(F32), 0.0),
                    waits=[route_done[ti // 4], pb_toks[-1] if pb_toks else None],
                    inc="V", strict=True)
        trs = []
        for j in range(4):
            tt = E.emit("tensor",
                        (lambda t, _i=pt_sb[:, ti * 128 + 32 * j: ti * 128 + 32 * j + 32],
                         _o=ps_bf[32 * j:32 * j + 32, j * 128:(j + 1) * 128], _j=j:
                         t.transpose(_o, _i, ident_sb[:], tile_position=(0, 32 * _j))),
                        waits=[mz, route_done[-1] if ti == 0 else None], inc="T")
            trs.append(tt)
        ev = E.emit("vector", (lambda v, _d=pbig[:, ti, :]: v.tensor_copy(_d, ps_bf)),
                    waits=trs, inc="V")
        pb_toks.append(ev)
    pbig_all = pb_toks[-1]

    w1_free = None
    gather_ps_free = {}
    prev_phase_acts = None
    w1_bank_free = {}
    w2_bank_free = {}
    w2_slot_free = {}
    inner_slot_free = {}
    rstage_free = {}
    rd_done = []
    for eb in range(2):
        l_w1 = E.dma(w1_eng, W1_sb[:], w_W1[:, :, eb * 64:(eb + 1) * 64, :], waits=[w1_free])
        gather_evacs = []
        for ti in range(NT):
            for ch in range(DCH):
                bi = 2 + (ch % 2)
                bank = psf[bi]
                rhs = pbig[:, ti, :].rearrange("p (g e) -> p g e", e=128)[:, :, eb * 64:(eb + 1) * 64]
                tt = E.emit("tensor",
                            (lambda t, _l=xhat[:, ti, ch * 128:(ch + 1) * 128], _r=rhs,
                             _o=bank[:, 0:256]:
                             t.matmul(_o, _l, _r, start=True, stop=True)),
                            waits=[pbig_all, gather_ps_free.get(bi), prev_phase_acts], inc="T")
                dst = XT[:, ch, :, :].rearrange("p e g -> p g e")[:, ti * 4:ti * 4 + 4, :]
                eng = "vector" if (ti + ch) % 2 == 0 else "scalar"
                xw = w1_free if eb == 1 else None
                if eng == "vector":
                    ev = E.emit("vector",
                                (lambda v, _s=bank[:, 0:256].rearrange("p (g e) -> p g e", e=64),
                                 _d=dst: v.tensor_copy(_d, _s)), waits=[tt, xw], inc="V")
                else:
                    ev = E.emit("scalar",
                                (lambda s, _s=bank[:, 0:256].rearrange("p (g e) -> p g e", e=64),
                                 _d=dst: s.copy(_d, _s)), waits=[tt, xw], inc="S")
                gather_ps_free[bi] = ev
                gather_evacs.append(ev)
        gV, gS = ("V", E.cnt["V"]), ("S", E.cnt["S"])
        rd_last = None
        for ei in range(64):
            es = eb * 64 + ei
            slot = ei % 2
            lw2 = E.dma(w2_eng, W2_sb[:, slot, :], w_W2[:, es, :], waits=[w2_slot_free.get(slot)])
            w1b = psf[slot][0:32, 0:64]
            tt = None
            for ch in range(DCH):
                tt = E.emit("tensor",
                            (lambda t, _l=W1_sb[:, ch, ei, :], _r=XT[:, ch, ei, :],
                             _o=w1b, _st=(ch == 0), _sp=(ch == DCH - 1):
                             t.matmul(_o, _l, _r, start=_st, stop=_sp)),
                            waits=[l_w1, gV, gS, logits_V, w1_bank_free.get(slot)], inc="T")
            iv = E.emit("scalar",
                        (lambda s, _o=inner[:, slot, :], _b=b1_sb[:, es:es + 1], _p=w1b:
                         s.activation(out=_o, in_=_p, func=AF.Relu, bias=_b)),
                        waits=[tt, t_b1, inner_slot_free.get(slot)], inc="S")
            w1_bank_free[slot] = iv
            for half in range(2):
                yb = psf[4 + half][0:64, :]
                tt2 = E.emit("tensor",
                             (lambda t, _l=inner[:, slot, :],
                              _r=W2_sb[:, slot, half * 512:(half + 1) * 512], _o=yb:
                              t.matmul(_o, _l, _r, start=True, stop=True)),
                             waits=[iv, lw2, w2_bank_free.get(half)], inc="T")
                sev = E.emit("vector",
                             (lambda v, _s=rstage[:, half, :], _p=yb: v.tensor_copy(_s, _p)),
                             waits=[tt2, rstage_free.get(half)], inc="V")
                w2_bank_free[half] = sev
                dm = E.dma("sync", Rd[es, :, half * 512:(half + 1) * 512], rstage[:, half, :],
                           waits=[sev])
                rstage_free[half] = dm
                rd_last = dm
            inner_slot_free[slot] = E.tensor_tok()
            w2_slot_free[slot] = E.tensor_tok()
        w1_free = E.tensor_tok()
        prev_phase_acts = ("S", E.cnt["S"])
        rd_done.append(rd_last)

    sc_ps_free = {}
    rsb_free = None
    xt_free2 = ln_done[NT - 1]
    w1_acts = ("S", E.cnt["S"])
    out_done = []
    ost = lg_sb[:, :]
    for ti in range(NT):
        xl = E.dma("scalar", xt[:], x_in[ti * 128:(ti + 1) * 128, :], waits=[xt_free2])
        mms = []
        for j in range(4):
            lr = E.dma("sync", rsb[:, :], Rd[:, ti * 4 + j, :],
                       waits=rd_done + ([rsb_free] if rsb_free else []))
            for half in range(2):
                E.emit("tensor",
                       (lambda t, _l=pt_sb[:, (ti * 4 + j) * 32:(ti * 4 + j) * 32 + 32],
                        _r=rsb[:, half * 512:(half + 1) * 512],
                        _o=psf[2 + half][32 * j:32 * j + 32, :], _j=j:
                        t.matmul(_o, _l, _r, start=True, stop=False,
                                 tile_position=(0, 32 * _j))),
                       waits=[lr, route_done[-1], sc_ps_free.get(half), w1_acts], inc="T")
                tt = E.emit("tensor",
                            (lambda t, _l=pt_sb[:, (ti * 4 + j) * 32:(ti * 4 + j) * 32 + 32],
                             _r=b2_sb[:, half * 512:(half + 1) * 512],
                             _o=psf[2 + half][32 * j:32 * j + 32, :], _j=j:
                             t.matmul(_o, _l, _r, start=False, stop=True,
                                      tile_position=(0, 32 * _j))),
                            waits=[t_b2], inc="T")
                mms.append(tt)
            rsb_free = E.tensor_tok()
        prevh = sc_ps_free.get("ost")
        for half in range(2):
            e1 = E.emit("vector",
                        (lambda v, _p=psf[2 + half]: v.tensor_scalar_mul(ost, _p[:], 0.5)),
                        waits=mms + ([prevh] if prevh else []), inc="V", strict=True)
            e3 = E.emit("vector",
                        (lambda v, _x=xt[:, half * 512:(half + 1) * 512]:
                         v.tensor_tensor(ost, ost, _x, ALU.add)),
                        waits=[e1, xl], inc="V", strict=True)
            sc_ps_free[half] = e3
            dm = E.dma("sync",
                       x_out[ti * 128:(ti + 1) * 128, half * 512:(half + 1) * 512], ost,
                       waits=[e3])
            prevh = dm
            out_done.append(dm)
        sc_ps_free["ost"] = prevh
        xt_free2 = sc_ps_free[1]

    if dbg:
        E.dma("gpsimd", d_pt[:, :], pt_sb[:, :], waits=[route_done[-1]])
        E.dma("gpsimd", d_xt[:, :, :, :], XT[:, :, :, :], waits=[gV, gS])
    E.emit("sync", lambda e: None, waits=out_done)
    if fused is None:
        E.finalize(ctx)
        ctx.close()
    return nc


def host_prep_l0attn(inp):
    H = {}
    C = lambda a: np.ascontiguousarray(a, dtype=np.float32)
    Wq = np.asarray(inp["L0_attn_Wqkv"])
    H["Wqkv"] = C(np.concatenate([Wq, Wq], axis=0))
    H["bqkv"] = C(np.asarray(inp["L0_attn_bqkv"]).reshape(1, 192))
    H["Wr"] = C(np.asarray(inp["L0_attn_Wr"]).reshape(8, 128, 1024).transpose(1, 0, 2))
    H["br"] = C(np.asarray(inp["L0_attn_br"]).reshape(1, 1024))
    mj = np.zeros((128, 4, 512), np.float32)
    for j in range(4):
        kk = np.arange(128)[:, None] + 128 * j
        qq = np.arange(512)[None, :]
        mj[:, j, :] = (kk < qq).astype(np.float32)
    H["masks"] = mj.astype(ml_dtypes.bfloat16)
    return H


def build_attn_pre(nc, fused=None, xin_name="x_in"):
    dio = _phase_io(nc, fused, [
        (xin_name, [TOK, D], F32, "ExternalInput"),
        ("ident", [128, 128], F32, "ExternalInput"),
    ] + ([("xhT_out", [128, DCH, TOK], BF16, "ExternalOutput")] if fused is None
         else [("xag_in", [128, DCH, TOK], BF16, "Internal")]))
    x_in, w_ident = dio[xin_name], dio["ident"]
    if fused is None:
        xhT_out = dio["xhT_out"]
        ctx = ExitStack()
        sb = lambda name, shape, dt: ctx.enter_context(nc.sbuf_tensor(name, shape, dt))
    else:
        xag_in = dio["xag_in"]
        fused["ar"].reset()
        sb = fused["ar"].alloc
    xt = sb("xt", [128, D], F32)
    xhat = sb("xhat", [128, D], BF16)
    xhatT = sb("xhatT", [128, DCH, TOK], BF16)
    ident_sb = sb("ident_sb", [128, 128], BF16)
    lnst = sb("lnst", [128, NT, 8], F32)
    lncst = sb("lncst", [128, 2], F32)
    sqb = sb("sqb", [128, D], F32)
    if fused is None:
        ps_tr = ctx.enter_context(nc.psum_tensor("ps_tr", [128, 128], BF16))[:]
        E = Em(nc)
        T4 = [ps_tr]
    else:
        T4 = [fused["P8"][i][:].bitcast(BF16)[:, 0:128] for i in (4, 5, 6, 7)]
        E = fused["E"]
    NB = len(T4)
    E.lnconsts = lncst
    E.ln_sq = sqb[:]
    t_id = E.dma("gpsimd", ident_sb[:], w_ident[:, :])
    E.emit("vector", lambda v: v.memset(lncst[:, 0:1], 1e-5), inc="V")
    E.emit("vector", lambda v: v.memset(lncst[:, 1:2], -0.5), inc="V")
    ln_done = {}
    tr_free = {}
    for ti in range(NT):
        xl = E.dma("sync", xt[:], x_in[ti * 128:(ti + 1) * 128, :], waits=[ln_done.get(ti - 1)])
        t9 = emit_ln(E, xt[:], xhat[:], lnst[:, ti, :], ready=[xl],
                     war=[tr_free.get(ti * DCH - 1)])
        ln_done[ti] = t9
        for ch in range(DCH):
            k = ti * DCH + ch
            bank = T4[k % NB]
            tt = E.emit("tensor",
                        (lambda t, _s=xhat[:, ch * 128:(ch + 1) * 128], _o=bank:
                         t.transpose(_o, _s, ident_sb[:])),
                        waits=[t9, t_id, tr_free.get(k - NB)], inc="T")
            ev = E.emit("vector",
                        (lambda v, _d=xhatT[:, ch, ti * 128:(ti + 1) * 128], _o=bank:
                         v.tensor_copy(_d, _o)),
                        waits=[tt], inc="V")
            tr_free[k] = ev
    fin = tr_free[NT * DCH - 1]
    dms = []
    if fused is None:
        dms.append(E.dma("sync", xhT_out[:, :, :], xhatT[:, :, :], waits=[fin]))
    else:
        dms.append(E.dma("sync", xag_in[:, :, :], xhatT[:, :, :], waits=[fin]))
    E.emit("sync", lambda e: None, waits=dms)
    if fused is None:
        E.finalize(ctx)
        ctx.close()
    return nc


def build_attn_core(nc, dbg=False, fused=None):
    S = 8192
    NKT = S // 128
    NQB = S // 512
    dio = _phase_io(nc, fused, [
        ("Wqkv", [128, 192], F32, "ExternalInput"),
        ("bqkv", [1, 192], F32, "ExternalInput"),
        ("masks", [128, 4, 512], F32, "ExternalInput"),
        ("ident", [128, 128], F32, "ExternalInput"),
    ] + ([("xhT", [128, 2, S], BF16, "ExternalInput"),
          ("oT_out", [128, 2, S], BF16, "ExternalOutput")] if fused is None
         else [("xag_out", [8 * 128, DCH, TOK], BF16, "Internal"),
               ("oa2a_in", [8, 128, 2, TOK], BF16, "Internal")]))
    w_Wqkv, w_bqkv, w_masks, w_ident = dio["Wqkv"], dio["bqkv"], dio["masks"], dio["ident"]
    if fused is None:
        xhT, oT_out = dio["xhT"], dio["oT_out"]
    else:
        xag_out, oa2a_in = dio["xag_out"], dio["oa2a_in"]
        pb_, ph_ = fused["pb"], fused["ph"]
    if dbg:
        d_qT = nc.dram_tensor("d_qT", [64, S], F32, kind="ExternalOutput")
        d_kT = nc.dram_tensor("d_kT", [64, S], F32, kind="ExternalOutput")
        d_vp = nc.dram_tensor("d_vp", [128, 32, 130], F32, kind="ExternalOutput")
        d_ts = nc.dram_tensor("d_ts", [64, 65], F32, kind="ExternalOutput")
        d_bo = nc.dram_tensor("d_bo", [128, 512], F32, kind="ExternalOutput")
        d_pb = nc.dram_tensor("d_pb", [128, 512], F32, kind="ExternalOutput")
    if fused is None:
        ctx = ExitStack()
        sb = lambda name, shape, dt: ctx.enter_context(nc.sbuf_tensor(name, shape, dt))
    else:
        fused["ar"].reset()
        sb = fused["ar"].alloc
    ident_sb = sb("ident_sb", [128, 128], BF16)
    xh = sb("xh", [128, 2, S], BF16)
    qT = sb("qT", [64, S], BF16)
    kT = sb("kT", [64, S], BF16)
    vp = sb("vp", [128, NKT // 2, 130], BF16)
    pb = sb("pb", [128, 4, 512], BF16)
    oh = sb("oh", [64, 4, S], BF16)
    masks = sb("masks_sb", [128, 4, 512], BF16)
    ones_sb = sb("ones_sb", [128, 512], BF16)
    tsT = sb("tsT", [64, 65], BF16)
    ts_col = sb("ts_col", [65, 64], BF16)
    bq_sb = sb("bq_sb", [1, 192], BF16)
    sums = sb("sums", [128, S], F32)
    onesf = sb("onesf", [128, 64], F32)
    Wq_sb = sb("Wq_sb", [128, 192], BF16)

    if fused is None:
        S4 = [ctx.enter_context(nc.psum_tensor(f"ps_s{i}", [128, 512], F32))[:]
              for i in range(4)]
        ps_o = [ctx.enter_context(nc.psum_tensor(f"ps_o{i}", [128, 512], F32))[:]
                for i in range(2)]
        ps_q = ctx.enter_context(nc.psum_tensor("ps_q", [128, 512], F32))[:]
        ps_tr = ctx.enter_context(nc.psum_tensor("ps_tr", [128, 128], BF16))[:]
        E = Em(nc)
    else:
        P8 = fused["P8"]
        S4 = [P8[0][:], P8[1][:], P8[5][:], P8[6][:]]
        ps_o = [P8[2][:], P8[3][:]]
        ps_q = P8[4][:]
        ps_tr = P8[7][:].bitcast(BF16)[:, 0:128]
        E = fused["E"]
    if fused is None:
        t_x = E.dma("sync", xh[:, :, :], xhT[:, :, :])
    else:
        for ch in range(2):
            v = (xag_out.ap().rearrange("(s p) c t -> p c s t", p=128)
                 [:, ch:ch + 1, 0:4, :].squeeze(1))
            vin = _dyn(v, ph_ * (2 * TOK))
            t_x = E.dma("sync", xh[:, ch, :].rearrange("p (r t) -> p r t", r=4), vin)
    t_w = E.dma("gpsimd", Wq_sb[:], w_Wqkv[:, :])
    t_b = E.dma("gpsimd", bq_sb[:], w_bqkv[:, :])
    t_m = E.dma("sync", masks[:], w_masks[:, :, :])
    t_id = E.dma("sync", ident_sb[:], w_ident[:, :])
    if fused is not None:
        iom = fused["io"]
        for e in range(4):
            E.dma("gpsimd", iom["We1bf"][:, e, :, :, :], iom["We1"][:, e, :, :, :])
            E.dma("gpsimd", iom["We2bf"][:, e, :, :], iom["We2"][:, e, :, :])
        E.dma("gpsimd", iom["W1bf"][:, :, :, :], iom["W1"][:, :, :, :])
        E.dma("gpsimd", iom["W2bf"][:, :, :], iom["W2"][:, :, :])
        E.dma("gpsimd", iom["Wpbf"][:, :, :], iom["Wp"][:, :, :])
        E.dma("gpsimd", iom["Wfbf"][:, :, :], iom["Wf"][:, :, :])
    E.emit("vector", lambda v: v.memset(ones_sb[:], 1.0), inc="V")
    E.emit("vector", lambda v: v.memset(onesf[:], 1.0), inc="V")
    ones_tok = ("V", E.cnt["V"])

    oh_dmas = []
    bg = {}
    pb_guard = {}
    og = {}
    for hh in range(4):
        chunk = hh // 2
        base = 64 * (hh % 2)
        tp = (base, 0)
        nb = 0
        qk_evac = None
        for qc in range(2):
            dstT = qT if qc == 0 else kT
            for tt_i in range(NQB):
                c0 = tt_i * 512
                idx = nb % 4
                bank = S4[idx]
                nb += 1
                E.emit("tensor",
                       (lambda t, _l=Wq_sb[base:base + 64, qc * 64:(qc + 1) * 64],
                        _r=xh[base:base + 64, chunk, c0:c0 + 512], _tp=tp, _o=bank:
                        t.matmul(_o[0:64, :], _l, _r, start=True, stop=False,
                                 tile_position=_tp)),
                       waits=[t_x, t_w, bg.get(idx)], inc="T")
                m2 = E.emit("tensor",
                            (lambda t, _l=bq_sb[0:1, qc * 64:(qc + 1) * 64], _o=bank:
                             t.matmul(_o[0:64, :], _l, ones_sb[0:1, :],
                                      start=False, stop=True)),
                            waits=[t_b, ones_tok], inc="T")
                ev = E.emit("vector",
                            (lambda v, _d=dstT[0:64, c0:c0 + 512], _o=bank:
                             v.tensor_copy(_d, _o[0:64, :])),
                            waits=[m2], inc="V")
                bg[idx] = ev
                qk_evac = ev
        v_evac = None
        for kt in range(NKT):
            c0 = kt * 128
            idx = nb % 4
            bank = S4[idx]
            nb += 1
            E.emit("tensor",
                   (lambda t, _l=xh[base:base + 64, chunk, c0:c0 + 128],
                    _r=Wq_sb[base:base + 64, 128:192], _tp=tp, _o=bank:
                    t.matmul(_o[:, 0:64], _l, _r, start=True, stop=False, tile_position=_tp)),
                   waits=[t_x, t_w, bg.get(idx)], inc="T")
            E.emit("tensor",
                   (lambda t, _o=bank:
                    t.matmul(_o[:, 0:64], ones_sb[0:1, 0:128], bq_sb[0:1, 128:192],
                             start=False, stop=False)),
                   waits=[t_b], inc="T")
            m3 = E.emit("tensor",
                        (lambda t, _o=bank:
                         t.matmul(_o[:, 64:65], ones_sb[0:1, 0:128],
                                  ones_sb[0:1, 0:1], start=False, stop=True)),
                        inc="T")
            ev = E.emit("vector",
                        (lambda v, _d=vp[:, kt // 2, (kt % 2) * 65:(kt % 2) * 65 + 65], _o=bank:
                         v.tensor_copy(_d, _o[:, 0:65])),
                        waits=[m3], inc="V")
            bg[idx] = ev
            v_evac = ev
        ts_mm = None
        for kt in range(NKT):
            ts_mm = E.emit("tensor",
                           (lambda t, _l=vp[:, kt // 2, (kt % 2) * 65:(kt % 2) * 65 + 65],
                            _c=63 - kt:
                            t.matmul(ps_q[0:65, _c:_c + 1], _l, ones_sb[:, 0:1],
                                     start=True, stop=True)),
                           waits=[v_evac, bg.get("q")], inc="T")
        e1 = E.emit("vector", lambda v: v.tensor_copy(ts_col[:, :], ps_q[0:65, 0:64]),
                    waits=[ts_mm], inc="V")
        bg["q"] = e1
        tr1 = E.emit("tensor", lambda t: t.transpose(ps_tr[0:64, 0:65], ts_col[:, :],
                                                     ident_sb[0:65, 0:65]),
                     waits=[e1, t_id, bg.get("tr")], inc="T")
        e2 = E.emit("vector", lambda v: v.tensor_copy(tsT[:, :], ps_tr[0:64, 0:65]),
                    waits=[tr1], inc="V")
        bg["tr"] = e2

        seq = [(B, kt) for B in range(NQB) for kt in range(4 * B + 4)]
        exp_tok = {}

        def emit_qk(i, _seq=seq, _hh=hh, _qk=qk_evac):
            B, kt = _seq[i]
            idx = i % 4
            bank = S4[idx]
            q0 = B * 512
            sm = E.emit("tensor",
                        (lambda t, _l=kT[0:64, kt * 128:(kt + 1) * 128],
                         _r=qT[0:64, q0:q0 + 512], _o=bank:
                         t.matmul(_o, _l, _r, start=True, stop=True)),
                        waits=[_qk, bg.get(idx)], inc="T")
            if kt >= 4 * B:
                j = kt - 4 * B
                sm = E.emit("vector",
                            (lambda v, _o=bank, _m=masks[:, j, :]:
                             v.tensor_tensor(_o, _o, _m, ALU.mult)),
                            waits=[sm, t_m], inc="V")
            ex = E.emit("scalar",
                        (lambda s, _o=bank, _d=pb[:, idx, :]:
                         s.activation(out=_d, in_=_o, func=AF.Exp, scale=0.125)),
                        waits=[sm, pb_guard.get(idx)], inc="S")
            exp_tok[i] = ex
            bg[idx] = ex

        LOOK = 4
        for i in range(min(LOOK, len(seq))):
            emit_qk(i)
        qnext = LOOK
        for i, (B, kt) in enumerate(seq):
            bo = ps_o[B % 2]
            if kt == 0 and B < NQB - 1:
                ksuf = 4 * (NQB - 1 - B)
                E.emit("tensor",
                       (lambda t, _k=ksuf, _o=bo: t.matmul(_o[0:65, :], tsT[0:_k, :],
                                                           ones_sb[0:_k, :],
                                                           start=True, stop=False)),
                       waits=[e2, og.get(B % 2)], inc="T")
            av = E.emit("tensor",
                        (lambda t, _l=vp[:, kt // 2, (kt % 2) * 65:(kt % 2) * 65 + 65],
                         _r=pb[:, i % 4, :], _o=bo,
                         _st=(B == NQB - 1 and kt == 0), _sp=(kt == 4 * B + 3):
                         t.matmul(_o[0:65, :], _l, _r, start=_st, stop=_sp)),
                        waits=[exp_tok.pop(i), v_evac, og.get(B % 2)], inc="T")
            pb_guard[i % 4] = av
            if qnext < len(seq):
                emit_qk(qnext)
                qnext += 1
            if kt == 4 * B + 3:
                q0 = B * 512
                E.emit("vector",
                       (lambda v, _d=oh[0:64, hh, q0:q0 + 512], _o=bo:
                        v.tensor_copy(_d, _o[0:64, :])),
                       waits=[av], inc="V")
                evs = E.emit("vector",
                             (lambda v, _d=sums[64:65, q0:q0 + 512], _o=bo:
                              v.tensor_copy(_d, _o[64:65, :])),
                             waits=[av], inc="V")
                og[B % 2] = evs
        rcp = E.emit("vector", lambda v: v.reciprocal(sums[64:65, :], sums[64:65, :]),
                     waits=[og[0], og[1]], inc="V", strict=True)
        nm = None
        for j in range(NQB):
            idx = j % 4
            bank = S4[idx]
            bc = E.emit("tensor",
                        (lambda t, _o=bank, _r=sums[64:65, j * 512:(j + 1) * 512]:
                         t.matmul(_o[0:64, :], onesf[64:65, 0:64], _r, start=True, stop=True,
                                  tile_position=(64, 0))),
                        waits=[rcp, bg.get(idx)], inc="T")
            nm = E.emit("vector",
                        (lambda v, _d=oh[0:64, hh, j * 512:(j + 1) * 512], _o=bank:
                         v.tensor_tensor(_d, _d, _o[0:64, :], ALU.mult)),
                        waits=[bc], inc="V", strict=True)
            bg[idx] = nm
        if dbg and hh == 0:
            E.dma("gpsimd", d_qT[:, :], qT[:, :], waits=[qk_evac])
            E.dma("gpsimd", d_kT[:, :], kT[:, :], waits=[qk_evac])
            E.dma("gpsimd", d_vp[:, :, :], vp[:, :, :], waits=[v_evac])
            E.dma("gpsimd", d_ts[:, :], tsT[:, :], waits=[e2])
        odone = [nm]
        if fused is None:
            oh_dmas.append(E.dma("sync", oT_out[base:base + 64, chunk, :],
                                 oh[0:64, hh, :], waits=odone))
        else:
            for m in range(4):
                ov = oa2a_in[m, base:base + 64, chunk, :]
                ov = _dyn(ov, pb_ * (4 * 128 * 2 * TOK))
                oh_dmas.append(E.dma("sync", ov,
                                     oh[0:64, hh, m * TOK:(m + 1) * TOK], waits=odone))
    E.emit("sync", lambda e: None, waits=oh_dmas)
    if fused is None:
        E.finalize(ctx)
        ctx.close()
    return nc


def build_attn_proj(nc, fused=None, xin_name="x_in", xout_name="x_out"):
    dio = _phase_io(nc, fused, [
        (xin_name, [TOK, D], F32, "ExternalInput"),
        (xout_name, [TOK, D], F32, "ExternalOutput"),
        ("Wr", [128, 8, 1024], F32, "ExternalInput"),
        ("br", [1, 1024], F32, "ExternalInput"),
    ] + ([("oT", [128, DCH, TOK], BF16, "ExternalInput")] if fused is None
         else [("oa2a_out", [8, 128, 2, TOK], BF16, "Internal")]))
    x_in, x_out, w_Wr, w_br = dio[xin_name], dio[xout_name], dio["Wr"], dio["br"]
    if fused is None:
        oT = dio["oT"]
        ctx = ExitStack()
        sb = lambda name, shape, dt: ctx.enter_context(nc.sbuf_tensor(name, shape, dt))
    else:
        oa2a_out = dio["oa2a_out"]
        fused["ar"].reset()
        sb = fused["ar"].alloc
    o_sb = sb("o_sb", [128, DCH, TOK], BF16)
    Wr_sb = sb("Wr_sb", [128, 8, 1024], BF16)
    br_sb = sb("br_sb", [1, 1024], BF16)
    xt = sb("xt", [128, D], F32)
    ost = sb("ost", [128, D], F32)
    ones_sb = sb("ones_sb", [1, 128], BF16)
    if fused is None:
        ps = [ctx.enter_context(nc.psum_tensor(f"ps{i}", [128, 512], F32))[:]
              for i in range(2)]
        E = Em(nc)
        t_o = E.dma("sync", o_sb[:, :, :], oT[:, :, :])
    else:
        P8 = fused["P8"]
        ps = [P8[0][:], P8[1][:]]
        E = fused["E"]
        pb_ = fused["pb"]
        for r_ in range(4):
            v = oa2a_out[r_, :, :, :]
            vin = _dyn(v, pb_ * (4 * 128 * 2 * TOK))
            t_o = E.dma("gpsimd", o_sb[:, 2 * r_:2 * r_ + 2, :], vin)
    t_w = E.dma("gpsimd", Wr_sb[:], w_Wr[:, :, :])
    t_b = E.dma("gpsimd", br_sb[:], w_br[:, :])
    ot1 = E.emit("vector", lambda v: v.memset(ones_sb[:], 1.0), inc="V")
    guard = {}
    outs = []
    for ti in range(NT):
        xl = E.dma("scalar", xt[:], x_in[ti * 128:(ti + 1) * 128, :], waits=[guard.get("xt")])
        for half in range(2):
            bank = ps[half]
            for och in range(DCH):
                E.emit("tensor",
                       (lambda t, _l=o_sb[:, och, ti * 128:(ti + 1) * 128],
                        _r=Wr_sb[:, och, half * 512:(half + 1) * 512], _o=bank,
                        _st=(och == 0):
                        t.matmul(_o, _l, _r, start=_st, stop=False)),
                       waits=[t_o, t_w, guard.get(("ps", half))], inc="T")
            tb = E.emit("tensor",
                        (lambda t, _o=bank, _r=br_sb[0:1, half * 512:(half + 1) * 512]:
                         t.matmul(_o, ones_sb[0:1, :], _r, start=False, stop=True)),
                        waits=[t_b, ot1], inc="T")
            e1 = E.emit("vector",
                        (lambda v, _o=bank, _d=ost[:, half * 512:(half + 1) * 512],
                         _x=xt[:, half * 512:(half + 1) * 512]:
                         v.tensor_tensor(_d, _o, _x, ALU.add)),
                        waits=[tb, xl, guard.get("ost")], inc="V")
            guard[("ps", half)] = e1
        guard["xt"] = e1
        dm = E.dma("sync", x_out[ti * 128:(ti + 1) * 128, :], ost[:], waits=[e1])
        guard["ost"] = dm
        outs.append(dm)
    E.emit("sync", lambda e: None, waits=outs)
    if fused is None:
        E.finalize(ctx)
        ctx.close()
    return nc


def host_prep_conv(inp):
    H = {}
    C = lambda a: np.ascontiguousarray(a, dtype=np.float32).astype(ml_dtypes.bfloat16)
    N1 = N2 = 128
    NFULL = N1 * N2
    K2 = 65
    t2 = np.arange(N2)[:, None]
    k2 = np.arange(K2)[None, :]
    H["L1c"] = C(np.concatenate([np.cos(2 * np.pi * t2 * k2 / N2),
                                 -np.sin(2 * np.pi * t2 * k2 / N2)], axis=1))
    t1 = np.arange(N1)[:, None]
    k1 = np.arange(N1)[None, :]
    M2 = np.zeros((128, K2, 3, 128), np.float32)
    for q in range(K2):
        ang = 2 * np.pi * t1 * (q + 128 * k1) / NFULL
        M2[:, q, 0, :] = np.cos(ang)
        M2[:, q, 1, :] = np.sin(ang)
        M2[:, q, 2, :] = -np.sin(ang)
    H["M2"] = C(M2)
    k1c = np.arange(N1)[:, None]
    jb = np.arange(N1)[None, :]
    H["I1"] = C(np.stack([
        np.concatenate([np.cos(2 * np.pi * k1c * jb / N1),
                        np.sin(2 * np.pi * k1c * jb / N1)], axis=1),
        np.concatenate([-np.sin(2 * np.pi * k1c * jb / N1),
                        np.cos(2 * np.pi * k1c * jb / N1)], axis=1)], axis=1))
    alpha = np.full(K2, 2.0)
    alpha[0] = 1.0
    alpha[64] = 1.0
    ja = np.arange(64)[None, :]
    k2c = np.arange(K2)[:, None]
    I2 = np.zeros((65, 128, 2, 64), np.float32)
    for b in range(N1):
        ang = 2 * np.pi * (b + 128 * ja) * k2c / NFULL
        I2[:, b, 0, :] = alpha[:, None] * np.cos(ang) / NFULL
        I2[:, b, 1, :] = -alpha[:, None] * np.sin(ang) / NFULL
    H["I2"] = C(I2)
    F = lambda a: np.ascontiguousarray(a, dtype=np.float32)
    H["Wp"] = F(np.asarray(inp["L1_attn_Wp"]).reshape(8, 128, 4096).transpose(1, 0, 2))
    H["bp"] = F(np.asarray(inp["L1_attn_bp"]).reshape(1, 4096))
    H["Wf"] = F(np.asarray(inp["L1_attn_Wf"]).reshape(8, 128, 3072).transpose(1, 0, 2))
    H["bf"] = F(np.asarray(inp["L1_attn_bf"]).reshape(1, 3072))
    T, Dm = 8192, 1024
    pos = np.arange(T, dtype=np.float64)[:, None]
    i = np.arange(Dm)[None, :]
    angle = pos / np.power(10000.0, (2 * (i // 2)).astype(np.float64) / Dm)
    pe = np.where(i % 2 == 0, np.sin(angle), np.cos(angle)).astype(np.float32)
    H["posT"] = C(pe.reshape(8192, 8, 128).transpose(2, 1, 0))
    return H


def build_xp(nc, fused=None, xin_name="x_in"):
    dio = _phase_io(nc, fused, [
        (xin_name, [TOK, D], F32, "ExternalInput"),
        ("Wp", [128, 8, 4096], F32, "ExternalInput"),
        ("bp", [1, 4096], F32, "ExternalInput"),
        ("Wf", [128, 8, 3072], F32, "ExternalInput"),
        ("bf", [1, 3072], F32, "ExternalInput"),
        ("posT", [128, 8, 1024], F32, "ExternalInput"),
        ("ident", [128, 128], F32, "ExternalInput"),
    ] + ([("xp_out", [TOK, 4096], BF16, "ExternalOutput"),
          ("f_out", [1024, 3072], BF16, "ExternalOutput")] if fused is None
         else [("xpa2a_in", [8, 4, TOK, 128], BF16, "Internal"),
               ("fa2a_in", [8, 3, 1024, 128], BF16, "Internal")]))
    x_in, w_ident, w_pos = dio[xin_name], dio["ident"], dio["posT"]
    w_Wp, w_bp, w_Wf, w_bf = dio["Wp"], dio["bp"], dio["Wf"], dio["bf"]
    if fused is not None:
        w_Wp, w_Wf = fused["io"]["Wpbf"], fused["io"]["Wfbf"]
    wp_eng = "scalar" if fused is not None else "gpsimd"
    wf_eng = "sync" if fused is not None else "gpsimd"
    if fused is None:
        xp_out, f_out = dio["xp_out"], dio["f_out"]
        ctx = ExitStack()
        sb = lambda name, shape, dt: ctx.enter_context(nc.sbuf_tensor(name, shape, dt))
    else:
        xpa2a_in, fa2a_in = dio["xpa2a_in"], dio["fa2a_in"]
        fused["ar"].reset()
        sb = fused["ar"].alloc
    xt = sb("xt", [128, D], F32)
    xhat = sb("xhat", [128, D], BF16)
    xhatT = sb("xhatT", [128, DCH, TOK], BF16)
    posT_sb = sb("posT_sb", [128, DCH, 1024], BF16)
    Wp_sb = sb("Wp_sb", [128, 8, 4096], BF16)
    Wf_sb = sb("Wf_sb", [128, 8, 3072], BF16)
    bp_sb = sb("bp_sb", [1, 4096], BF16)
    bf_sb = sb("bf_sb", [1, 3072], BF16)
    ident_sb = sb("ident_sb", [128, 128], BF16)
    ones_sb = sb("ones_sb", [1, 128], BF16)
    ost = sb("ost", [128, 512], BF16)
    lnst = sb("lnst", [128, NT, 8], F32)
    lncst = sb("lncst", [128, 2], F32)
    sqb = sb("sqb", [128, D], F32)
    if fused is None:
        ps = [ctx.enter_context(nc.psum_tensor(f"ps{i}", [128, 512], F32))[:]
              for i in range(2)]
        ps_tr = ctx.enter_context(nc.psum_tensor("ps_tr", [128, 128], BF16))[:]
        E = Em(nc)
    else:
        P8 = fused["P8"]
        ps = [P8[0][:], P8[1][:]]
        ps_tr = P8[7][:].bitcast(BF16)[:, 0:128]
        E = fused["E"]
    E.lnconsts = lncst
    E.ln_sq = sqb[:]
    t_id = E.dma("gpsimd", ident_sb[:], w_ident[:, :])
    t_wp = E.dma(wp_eng, Wp_sb[:], w_Wp[:, :, :])
    t_bp = E.dma("gpsimd", bp_sb[:], w_bp[:, :])
    t_wf = E.dma(wf_eng, Wf_sb[:], w_Wf[:, :, :])
    t_bf = E.dma("gpsimd", bf_sb[:], w_bf[:, :])
    t_pos = E.dma("gpsimd", posT_sb[:], w_pos[:, :, :])
    E.emit("vector", lambda v: v.memset(lncst[:, 0:1], 1e-5), inc="V")
    E.emit("vector", lambda v: v.memset(lncst[:, 1:2], -0.5), inc="V")
    t_on = E.emit("vector", lambda v: v.memset(ones_sb[:], 1.0), inc="V")
    TB4 = [ps_tr] if fused is None else [P8[i][:].bitcast(BF16)[:, 0:128] for i in (4, 5, 6, 7)]
    NB4 = len(TB4)
    ln_done = {}
    tr_free = {}
    for ti in range(NT):
        xl = E.dma("sync", xt[:], x_in[ti * 128:(ti + 1) * 128, :], waits=[ln_done.get(ti - 1)])
        t9 = emit_ln(E, xt[:], xhat[:], lnst[:, ti, :], ready=[xl],
                     war=[tr_free.get(ti * DCH - 1)])
        ln_done[ti] = t9
        for ch in range(DCH):
            k = ti * DCH + ch
            bank = TB4[k % NB4]
            tt = E.emit("tensor",
                        (lambda t, _s=xhat[:, ch * 128:(ch + 1) * 128], _o=bank:
                         t.transpose(_o, _s, ident_sb[:])),
                        waits=[t9, t_id, tr_free.get(k - NB4)], inc="T")
            ev = E.emit("vector",
                        (lambda v, _d=xhatT[:, ch, ti * 128:(ti + 1) * 128], _o=bank:
                         v.tensor_copy(_d, _o)),
                        waits=[tt], inc="V")
            tr_free[k] = ev
    xT_all = ev
    guard = {}
    outs = []
    for ti in range(NT):
        for cb in range(8):
            bank = ps[cb % 2]
            for dch in range(DCH):
                E.emit("tensor",
                       (lambda t, _l=xhatT[:, dch, ti * 128:(ti + 1) * 128],
                        _r=Wp_sb[:, dch, cb * 512:(cb + 1) * 512], _o=bank, _st=(dch == 0):
                        t.matmul(_o, _l, _r, start=_st, stop=False)),
                       waits=[xT_all, t_wp, guard.get(cb % 2)], inc="T")
            tb = E.emit("tensor",
                        (lambda t, _o=bank, _r=bp_sb[0:1, cb * 512:(cb + 1) * 512]:
                         t.matmul(_o, ones_sb[0:1, :], _r, start=False, stop=True)),
                        waits=[t_bp, t_on], inc="T")
            ev = E.emit("vector", (lambda v, _o=bank: v.tensor_copy(ost[:], _o)),
                        waits=[tb, guard.get("ost")], inc="V")
            if fused is None:
                dm = E.dma("sync", xp_out[ti * 128:(ti + 1) * 128, cb * 512:(cb + 1) * 512],
                           ost[:], waits=[ev])
            else:
                s_, mh = cb // 2, (cb % 2) * 4
                dm = E.dma("sync",
                           xpa2a_in[mh:mh + 4, s_, ti * 128:(ti + 1) * 128, :]
                           .rearrange("m p c -> p m c"),
                           ost[:].rearrange("p (m c) -> p m c", c=128), waits=[ev])
            guard[cb % 2] = ev
            guard["ost"] = dm
            outs.append(dm)
    for ti in range(8):
        for cb in range(6):
            bank = ps[cb % 2]
            for dch in range(DCH):
                E.emit("tensor",
                       (lambda t, _l=posT_sb[:, dch, ti * 128:(ti + 1) * 128],
                        _r=Wf_sb[:, dch, cb * 512:(cb + 1) * 512], _o=bank, _st=(dch == 0):
                        t.matmul(_o, _l, _r, start=_st, stop=False)),
                       waits=[t_pos, t_wf, guard.get(cb % 2)], inc="T")
            tb = E.emit("tensor",
                        (lambda t, _o=bank, _r=bf_sb[0:1, cb * 512:(cb + 1) * 512]:
                         t.matmul(_o, ones_sb[0:1, :], _r, start=False, stop=True)),
                        waits=[t_bf, t_on], inc="T")
            ev = E.emit("vector", (lambda v, _o=bank: v.tensor_copy(ost[:], _o)),
                        waits=[tb, guard.get("ost")], inc="V")
            if fused is None:
                dm = E.dma("sync", f_out[ti * 128:(ti + 1) * 128, cb * 512:(cb + 1) * 512],
                           ost[:], waits=[ev])
            else:
                s_, mh = cb // 2, (cb % 2) * 4
                dm = E.dma("sync",
                           fa2a_in[mh:mh + 4, s_, ti * 128:(ti + 1) * 128, :]
                           .rearrange("m p c -> p m c"),
                           ost[:].rearrange("p (m c) -> p m c", c=128), waits=[ev])
            guard[cb % 2] = ev
            guard["ost"] = dm
            outs.append(dm)
    E.emit("sync", lambda e: None, waits=outs)
    if fused is None:
        E.finalize(ctx)
        ctx.close()
    return nc


def build_conv(nc, dbg=False, maxph=99, fused=None):
    G = 64
    K2 = 65
    dio = _phase_io(nc, fused, [
        ("L1c", [128, 130], F32, "ExternalInput"),
        ("M2", [128, 65, 3, 128], F32, "ExternalInput"),
        ("I1", [128, 2, 256], F32, "ExternalInput"),
        ("I2", [65, 128, 2, 64], F32, "ExternalInput"),
    ] + ([("u_in", [4, 2, 128, 64, 128], BF16, "ExternalInput"),
          ("f_in", [3, 128, 64, 128], BF16, "ExternalInput"),
          ("y_out", [2, 128, 64, 128], BF16, "ExternalOutput")] if fused is None
         else [("xpa2a_out", [8, 4, TOK, 128], BF16, "Internal"),
               ("fa2a_out", [8, 3, 1024, 128], BF16, "Internal"),
               ("ya2a_in", [8, 128, TOK], BF16, "Internal")]))
    c_L1, c_M2, c_I1, c_I2 = dio["L1c"], dio["M2"], dio["I1"], dio["I2"]
    if fused is None:
        u_in, f_in, y_out = dio["u_in"], dio["f_in"], dio["y_out"]
    else:
        xpa2a_out, fa2a_out, ya2a_in = dio["xpa2a_out"], dio["fa2a_out"], dio["ya2a_in"]
    if dbg:
        d_A = nc.dram_tensor("d_A", [128, 2, 65, 64], F32, kind="ExternalOutput")
        d_X = nc.dram_tensor("d_X", [128, 2, 65, 64], F32, kind="ExternalOutput")
        d_G = nc.dram_tensor("d_G", [128, 2, 65, 64], F32, kind="ExternalOutput")
        d_Z = nc.dram_tensor("d_Z", [128, 2, 64, 65], F32, kind="ExternalOutput")
        d_At = nc.dram_tensor("d_At", [65, 2, 128, 64], F32, kind="ExternalOutput")
    if fused is None:
        ctx = ExitStack()
        sb = lambda name, shape, dt: ctx.enter_context(nc.sbuf_tensor(name, shape, dt))
    else:
        fused["ar"].reset()
        sb = fused["ar"].alloc
    M2_sb = sb("M2_sb", [128, 65, 3, 128], BF16)
    I2_sb = sb("I2_sb", [65, 128, 2, 64], BF16)
    I1_sb = sb("I1_sb", [128, 2, 256], BF16)
    L1_sb = sb("L1_sb", [128, 130], BF16)
    yD1 = sb("yD1", [64, G, 128], BF16)
    fD1 = sb("fD1", [64, G, 128], BF16)
    gD1 = sb("gD1", [64, G, 128], BF16) if fused is None else None
    gD1s = sb("gD1s", [64, 128, G], BF16)[:] if fused is not None else None
    Ab_t = sb("Ab_t", [128, 2, K2, G], BF16)
    Fb_t = sb("Fb_t", [128, 2, K2, G], BF16)
    arena = sb("arena", [128, 16640], BF16)
    Xb = arena[:, 0:8320].rearrange("p (x q c) -> p x q c", x=2, q=K2)
    Gb = arena[:, 8320:16640].rearrange("p (x q c) -> p x q c", x=2, q=K2)
    At = arena[0:65, 0:16384].rearrange("p (x j c) -> p x j c", x=2, j=128)
    Ab = Ab_t[:]
    Fb = Fb_t[:]
    Zb = Ab_t[:].rearrange("p x q c -> p x c q")
    Sc = Fb_t[:].rearrange("p x q c -> p x c q")

    if fused is None:
        psf = [ctx.enter_context(nc.psum_tensor(f"cps{i}", [128, 512], F32))[:]
               for i in range(8)]
        E = Em(nc)
    else:
        psf = [p[:] for p in fused["P8"]]
        E = fused["E"]
    uSt = (Ab_t[:].rearrange("p x q c -> p (x q c)")[0:64, 0:128 * G]
           .rearrange("p (a c) -> p a c", c=G)) if fused is not None else None
    t_m2 = E.dma("scalar" if fused is not None else "gpsimd", M2_sb[:], c_M2[:, :, :, :])
    t_i2 = E.dma("scalar" if fused is not None else "gpsimd", I2_sb[:], c_I2[:, :, :, :])
    t_i1 = E.dma("gpsimd", I1_sb[:], c_I1[:, :, :])
    t_l1 = E.dma("gpsimd", L1_sb[:], c_L1[:, :])
    gd = {}
    out_dmas = []

    def vs():
        return [("V", E.cnt["V"]), ("S", E.cnt["S"])]

    def fwd_l1(srcD1, Adst, tok_src, war):
        for c0 in range(0, G, 3):
            ncc = min(3, G - c0)
            bi = (c0 // 3) % 2
            bank = psf[bi]
            mm = None
            for j in range(ncc):
                mm = E.emit("tensor",
                            (lambda t, _l=srcD1[0:64, c0 + j, :],
                             _o=bank[:, j * 130:(j + 1) * 130]:
                             t.matmul(_o, _l, L1_sb[0:64, :], start=True, stop=True)),
                            waits=[tok_src, t_l1] + gd.get(("l1b", bi), []), inc="T")
            bv = bank[:, 0:ncc * 130].rearrange("p (c q) -> p c q", q=130)
            eng = "vector" if bi == 0 else "scalar"
            sem = "V" if bi == 0 else "S"
            def mkcopy(comp, _bv=bv, _ncc=ncc, _c0=c0, _Adst=Adst):
                s_ap = _bv[:, :, comp * K2:(comp + 1) * K2].rearrange("p c q -> p q c")
                d_ap = _Adst[:, comp, :, _c0:_c0 + _ncc]
                return lambda en: (en.tensor_copy(d_ap, s_ap) if hasattr(en, "tensor_copy")
                                   else en.copy(d_ap, s_ap))
            E.emit(eng, mkcopy(0), waits=[mm] + war, inc=sem)
            e2 = E.emit(eng, mkcopy(1), waits=[], inc=sem)
            gd[("l1b", bi)] = [e2]
        return vs()

    def fwd_l2(Asrc, Fsrc, Xdst, Gdst, tokAF, war):
        for o0 in range(0, K2, 8):
            nq = min(8, K2 - o0)
            par = (o0 // 8) % 2
            bks = [psf[4 * par + i] for i in range(4)]
            mm = None
            for qq in range(nq):
                q = o0 + qq
                outs = [b[:, qq * 64:(qq + 1) * 64] for b in bks]
                mm = E.emit("tensor",
                            (lambda t, _l=M2_sb[:, q, 0, :], _r=Asrc[:, 0, q, :], _o=outs[0]:
                             t.matmul(_o, _l, _r, start=True, stop=False)),
                            waits=[t_m2] + tokAF + gd.get(("l2b", par), []), inc="T")
                E.emit("tensor", (lambda t, _l=M2_sb[:, q, 0, :], _r=Asrc[:, 1, q, :], _o=outs[1]:
                                  t.matmul(_o, _l, _r, start=True, stop=False)), inc="T")
                E.emit("tensor", (lambda t, _l=M2_sb[:, q, 0, :], _r=Fsrc[:, 0, q, :], _o=outs[2]:
                                  t.matmul(_o, _l, _r, start=True, stop=False)), inc="T")
                E.emit("tensor", (lambda t, _l=M2_sb[:, q, 0, :], _r=Fsrc[:, 1, q, :], _o=outs[3]:
                                  t.matmul(_o, _l, _r, start=True, stop=False)), inc="T")
                E.emit("tensor", (lambda t, _l=M2_sb[:, q, 1, :], _r=Asrc[:, 1, q, :], _o=outs[0]:
                                  t.matmul(_o, _l, _r, start=False, stop=True)), inc="T")
                E.emit("tensor", (lambda t, _l=M2_sb[:, q, 1, :], _r=Fsrc[:, 1, q, :], _o=outs[2]:
                                  t.matmul(_o, _l, _r, start=False, stop=True)), inc="T")
                E.emit("tensor", (lambda t, _l=M2_sb[:, q, 2, :], _r=Asrc[:, 0, q, :], _o=outs[1]:
                                  t.matmul(_o, _l, _r, start=False, stop=True)), inc="T")
                mm = E.emit("tensor",
                            (lambda t, _l=M2_sb[:, q, 2, :], _r=Fsrc[:, 0, q, :], _o=outs[3]:
                             t.matmul(_o, _l, _r, start=False, stop=True)), inc="T")
            toks = []
            for i, (dst, comp) in enumerate(((Xdst, 0), (Xdst, 1), (Gdst, 0), (Gdst, 1))):
                eng = "vector" if i % 2 == 0 else "scalar"
                def bld(en, _b=bks[i], _d=dst[:, comp, o0:o0 + nq, :], _n=nq):
                    srcap = _b[:, 0:_n * 64].rearrange("p (q c) -> p q c", c=64)
                    return en.tensor_copy(_d, srcap) if hasattr(en, "tensor_copy") else en.copy(_d, srcap)
                toks.append(E.emit(eng, bld, waits=[mm] + war,
                                   inc="V" if eng == "vector" else "S"))
            gd[("l2b", par)] = toks
        return vs() + [E.tensor_tok()]

    def pointwise(war):
        t1 = E.emit("vector", lambda v: v.tensor_tensor(
            Sc[:, 0], Xb[:, 0].rearrange("p q c -> p c q"),
            Gb[:, 0].rearrange("p q c -> p c q"), ALU.mult), waits=war, inc="V", strict=True)
        t2 = E.emit("vector", lambda v: v.tensor_tensor(
            Sc[:, 1], Xb[:, 1].rearrange("p q c -> p c q"),
            Gb[:, 1].rearrange("p q c -> p c q"), ALU.mult), inc="V")
        t3 = E.emit("vector", lambda v: v.tensor_tensor(
            Zb[:, 0], Sc[:, 0], Sc[:, 1], ALU.subtract), waits=[t1, t2], inc="V", strict=True)
        t4 = E.emit("vector", lambda v: v.tensor_tensor(
            Sc[:, 0], Xb[:, 0].rearrange("p q c -> p c q"),
            Gb[:, 1].rearrange("p q c -> p c q"), ALU.mult), waits=[t3], inc="V", strict=True)
        t5 = E.emit("vector", lambda v: v.tensor_tensor(
            Sc[:, 1], Xb[:, 1].rearrange("p q c -> p c q"),
            Gb[:, 0].rearrange("p q c -> p c q"), ALU.mult), inc="V")
        t6 = E.emit("vector", lambda v: v.tensor_tensor(
            Zb[:, 1], Sc[:, 0], Sc[:, 1], ALU.add), waits=[t4, t5], inc="V", strict=True)
        return [t3, t6]

    def inv_l1(war):
        for c0 in range(0, G, 2):
            bi = (c0 // 2) % 4
            bank = psf[bi]
            mm = None
            for j in range(2):
                c = c0 + j
                E.emit("tensor",
                       (lambda t, _l=Zb[:, 0, c, :], _o=bank[0:65, j * 256:(j + 1) * 256]:
                        t.matmul(_o, _l, I1_sb[:, 0, :], start=True, stop=False)),
                       waits=[t_i1] + gd.get(("i1b", bi), []) + war, inc="T")
                mm = E.emit("tensor",
                            (lambda t, _l=Zb[:, 1, c, :], _o=bank[0:65, j * 256:(j + 1) * 256]:
                             t.matmul(_o, _l, I1_sb[:, 1, :], start=False, stop=True)),
                            inc="T")
            bv = bank[0:65, 0:512].rearrange("p (c x j) -> p c x j", x=2, j=128)
            eng = "vector" if bi % 2 == 0 else "scalar"
            sem = "V" if bi % 2 == 0 else "S"
            def mkcopy(comp, _bv=bv, _c0=c0):
                s_ap = _bv[:, :, comp, :].rearrange("p c j -> p j c")
                d_ap = At[:, comp, :, _c0:_c0 + 2]
                return lambda en: (en.tensor_copy(d_ap, s_ap) if hasattr(en, "tensor_copy")
                                   else en.copy(d_ap, s_ap))
            E.emit(eng, mkcopy(0), waits=[mm] + war, inc=sem)
            e2 = E.emit(eng, mkcopy(1), waits=[], inc=sem)
            gd[("i1b", bi)] = [e2]
        return vs()

    def inv_l2(gsrc, ydst, tokAt, tok_g):
        for j0 in range(0, 128, 8):
            par = (j0 // 8) % 2
            bank = psf[6 + par]
            mm = None
            for jj in range(8):
                jb = j0 + jj
                E.emit("tensor",
                       (lambda t, _l=I2_sb[:, jb, 0, :], _r=At[:, 0, jb, :],
                        _o=bank[0:64, jj * 64:(jj + 1) * 64]:
                        t.matmul(_o, _l, _r, start=True, stop=False)),
                       waits=[t_i2] + gd.get(("i2b", par), []) + tokAt, inc="T")
                mm = E.emit("tensor",
                            (lambda t, _l=I2_sb[:, jb, 1, :], _r=At[:, 1, jb, :],
                             _o=bank[0:64, jj * 64:(jj + 1) * 64]:
                             t.matmul(_o, _l, _r, start=False, stop=True)),
                            inc="T")
            bv = bank[0:64, :].rearrange("p (j c) -> p j c", c=64)
            e = E.emit("vector",
                       (lambda v, _s=bv.rearrange("p j c -> p c j"),
                        _g=gsrc[0:64, :, j0:j0 + 8], _d=ydst[0:64, :, j0:j0 + 8]:
                        v.tensor_tensor(_d, _s, _g, ALU.mult)),
                       waits=[mm, tok_g], inc="V")
            gd[("i2b", par)] = [e]
        return [("V", E.cnt["V"])]

    def load_D1(dst_name, dst, kind, s, b, ch0, waits):
        toks = []
        if fused is None:
            srct = {"u": u_in, "f": f_in}[kind if kind != "g" else "u"]
            if kind == "f":
                ap = f_in[s, ch0:ch0 + G, :, :].rearrange("c t x -> t c x")
            else:
                ap = u_in[s, b, ch0:ch0 + G, :, :].rearrange("c t x -> t c x")
            return [E.dma("sync", dst[:, :, :], ap, waits=waits)], None
        if kind in ("u", "g"):
            tgt = uSt if kind == "u" else dst
            for qi in range(4):
                toks.append(E.dma("sync", tgt[16 * qi:16 * qi + 16, :, :],
                                  xpa2a_out[b * 4 + qi, s, :, ch0:ch0 + G]
                                  .rearrange("(a j) c -> a j c", j=128), waits=waits))
        else:
            for si in range(8):
                toks.append(E.dma("sync", uSt[8 * si:8 * si + 8, :, :],
                                  fa2a_out[si, s, :, ch0:ch0 + G]
                                  .rearrange("(a j) c -> a j c", j=128), waits=waits))
        if kind == "g":
            return toks, None
        rv = E.emit("vector",
                    (lambda v, _d=dst[:, :, :], _s=uSt[:].rearrange("t a c -> t c a")[0:64]:
                     v.tensor_copy(_d, _s)), waits=toks, inc="V")
        return [rv], rv

    for g in range(4 if maxph >= 99 else 1):
        b, ch0 = g // 2, (g % 2) * 64
        if fused is None:
            tok_ys, _ = load_D1("y", yD1, "u", 0, b, ch0, gd.get("yD1w", []))
            tok_y = tok_ys[0]
        else:
            tok_ys, _ = load_D1("y", yD1, "u", 0, b, ch0,
                                gd.get("yD1w", []) + gd.get("uStw", [])
                                + gd.get("Zread", []) + [E.tensor_tok()])
            gd["uStw"] = [tok_ys[0]]
            tok_y = tok_ys[0]
        for s in range(3 if maxph >= 99 else 1):
            if fused is None:
                tok_fs, _ = load_D1("f", fD1, "f", s, b, ch0, gd.get("fD1w", []))
                tok_f = tok_fs[0]
                tok_gs, _ = load_D1("g", gD1, "g", s + 1, b, ch0, gd.get("gD1w", []))
                tok_g = tok_gs[0]
                gview = gD1
            else:
                tok_fs, _ = load_D1("f", fD1, "f", s, b, ch0,
                                    gd.get("fD1w", []) + gd.get("uStw", [])
                                    + gd.get("Zread", []) + [E.tensor_tok()])
                gd["uStw"] = [tok_fs[0]]
                tok_f = tok_fs[0]
                tok_gs, _ = load_D1("g", gD1s, "g", s + 1, b, ch0, gd.get("gD1w", []))
                tok_g = tok_gs[-1]
                gview = gD1s.rearrange("t a c -> t c a")
            warA = gd.get("Zread", []) + (gd.get("uStw", []) if fused is not None else [])
            tokA = fwd_l1(yD1, Ab, tok_y, warA)
            if maxph <= 1:
                dm = E.dma("sync", y_out[0, 0:64, :, :].rearrange("c a j -> a c j"),
                           yD1[:, :, :], waits=tokA)
                E.emit("sync", lambda e: None, waits=[dm])
                E.finalize(ctx)
                ctx.close()
                return nc
            tokF = fwd_l1(fD1, Fb, tok_f, warA)
            warX = gd.get("Atread", [])
            if maxph <= 2:
                dm = E.dma("sync", y_out[0, 0:64, :, :].rearrange("c a j -> a c j"),
                           yD1[:, :, :], waits=tokA + tokF)
                E.emit("sync", lambda e: None, waits=[dm])
                E.finalize(ctx)
                ctx.close()
                return nc
            tokX = fwd_l2(Ab, Fb, Xb, Gb, tokA + tokF, warX)
            if dbg and g == 0 and s == 0:
                da = E.dma("gpsimd", d_A[:, :, :, :], Ab, waits=tokA)
                db = E.dma("gpsimd", d_X[:, :, :, :], Xb, waits=tokX)
                dc = E.dma("gpsimd", d_G[:, :, :, :], Gb, waits=tokX)
                tokX = tokX + [da, db, dc]
            if maxph <= 3:
                dm = E.dma("sync", y_out[0, 0:64, :, :].rearrange("c a j -> a c j"),
                           yD1[:, :, :], waits=tokX)
                E.emit("sync", lambda e: None, waits=[dm])
                E.finalize(ctx)
                ctx.close()
                return nc
            tokZ = pointwise(tokX)
            if dbg and g == 0 and s == 0:
                dz = E.dma("gpsimd", d_Z[:, :, :, :].rearrange("p x c q -> p (x c q)"),
                           Ab_t[:].rearrange("p x q c -> p (x q c)"), waits=tokZ)
                tokZ = tokZ + [dz]
            if maxph <= 4:
                dm = E.dma("sync", y_out[0, 0:64, :, :].rearrange("c a j -> a c j"),
                           yD1[:, :, :], waits=tokZ)
                E.emit("sync", lambda e: None, waits=[dm])
                E.finalize(ctx)
                ctx.close()
                return nc
            tokT = inv_l1(tokZ)
            gd["Zread"] = [E.tensor_tok()]
            if dbg and g == 0 and s == 0:
                dt_ = E.dma("gpsimd", d_At[:, :, :, :], At, waits=tokT)
                tokT = tokT + [dt_]
            if maxph <= 5:
                dm = E.dma("sync", y_out[0, 0:64, :, :].rearrange("c a j -> a c j"),
                           yD1[:, :, :], waits=tokT)
                E.emit("sync", lambda e: None, waits=[dm])
                E.finalize(ctx)
                ctx.close()
                return nc
            tok_yv = inv_l2(gview if fused is not None else gD1, yD1, tokT, tok_g)
            gd["Atread"] = [E.tensor_tok()]
            gd["fD1w"] = [E.tensor_tok()]
            gd["gD1w"] = tok_yv
            tok_y = tok_yv[0]
        if fused is None:
            dm = E.dma("sync", y_out[b, ch0:ch0 + G, :, :].rearrange("c a j -> a c j"),
                       yD1[:, :, :], waits=[tok_y])
            out_dmas.append(dm)
            gd["yD1w"] = [dm]
        else:
            dms = []
            for tq in range(4):
                dms.append(E.dma("sync",
                                 ya2a_in[b * 4 + tq, ch0:ch0 + G, :]
                                 .rearrange("c (a j) -> a c j", j=128),
                                 yD1[16 * tq:16 * tq + 16, :, :], waits=[tok_y]))
            out_dmas += dms
            gd["yD1w"] = dms
        if maxph <= 6:
            break
    E.emit("sync", lambda e: None, waits=out_dmas)
    if fused is None:
        E.finalize(ctx)
        ctx.close()
    return nc


def build_fused(nc, dbg=False):
    io = {}

    def dram(name, shape, dt, kind):
        io[name] = nc.dram_tensor(name, shape, dt, kind=kind)

    dram("x_in0", [TOK, D], F32, "ExternalInput")
    dram("x_out", [TOK, D], F32, "ExternalOutput")
    for n, s in (("Wc", [128, 8, 128]), ("bc", [1, 128]),
                 ("W1", [128, 8, 128, 32]), ("b1", [32, 128]), ("W2", [32, 128, 1024]),
                 ("b2", [128, 1024]),
                 ("Wqkv", [128, 192]), ("bqkv", [1, 192]),
                 ("Wr", [128, 8, 1024]), ("br", [1, 1024]),
                 ("rW1", [128, 8, 16, 128]), ("rb1", [128, 16]), ("rW2", [128, 16, 4]),
                 ("rb2", [1, 4]), ("We1", [128, 4, 8, 32, 128]), ("be1", [128, 4, 32]),
                 ("We2", [128, 4, 32, 1024]), ("be2", [1, 4, 1024]),
                 ("Wp", [128, 8, 4096]), ("bp", [1, 4096]), ("Wf", [128, 8, 3072]),
                 ("bf", [1, 3072])):
        dram(n, s, F32, "ExternalInput")
    for n, s in (("ident", [128, 128]), ("masks", [128, 4, 512]), ("tie", [1, 512]),
                 ("posT", [128, 8, 1024]),
                 ("L1c", [128, 130]), ("M2", [128, 65, 3, 128]), ("I1", [128, 2, 256]),
                 ("I2", [65, 128, 2, 64])):
        dram(n, s, BF16, "ExternalInput")
    for n, s in (("We1bf", [128, 4, 8, 32, 128]), ("We2bf", [128, 4, 32, 1024]),
                 ("W1bf", [128, 8, 128, 32]), ("W2bf", [32, 128, 1024]),
                 ("Wpbf", [128, 8, 4096]), ("Wfbf", [128, 8, 3072])):
        dram(n, s, BF16, "Internal")
    for n in ("xb1", "xb2", "xb3", "xb4"):
        dram(n, [TOK, D], F32, "ExternalOutput" if dbg else "Internal")
    dram("Rd", [128, 64, 1024], BF16, "Internal")
    dram("xag_in", [128, DCH, TOK], BF16, "Internal")
    dram("xag_out", [4 * 128, DCH, TOK], BF16, "Internal")
    dram("oa2a_in", [8, 128, 2, TOK], BF16, "Internal")
    dram("oa2a_out", [8, 128, 2, TOK], BF16, "Internal")
    dram("xpa2a_in", [8, 4, TOK, 128], BF16, "Internal")
    dram("xpa2a_out", [8, 4, TOK, 128], BF16, "Internal")
    dram("fa2a_in", [8, 3, 1024, 128], BF16, "Internal")
    dram("fa2a_out", [8, 3, 1024, 128], BF16, "Internal")
    dram("ya2a_in", [8, 128, TOK], BF16, "Internal")
    dram("ya2a_out", [8, 128, TOK], BF16, "Internal")
    if dbg:
        dram("d_y", [8, 128, TOK], BF16, "ExternalOutput")
        dram("d_xp", [8, 4, TOK, 128], BF16, "ExternalOutput")
        dram("d_f", [8, 3, 1024, 128], BF16, "ExternalOutput")
        dram("d_yin", [8, 128, TOK], BF16, "ExternalOutput")
        dram("d_xpi", [8, 4, TOK, 128], BF16, "ExternalOutput")
        dram("d_fi", [8, 3, 1024, 128], BF16, "ExternalOutput")

    ctx = ExitStack()
    E = Em(nc)
    ar = Arena(nc)
    P8 = [nc.alloc_psum_tensor(f"PSB{i}", [128, 512], F32) for i in range(8)]
    pid = nc.partition_id(mybir.ALL_ENGINES)
    fused = {"E": E, "ar": ar, "P8": P8, "io": io,
             "pb": pid // 4, "ph": pid % 4}
    G4 = [[0, 1, 2, 3], [4, 5, 6, 7]]
    G8 = [[0, 1, 2, 3, 4, 5, 6, 7]]
    ncc = [0]

    def coll(kind, in_name, out_name, groups):
        E.phase_barrier()
        ncc[0] += 1
        sem = f"cc{ncc[0]}"
        E.semnames.add(sem)

        def flat(t):
            n = len(t.shape)
            if n == 4:
                return t.ap().rearrange("a b c d -> (a b c) d")
            if n == 3:
                return t.ap().rearrange("a b c -> (a b) c")
            return t.ap()
        iap, oap = flat(io[in_name]), flat(io[out_name])

        def bld(e, _i=iap, _o=oap, _g=groups, _k=kind):
            from concourse.aot_env import allow_ici_all_to_all
            with allow_ici_all_to_all():
                return e.collective_compute(_k, ALU.bypass, replica_groups=_g,
                                            ins=[_i.opt()], outs=[_o.opt()])
        tok = E.emit("gpsimd", bld, waits=E.all_toks(), inc=sem, inc_by=1)
        if not hasattr(E, "_dmaprev"):
            E._dmaprev = {}
        E._dmaprev[sem] = tok
        E.phase_barrier()

    build_l0ffn(nc, fused=fused, xin_name="x_in0", xout_name="xb1")
    E.phase_barrier()
    build_attn_pre(nc, fused=fused, xin_name="xb1")
    coll("AllGather", "xag_in", "xag_out", G4)
    build_attn_core(nc, fused=fused)
    coll("AllToAll", "oa2a_in", "oa2a_out", G8)
    build_attn_proj(nc, fused=fused, xin_name="xb1", xout_name="xb2")
    E.phase_barrier()
    build_l0ffn(nc, fused=fused, use_pc=True, xin_name="xb2", xout_name="xb3")
    E.phase_barrier()
    build_l1ffn(nc, fused=fused, use_pc=True, xin_name="xb3", xout_name="xb4")
    E.phase_barrier()
    build_xp(nc, fused=fused, xin_name="xb4")
    if dbg:
        E.phase_barrier()
        E.dma("sync", io["d_xpi"][:, :, :, :], io["xpa2a_in"][:, :, :, :],
              waits=E.all_toks())
        E.dma("sync", io["d_fi"][:, :, :, :], io["fa2a_in"][:, :, :, :],
              waits=E.all_toks())
        E.phase_barrier()
    coll("AllToAll", "xpa2a_in", "xpa2a_out", G8)
    coll("AllToAll", "fa2a_in", "fa2a_out", G8)
    if dbg:
        E.dma("sync", io["d_xp"][:, :, :, :], io["xpa2a_out"][:, :, :, :],
              waits=E.all_toks())
        E.dma("sync", io["d_f"][:, :, :, :], io["fa2a_out"][:, :, :, :],
              waits=E.all_toks())
        E.phase_barrier()
    build_conv(nc, fused=fused)
    if dbg:
        E.phase_barrier()
        E.dma("sync", io["d_yin"][:, :, :], io["ya2a_in"][:, :, :],
              waits=E.all_toks())
        E.phase_barrier()
    coll("AllToAll", "ya2a_in", "ya2a_out", G8)
    if dbg:
        E.dma("sync", io["d_y"][:, :, :], io["ya2a_out"][:, :, :],
              waits=E.all_toks())
        E.phase_barrier()
    build_l1ffn(nc, fused=fused, with_y=True, use_pc=True, xin_name="xb4", xout_name="x_out")
    E.finalize(ctx)
    ctx.close()
    return nc


kernel_fused_dbg = [False]


def kernel_fused(inp):
    x = np.ascontiguousarray(inp["L0_attn_x"].reshape(16384, 1024), dtype=np.float32)
    H = {}
    H.update(host_prep_l0ffn(inp))
    H.update(host_prep_l0attn(inp))
    H.update(host_prep_l1ffn(inp))
    H.update(host_prep_conv(inp))
    nc = bass.Bass("TRN2", target_bir_lowering=False, debug=False)
    build_fused(nc, dbg=kernel_fused_dbg[0])
    names = ("Wc", "bc", "tie", "W1", "b1", "W2", "b2", "Wqkv", "bqkv", "masks",
             "Wr", "br", "rW1", "rb1", "rW2", "rb2", "We1", "be1", "We2", "be2",
             "Wp", "bp", "Wf", "bf", "L1c", "M2", "I1", "I2")
    in_maps = []
    for i in range(NCORE):
        m = {n: H[n] for n in names}
        m["ident"] = _IDENTB
        m["x_in0"] = np.ascontiguousarray(x[i * TOK:(i + 1) * TOK])
        m["posT"] = np.ascontiguousarray(H["posT"][:, :, 1024 * i:1024 * (i + 1)])
        in_maps.append(m)
    from concourse.aot_env import allow_ici_all_to_all
    with allow_ici_all_to_all():
        res = run_bass_kernel_spmd(nc, in_maps, core_ids=list(range(NCORE)))
    out = np.concatenate([np.asarray(r["x_out"]) for r in res.results], axis=0)
    if kernel_fused_dbg[0]:
        kernel_fused_dbg.append(res.results)
    return out.reshape(2, 8192, 1024)


def _shard_rows(a):
    return [np.ascontiguousarray(a[i * TOK:(i + 1) * TOK]) for i in range(NCORE)]


_IDENT = np.eye(128, dtype=np.float32)
_IDENTB = np.eye(128, dtype=ml_dtypes.bfloat16)


def _run(builder, in_maps, outs, **bkw):
    nc = bass.Bass("TRN2", target_bir_lowering=False, debug=False)
    builder(nc, **bkw)
    res = run_bass_kernel_spmd(nc, in_maps, core_ids=list(range(NCORE)))
    return [[np.asarray(r[o]) for o in outs] for r in res.results]


def kernel(**inp):
    inp = {k: np.asarray(v) for k, v in inp.items()}
    return kernel_fused(inp).astype(np.float32)


def kernel_multilaunch(**inp):
    inp = {k: np.asarray(v) for k, v in inp.items()}
    x = np.ascontiguousarray(inp["L0_attn_x"].reshape(16384, 1024), dtype=np.float32)
    H0f = host_prep_l0ffn(inp)
    H0a = host_prep_l0attn(inp)
    H1f = host_prep_l1ffn(inp)
    Hc = host_prep_conv(inp)

    def shards(a):
        return [np.ascontiguousarray(a[i * TOK:(i + 1) * TOK]) for i in range(NCORE)]

    def l0ffn(xcur):
        sh = shards(xcur)
        maps = [{"x_in": sh[i], "ident": _IDENT,
                 **{k: H0f[k] for k in ("Wc", "bc", "tie", "W1", "b1", "W2", "b2")}}
                for i in range(NCORE)]
        out = _run(build_l0ffn, maps, ["x_out"])
        return np.concatenate([o[0] for o in out], axis=0)

    def l0attn(xcur):
        sh = shards(xcur)
        out = _run(build_attn_pre, [{"x_in": sh[i], "ident": _IDENT} for i in range(NCORE)],
                   ["xhT_out"])
        xhT = [o[0] for o in out]
        xhT_full = [np.concatenate([xhT[4 * b + i] for i in range(4)], axis=2) for b in range(2)]
        maps = []
        for c in range(NCORE):
            b, hq = c // 4, c % 4
            maps.append({"xhT": np.ascontiguousarray(xhT_full[b][:, 2 * hq:2 * hq + 2, :]),
                         "Wqkv": H0a["Wqkv"], "bqkv": H0a["bqkv"], "masks": H0a["masks"],
                         "ident": _IDENT})
        out = _run(build_attn_core, maps, ["oT_out"])
        oT = [o[0] for o in out]
        oT_full = [np.concatenate([oT[4 * b + i] for i in range(4)], axis=1) for b in range(2)]
        maps = []
        for c in range(NCORE):
            b, tq = c // 4, c % 4
            maps.append({"x_in": sh[c],
                         "oT": np.ascontiguousarray(oT_full[b][:, :, tq * 2048:(tq + 1) * 2048]),
                         "Wr": H0a["Wr"], "br": H0a["br"]})
        out = _run(build_attn_proj, maps, ["x_out"])
        return np.concatenate([o[0] for o in out], axis=0)

    def l1ffn(xcur, y=None):
        sh = shards(xcur)
        ysh = shards(y) if y is not None else None
        maps = []
        for i in range(NCORE):
            m = {"x_in": sh[i], "ident": _IDENT,
                 **{k: H1f[k] for k in ("rW1", "rb1", "rW2", "rb2", "We1", "be1", "We2", "be2")}}
            if ysh is not None:
                m["y_in"] = ysh[i]
            maps.append(m)
        out = _run(build_l1ffn, maps, ["x_out"], with_y=(y is not None))
        return np.concatenate([o[0] for o in out], axis=0)

    def l1attn(xcur):
        sh = shards(xcur)
        maps = [{"x_in": sh[i], "Wp": Hc["Wp"], "bp": Hc["bp"], "Wf": Hc["Wf"],
                 "bf": Hc["bf"],
                 "posT": np.ascontiguousarray(Hc["posT"][:, :, 1024 * i:1024 * (i + 1)]),
                 "ident": _IDENT} for i in range(NCORE)]
        out = _run(build_xp, maps, ["xp_out", "f_out"])
        xp_full = np.concatenate([o[0] for o in out], axis=0)
        f_full = np.concatenate([o[1] for o in out], axis=0)
        uu = xp_full.reshape(2, 64, 128, 4, 1024).transpose(3, 0, 4, 1, 2)
        ff = f_full.reshape(64, 128, 3, 1024).transpose(2, 3, 0, 1)
        maps = [{"u_in": np.ascontiguousarray(uu[:, :, 128 * i:128 * (i + 1)]),
                 "f_in": np.ascontiguousarray(ff[:, 128 * i:128 * (i + 1)]),
                 "L1c": Hc["L1c"], "M2": Hc["M2"], "I1": Hc["I1"], "I2": Hc["I2"]}
                for i in range(NCORE)]
        out = _run(build_conv, maps, ["y_out"])
        y = np.stack([o[0] for o in out], axis=0)
        return np.ascontiguousarray(
            y.transpose(1, 3, 4, 0, 2).reshape(16384, 1024))

    x = l0ffn(x)
    x = l0attn(x)
    x = l0ffn(x)
    x = l1ffn(x)
    y = l1attn(x)
    x = l1ffn(x, y=y)
    return x.reshape(2, 8192, 1024)
